# version 01
# baseline (speedup 1.0000x reference)
import math
import numpy as np
from contextlib import ExitStack

import concourse.bass as bass
import concourse.mybir as mybir
from concourse.bass_utils import run_bass_kernel_spmd

F32 = mybir.dt.float32
BF16 = mybir.dt.bfloat16
I32 = mybir.dt.int32
U32 = mybir.dt.uint32
AF = mybir.ActivationFunctionType
ALU = mybir.AluOpType
AX = mybir.AxisListType

LI = 0.8 - 0.6 * math.exp(-0.3)
D = 2048
TOK = 1024
BT = 2048
NF = 16

ENGS = ("sync", "vector", "scalar", "tensor", "gpsimd")


class Prog:

    def __init__(self, nc, ctx):
        self.nc = nc
        self.ctx = ctx
        self.q = {e: [] for e in ENGS}
        self.sems = {}

    def sem(self, name):
        h = self.ctx.enter_context(self.nc.semaphore(name))
        self.sems[name] = [h, 0]
        return name

    def val(self, name):
        return self.sems[name][1]

    CHAIN = {"vector": "ve", "scalar": "se", "gpsimd": "gp"}

    def do(self, eng, fn, wait=(), inc=(), dma=False):
        wait = list(wait)
        inc = list(inc)
        ch = self.CHAIN.get(eng)
        if ch and not dma:
            wait.append((ch, self.sems[ch][1]))
            if not any(s == ch for s, _ in inc):
                inc.append((ch, 1))
        if dma:
            for s, n in inc:
                wait.append((s, self.sems[s][1]))
        incs = []
        for s, n in inc:
            self.sems[s][1] += n
            incs.append((self.sems[s][0], n))
        waits = [(self.sems[s][0], v) for s, v in wait if v > 0]

        def run(e, fn=fn, waits=waits, incs=incs):
            for hh, v in waits:
                e.wait_ge(hh, v)
            r = fn(e)
            for hh, n in incs:
                r.then_inc(hh, n)

        self.q[eng].append(run)

    def replay(self, block):
        def make(lst):
            def f(e):
                for fn in lst:
                    fn(e)
            return f
        block.sync(make(self.q["sync"]))
        block.vector(make(self.q["vector"]))
        block.scalar(make(self.q["scalar"]))
        block.tensor(make(self.q["tensor"]))
        block.gpsimd(make(self.q["gpsimd"]))


def build(debug=0):
    nc = bass.Bass("TRN2", target_bir_lowering=False)

    x_own = nc.dram_tensor("x_own", [TOK, D], F32, kind="ExternalInput").ap()
    x_partner = nc.dram_tensor("x_partner", [TOK, D], F32, kind="ExternalInput").ap()
    w_qkv = nc.dram_tensor("w_qkv", [D, 3 * D], F32, kind="ExternalInput").ap()
    w_proj = nc.dram_tensor("w_proj", [D, D], F32, kind="ExternalInput").ap()
    lam_q1 = nc.dram_tensor("lam_q1", [64], F32, kind="ExternalInput").ap()
    lam_k1 = nc.dram_tensor("lam_k1", [64], F32, kind="ExternalInput").ap()
    lam_q2 = nc.dram_tensor("lam_q2", [64], F32, kind="ExternalInput").ap()
    lam_k2 = nc.dram_tensor("lam_k2", [64], F32, kind="ExternalInput").ap()
    rms_scale = nc.dram_tensor("rms_scale", [128], F32, kind="ExternalInput").ap()
    wq_r = nc.dram_tensor("wq_r", [D, 128], F32, kind="ExternalInput").ap()
    keys_own = nc.dram_tensor("keys_own", [64, 64], F32, kind="ExternalInput").ap()
    down_embed = nc.dram_tensor("down_embed", [4096, D], F32, kind="ExternalInput").ap()
    up_embed = nc.dram_tensor("up_embed", [4096, D], F32, kind="ExternalInput").ap()
    gate_w = nc.dram_tensor("gate_w", [D, 8192], F32, kind="ExternalInput").ap()
    up_w = nc.dram_tensor("up_w", [D, 8192], F32, kind="ExternalInput").ap()
    down_w = nc.dram_tensor("down_w", [8192, D], F32, kind="ExternalInput").ap()
    ag_idx = nc.dram_tensor("ag_idx", [128, 16], I32, kind="ExternalInput").ap()
    contrib = nc.dram_tensor("contrib", [2048, 16], F32)
    ag_out = nc.dram_tensor("ag_out", [8 * 2048, 16], F32)

    out = nc.dram_tensor("out", [TOK, D], F32, kind="ExternalOutput").ap()
    dbg = {}
    if debug:
        dbg["x1"] = nc.dram_tensor("dbg_x1", [D, TOK], F32, kind="ExternalOutput").ap()
    if debug >= 3:
        dbg["x1a"] = nc.dram_tensor("dbg_x1a", [D, TOK], F32, kind="ExternalOutput").ap()
        dbg["h"] = nc.dram_tensor("dbg_h", [D, TOK], BF16, kind="ExternalOutput").ap()
    if debug >= 4:
        dbg["eidx"] = nc.dram_tensor("dbg_eidx", [128, 64], I32, kind="ExternalOutput").ap()
        dbg["prall"] = nc.dram_tensor("dbg_prall", [128, 64], F32, kind="ExternalOutput").ap()
        dbg["ewr0"] = nc.dram_tensor("dbg_ewr0", [128, 8], F32, kind="ExternalOutput").ap()
        dbg["ewf0"] = nc.dram_tensor("dbg_ewf0", [128, 8], F32, kind="ExternalOutput").ap()
        dbg["acc0"] = nc.dram_tensor("dbg_acc0", [128, D], F32, kind="ExternalOutput").ap()
        dbg["sxy0"] = nc.dram_tensor("dbg_sxy0", [128, 32], F32, kind="ExternalOutput").ap()
    if debug >= 2:
        dbg["negB"] = nc.dram_tensor("dbg_negB", [2, 128, 1], F32, kind="ExternalOutput").ap()
        dbg["q"] = nc.dram_tensor("dbg_q", [2, 128, 128], BF16, kind="ExternalOutput").ap()
        dbg["k"] = nc.dram_tensor("dbg_k", [2, 128, 128], BF16, kind="ExternalOutput").ap()
        dbg["rows"] = nc.dram_tensor("dbg_rows", [2, 3, TOK], F32, kind="ExternalOutput").ap()
        dbg["pr"] = nc.dram_tensor("dbg_pr", [2, 128, 512], BF16, kind="ExternalOutput").ap()

    with ExitStack() as top:
        P = Prog(nc, top)
        sb = lambda n, s, d=F32: top.enter_context(nc.sbuf_tensor(n, s, d))

        for s in ("hl", "sl", "gw", "gg", "cc", "pe", "ve", "se", "gp"):
            P.sem(s)

        iota64 = sb("iota64", [128, 64])
        ident32 = sb("ident32", [128, 128])
        identbf = sb("identbf", [128, 128], BF16)
        ones_col = sb("ones_col", [128, 1], BF16)
        ones_rows = sb("ones_rows", [65, 128], BF16)
        eps_ln = sb("eps_ln", [128, 1])
        rms_col = sb("rms_col", [128, 1])
        lam_t = sb("lam_t", [1, 1])
        sc_a = sb("sc_a", [1, 1])
        sc_b = sb("sc_b", [1, 1])
        sc_bb = sb("sc_bb", [1, 1], BF16)

        P.do("gpsimd", lambda g: g.iota(iota64[:], pattern=[[1, 64]], base=0,
                                        channel_multiplier=0,
                                        allow_small_or_imprecise_dtypes=True),
             inc=[("gp", 1)])
        for fn in (
            lambda g: g.memset(ident32[:], 0.0),
            lambda g: g.affine_select(out=ident32[:], in_=ident32[:],
                                      compare_op=ALU.not_equal, fill=1.0, base=0,
                                      pattern=[[-1, 128]], channel_multiplier=1),
            lambda g: g.memset(identbf[:], 0.0),
            lambda g: g.affine_select(out=identbf[:], in_=identbf[:],
                                      compare_op=ALU.not_equal, fill=1.0, base=0,
                                      pattern=[[-1, 128]], channel_multiplier=1),
            lambda g: g.memset(ones_col[:], 1.0),
            lambda g: g.memset(ones_rows[:], 1.0),
            lambda g: g.memset(eps_ln[:], 1e-5),
        ):
            P.do("gpsimd", fn, wait=[("gp", P.val("gp"))], inc=[("gp", 1)])
        gp_consts = P.val("gp")

        p0ctx = ExitStack()
        lamrows = p0ctx.enter_context(nc.sbuf_tensor("lamrows", [1, 256], F32))
        rowtmp = p0ctx.enter_context(nc.sbuf_tensor("rowtmp", [1, 64], F32))
        rowtmp2 = p0ctx.enter_context(nc.sbuf_tensor("rowtmp2", [1, 64], F32))
        P.do("sync", lambda e: e.dma_start(
            out=rms_col[:], in_=rms_scale.rearrange("(p o) -> p o", o=1),
            allow_slow_non_contiguous=True), inc=[("hl", 16)], dma=True)
        for i, t in enumerate((lam_q1, lam_k1, lam_q2, lam_k2)):
            P.do("sync", lambda e, i=i, t=t: e.dma_start(
                out=lamrows[0:1, 64 * i:64 * (i + 1)], in_=t.rearrange("(o f) -> o f", o=1)),
                inc=[("hl", 16)], dma=True)
        hl_small = P.val("hl")

        P.do("vector", lambda v: v.tensor_mul(rowtmp[:], lamrows[0:1, 0:64],
                                              lamrows[0:1, 64:128]),
             wait=[("hl", hl_small)])
        P.do("vector", lambda v: v.tensor_mul(rowtmp2[:], lamrows[0:1, 128:192],
                                              lamrows[0:1, 192:256]))
        P.do("vector", lambda v: v.reduce_sum(sc_a[:], rowtmp[:], axis=AX.X))
        P.do("vector", lambda v: v.reduce_sum(sc_b[:], rowtmp2[:], axis=AX.X),
             inc=[("ve", 1)])
        ve_ls = P.val("ve")
        P.do("scalar", lambda s: s.activation(sc_a[:], sc_a[:], AF.Exp),
             wait=[("ve", ve_ls)])
        P.do("scalar", lambda s: s.activation(sc_b[:], sc_b[:], AF.Exp),
             inc=[("se", 1)])
        se_le = P.val("se")
        P.do("vector", lambda v: v.tensor_sub(lam_t[:], sc_a[:], sc_b[:]),
             wait=[("se", se_le)])
        P.do("vector", lambda v: v.tensor_scalar(out=lam_t[:], in0=lam_t[:],
                                                 scalar1=LI, scalar2=None, op0=ALU.add),
             inc=[("ve", 1)])
        ve_lam = P.val("ve")
        p0ctx.close()

        out_all = [sb(f"oa{j}", [128, TOK], BF16) for j in range(NF)]

        ln_stack = ExitStack()
        ln_fm = [ln_stack.enter_context(nc.sbuf_tensor(f"ln_fm{j}", [128, BT], BF16))
                 for j in range(NF)]

        with ExitStack() as p1:
            sb1 = lambda n, s, d=F32: p1.enter_context(nc.sbuf_tensor(n, s, d))
            ps1 = lambda n, s, d=F32: p1.enter_context(nc.psum_tensor(n, s, d))
            xt = [sb1(f"p1_xt{p}", [128, D]) for p in range(2)]
            lnt = [sb1(f"p1_lnt{p}", [128, D], BF16) for p in range(2)]
            bnbuf = [sb1(f"p1_bn{p}", [128, 4, 6]) for p in range(2)]
            bnagg = [sb1(f"p1_ag{p}", [128, 2]) for p in range(2)]
            mrow = [sb1(f"p1_m{p}", [128, 1]) for p in range(2)]
            rrow = [sb1(f"p1_r{p}", [128, 1]) for p in range(2)]
            tp_ps = [ps1(f"p1_tp{p}", [128, 512], BF16) for p in range(2)]

            xt_free = [0, 0]
            lnt_free = [0, 0]
            tp_free = [0, 0]

            for i in range(16):
                par = i % 2
                src = x_own if i < 8 else x_partner
                r0 = (i % 8) * 128
                P.do("sync", lambda e, src=src, r0=r0, par=par: e.dma_start(
                    out=xt[par][:], in_=src[r0:r0 + 128, :]),
                    wait=[("ve", xt_free[par])], inc=[("hl", 16)], dma=True)
                hl_i = P.val("hl")
                for cth in range(4):
                    P.do("vector", lambda v, par=par, cth=cth: v.bn_stats(
                        bnbuf[par][:, cth, :], xt[par][:, 512 * cth:512 * (cth + 1)]),
                        wait=[("hl", hl_i)] if cth == 0 else ())
                P.do("vector", lambda v, par=par: v.bn_aggr(
                    bnagg[par][:], bnbuf[par].ap().rearrange("p a b -> p (a b)")))
                P.do("vector", lambda v, par=par: v.tensor_copy(mrow[par][:],
                                                                bnagg[par][:, 0:1]))
                P.do("vector", lambda v, par=par: v.tensor_add(
                    rrow[par][:], bnagg[par][:, 1:2], eps_ln[:]),
                    wait=[("gp", gp_consts)], inc=[("ve", 1)])
                ve_var = P.val("ve")
                P.do("scalar", lambda s, par=par: s.activation(rrow[par][:], rrow[par][:],
                                                               AF.Ln),
                     wait=[("ve", ve_var)])
                P.do("scalar", lambda s, par=par: s.activation(rrow[par][:], rrow[par][:],
                                                               AF.Exp, scale=-0.5),
                     inc=[("se", 1)])
                se_rstd = P.val("se")
                P.do("vector", lambda v, par=par: v.tensor_scalar(
                    out=lnt[par][:], in0=xt[par][:], scalar1=mrow[par][:, 0:1],
                    scalar2=rrow[par][:, 0:1], op0=ALU.subtract, op1=ALU.mult),
                    wait=[("se", se_rstd), ("pe", lnt_free[par])], inc=[("ve", 1)])
                ve_ln = P.val("ve")
                xt_free[par] = ve_ln
                for grp in range(4):
                    gpar = grp % 2
                    for m in range(4):
                        jj = grp * 4 + m
                        P.do("tensor", lambda t, par=par, gpar=gpar, m=m, jj=jj:
                             t.transpose(tp_ps[gpar][:, 128 * m:128 * (m + 1)],
                                         lnt[par][:, 128 * jj:128 * (jj + 1)], identbf[:]),
                             wait=([("ve", ve_ln), ("se", tp_free[gpar]),
                                    ("gp", gp_consts)] if m == 0 else ()),
                             inc=[("pe", 1)] if m == 3 else ())
                    pe_tp = P.val("pe")
                    if grp == 3:
                        lnt_free[par] = pe_tp
                    for m in range(4):
                        jj = grp * 4 + m
                        P.do("scalar", lambda s, i=i, gpar=gpar, m=m, jj=jj: s.copy(
                            ln_fm[jj][:, 128 * i:128 * (i + 1)],
                            tp_ps[gpar][:, 128 * m:128 * (m + 1)]),
                            wait=[("pe", pe_tp)] if m == 0 else (),
                            inc=[("se", 1)] if m == 3 else ())
                    tp_free[gpar] = P.val("se")
            se_ln_done = P.val("se")


        wq_view = w_qkv.rearrange("(f p) (blk grp c) -> p f blk grp c",
                                  p=128, blk=6, grp=2)

        epi_ve = 0
        se_prev_ctx = se_ln_done
        ve_prev_ctx = 0
        for g in range(2):
            with ExitStack() as pg:
                sbg = lambda n, s, d=F32: pg.enter_context(nc.sbuf_tensor(n, s, d))
                q_g = [sbg(f"g{g}_q{h}", [128, TOK], BF16) for h in range(8)]
                k_g = [sbg(f"g{g}_k{h}", [128, BT], BF16) for h in range(8)]
                v_g = [sbg(f"g{g}_v{i}", [128, TOK], BF16) for i in range(16)]
                wband = sbg(f"g{g}_wband", [128, 16 * 512], BF16)
                wband_4d = wband.ap().rearrange("p (f hcs b c) -> p f hcs b c",
                                                f=16, hcs=4, b=2)
                wband_vv = wband.ap().rearrange("p (f c) -> p f c", f=16)

                def qk_lhsT(f, hc4):
                    return wband[:, f * 512 + hc4 * 128: f * 512 + (hc4 + 1) * 128]

                with ExitStack() as pq:
                    q_ps = [pq.enter_context(
                        nc.psum_tensor(f"g{g}_qps{p}", [128, TOK], F32)) for p in range(2)]
                    qps_free = [se_prev_ctx, se_prev_ctx]
                    qps_extra = [("ve", ve_prev_ctx)]
                    for hb in range(2):
                        for bnd in range(2):
                            for hc4 in range(4):
                                P.do("gpsimd", lambda e, g=g, hb=hb, bnd=bnd, hc4=hc4:
                                     e.dma_start(
                                         out=wband_4d[:, :, hc4, bnd, :],
                                         in_=wq_view[:, :, bnd, g,
                                                     256 * hb + 64 * hc4:
                                                     256 * hb + 64 * (hc4 + 1)]),
                                     wait=[("gw", P.val("gw")), ("pe", P.val("pe")),
                                           ("ve", P.val("ve")), ("se", P.val("se")),
                                           ("sl", P.val("sl"))],
                                     inc=[("gw", 16)], dma=True)
                        gw_q = P.val("gw")
                        for hc4 in range(4):
                            hc = 4 * hb + hc4
                            par = hc % 2
                            for f in range(16):
                                for half in range(2):
                                    P.do("tensor",
                                         lambda t, hc4=hc4, f=f, half=half, par=par:
                                         t.matmul(q_ps[par][:, 512 * half:512 * (half + 1)],
                                                  qk_lhsT(f, hc4),
                                                  ln_fm[f][:, 512 * half:512 * (half + 1)],
                                                  start=(f == 0), stop=(f == 15)),
                                         wait=(([("gw", gw_q), ("se", qps_free[par]),
                                                 ("se", se_ln_done)] + qps_extra)
                                               if (f == 0 and half == 0) else ()),
                                         inc=([("pe", 1)] if (f == 15 and half == 1)
                                              else ()))
                            pe_q = P.val("pe")
                            P.do("scalar", lambda s, hc=hc, par=par: s.copy(
                                q_g[hc][:], q_ps[par][:]),
                                wait=[("pe", pe_q)], inc=[("se", 1)])
                            qps_free[par] = P.val("se")
                    se_q_end = P.val("se")

                with ExitStack() as pk:
                    k_ps = [pk.enter_context(
                        nc.psum_tensor(f"g{g}_kps{p}", [128, 512], F32)) for p in range(8)]
                    kps_free = [se_q_end] * 2
                    for hb in range(2):
                        for bnd in range(2):
                            for hc4 in range(4):
                                P.do("gpsimd", lambda e, g=g, hb=hb, bnd=bnd, hc4=hc4:
                                     e.dma_start(
                                         out=wband_4d[:, :, hc4, bnd, :],
                                         in_=wq_view[:, :, 2 + bnd, g,
                                                     256 * hb + 64 * hc4:
                                                     256 * hb + 64 * (hc4 + 1)]),
                                     wait=[("gw", P.val("gw")), ("pe", P.val("pe")),
                                           ("ve", P.val("ve")), ("se", P.val("se")),
                                           ("sl", P.val("sl"))],
                                     inc=[("gw", 16)], dma=True)
                        gw_k = P.val("gw")
                        for hc4 in range(4):
                            hc = 4 * hb + hc4
                            base = (hc % 2) * 4
                            for f in range(16):
                                for ts4 in range(4):
                                    P.do("tensor",
                                         lambda t, hc4=hc4, f=f, ts4=ts4, base=base:
                                         t.matmul(k_ps[base + ts4][:],
                                                  qk_lhsT(f, hc4),
                                                  ln_fm[f][:, 512 * ts4:512 * (ts4 + 1)],
                                                  start=(f == 0), stop=(f == 15)),
                                         wait=([("gw", gw_k), ("se", kps_free[hc % 2])]
                                               if (f == 0 and ts4 == 0) else ()),
                                         inc=([("pe", 1)] if (f == 15 and ts4 == 3)
                                              else ()))
                            pe_k = P.val("pe")
                            for ts4 in range(4):
                                P.do("scalar", lambda s, hc=hc, ts4=ts4, base=base: s.copy(
                                    k_g[hc][:, 512 * ts4:512 * (ts4 + 1)],
                                    k_ps[base + ts4][:]),
                                    wait=[("pe", pe_k)] if ts4 == 0 else (),
                                    inc=[("se", 1)] if ts4 == 3 else ())
                            kps_free[hc % 2] = P.val("se")
                    se_k_end = P.val("se")

                with ExitStack() as pv:
                    v_ps = [pv.enter_context(
                        nc.psum_tensor(f"g{g}_vps{p}", [128, 512], F32)) for p in range(2)]
                    vps_free = [se_k_end, se_k_end]
                    vsrc = wq_view[:, :, 4 + g, :, :].rearrange("p f grp c -> p f (grp c)")
                    for hb in range(2):
                        P.do("gpsimd", lambda e, hb=hb, vsrc=vsrc: e.dma_start(
                            out=wband_vv, in_=vsrc[:, :, 512 * hb:512 * (hb + 1)]),
                            wait=[("gw", P.val("gw")), ("pe", P.val("pe")),
                                  ("ve", P.val("ve")), ("se", P.val("se")),
                                  ("sl", P.val("sl"))],
                            inc=[("gw", 16)], dma=True)
                        gw_v = P.val("gw")
                        for ti in range(16):
                            par = ti % 2
                            for f in range(16):
                                P.do("tensor", lambda t, ti=ti, f=f, par=par: t.matmul(
                                    v_ps[par][:],
                                    ln_fm[f][:, 128 * ti:128 * (ti + 1)],
                                    wband_vv[:, f, :],
                                    start=(f == 0), stop=(f == 15)),
                                    wait=([("gw", gw_v), ("se", vps_free[par])]
                                          if f == 0 else ()),
                                    inc=([("pe", 1)] if f == 15 else ()))
                            pe_v = P.val("pe")
                            P.do("scalar", lambda s, ti=ti, hb=hb, par=par: s.copy(
                                v_g[ti][:, 512 * hb:512 * (hb + 1)], v_ps[par][:]),
                                wait=[("pe", pe_v)], inc=[("se", 1)])
                            vps_free[par] = P.val("se")
                    se_v_end = P.val("se")

                negB8 = sbg(f"g{g}_negB8", [128, 8])
                with ExitStack() as pb:
                    sqt = pb.enter_context(nc.sbuf_tensor(f"g{g}_bbsq", [128, BT], BF16))
                    qn_ps = pb.enter_context(nc.psum_tensor(f"g{g}_bbqn", [1, TOK], F32))
                    kn_ps = pb.enter_context(nc.psum_tensor(f"g{g}_bbkn", [1, BT], F32))
                    bps = pb.enter_context(nc.psum_tensor(f"g{g}_bbps", [128, 1], F32))
                    nps_free = []
                    bps_free = []
                    for hc in range(8):
                        P.do("vector", lambda v, hc=hc: v.tensor_mul(
                            sqt[:, 0:TOK], q_g[hc][:], q_g[hc][:]),
                            wait=([("se", se_v_end), ("ve", ve_lam)] if hc == 0 else
                                  [("pe", P.val("pe"))]),
                            inc=[("ve", 1)])
                        ve_sq = P.val("ve")
                        for half in range(2):
                            P.do("tensor", lambda t, half=half: t.matmul(
                                qn_ps[:, 512 * half:512 * (half + 1)], ones_col[:],
                                sqt[:, 512 * half:512 * (half + 1)],
                                start=True, stop=True),
                                wait=(([("ve", ve_sq), ("se", se_v_end)] + nps_free)
                                      if half == 0 else ()),
                                inc=[("pe", 1)] if half == 1 else ())
                        pe_qn = P.val("pe")
                        P.do("vector", lambda v, hc=hc: v.tensor_mul(
                            sqt[:], k_g[hc][:], k_g[hc][:]),
                            wait=[("pe", pe_qn)], inc=[("ve", 1)])
                        ve_sk = P.val("ve")
                        for qtr in range(4):
                            P.do("tensor", lambda t, qtr=qtr: t.matmul(
                                kn_ps[:, 512 * qtr:512 * (qtr + 1)], ones_col[:],
                                sqt[:, 512 * qtr:512 * (qtr + 1)],
                                start=True, stop=True),
                                wait=[("ve", ve_sk)] if qtr == 0 else (),
                                inc=[("pe", 1)] if qtr == 3 else ())
                        pe_kn = P.val("pe")
                        P.do("vector", lambda v: v.reduce_max(sc_a[:], qn_ps[:],
                                                              axis=AX.X),
                             wait=[("pe", pe_kn)])
                        P.do("vector", lambda v: v.reduce_max(sc_b[:], kn_ps[:],
                                                              axis=AX.X))
                        P.do("vector", lambda v: v.tensor_mul(sc_a[:], sc_a[:],
                                                              sc_b[:]),
                             inc=[("ve", 1)])
                        ve_bb = P.val("ve")
                        nps_free = [("ve", ve_bb)]
                        P.do("scalar", lambda s: s.activation(sc_a[:], sc_a[:], AF.Ln),
                             wait=[("ve", ve_bb)])
                        P.do("scalar", lambda s: s.activation(sc_a[:], sc_a[:], AF.Exp,
                                                              scale=0.5),
                             inc=[("se", 1)])
                        se_bb = P.val("se")
                        P.do("vector", lambda v: v.tensor_scalar(
                            out=sc_a[:], in0=sc_a[:], scalar1=-0.125 * 1.05,
                            scalar2=None, op0=ALU.mult), wait=[("se", se_bb)])
                        P.do("vector", lambda v: v.tensor_copy(sc_bb[:], sc_a[:]),
                             inc=[("ve", 1)])
                        ve_bB = P.val("ve")
                        P.do("tensor", lambda t: t.matmul(bps[:], ones_rows[0:1, :],
                                                          sc_bb[:],
                                                          start=True, stop=True),
                             wait=[("ve", ve_bB)] + bps_free, inc=[("pe", 1)])
                        pe_bb = P.val("pe")
                        P.do("scalar", lambda s, hc=hc: s.copy(negB8[:, hc:hc + 1],
                                                               bps[:]),
                             wait=[("pe", pe_bb)], inc=[("se", 1)])
                        bps_free = [("se", P.val("se"))]
                    se_negB = P.val("se")
                    if debug >= 2:
                        P.do("scalar", lambda s, g=g: s.dma_start(
                            out=dbg["negB"][g, :, 0:1], in_=negB8[:, 0:1]),
                            wait=[("se", se_negB)], inc=[("sl", 16)], dma=True)
                        P.do("scalar", lambda s, g=g: s.dma_start(
                            out=dbg["q"][g, :, :], in_=q_g[0][:, 0:128]),
                            inc=[("sl", 16)], dma=True)
                        P.do("scalar", lambda s, g=g: s.dma_start(
                            out=dbg["k"][g, :, :], in_=k_g[0][:, 0:128]),
                            inc=[("sl", 16)], dma=True)

                with ExitStack() as pa:
                    sba = lambda n, s, d=F32: pa.enter_context(nc.sbuf_tensor(n, s, d))
                    psa = lambda n, s, d=F32: pa.enter_context(nc.psum_tensor(n, s, d))
                    s_ps = [psa(f"g{g}_at_s{i}", [128, 512]) for i in range(4)]
                    pv_ps = [psa(f"g{g}_at_pv{b}", [128, 512]) for b in range(2)]
                    sum_ps = [psa(f"g{g}_at_sm{b}", [1, 512]) for b in range(2)]
                    probs = [sba(f"g{g}_at_pr{i}", [128, 512], BF16) for i in range(4)]
                    u_t = sba(f"g{g}_at_u", [128, TOK], BF16)
                    t2b = sba(f"g{g}_at_t2", [128, TOK], BF16)
                    rows_f = sba(f"g{g}_at_rf", [65, TOK])
                    rows_b = sba(f"g{g}_at_rb", [65, TOK], BF16)

                    s_free = [[("se", se_negB)] for _ in range(4)]
                    sum_free = [[("se", se_negB)] for _ in range(2)]
                    pv_free = [[("se", se_negB)] for _ in range(2)]
                    probs_free = [0] * 4

                    for hc in range(8):
                        for qh in range(2):
                            for j in range(16):
                                par = j % 2
                                for b in range(2):
                                    i4 = 2 * b + par
                                    lo, hi = (0, 64) if b == 0 else (64, 128)
                                    tp = (0, 0) if b == 0 else (64, 0)
                                    P.do("tensor",
                                         lambda t, hc=hc, qh=qh, j=j, i4=i4, lo=lo,
                                         hi=hi, tp=tp: t.matmul(
                                             s_ps[i4][:],
                                             k_g[hc][lo:hi, 128 * j:128 * (j + 1)],
                                             q_g[hc][lo:hi, 512 * qh:512 * (qh + 1)],
                                             start=True, stop=True, tile_position=tp),
                                         wait=s_free[i4], inc=[("pe", 1)])
                                    s_free[i4] = []
                                    pe_s = P.val("pe")
                                    P.do("scalar", lambda s, i4=i4, hc=hc: s.activation(
                                        probs[i4][:], s_ps[i4][:], AF.Exp,
                                        bias=negB8[:, hc:hc + 1], scale=0.125),
                                        wait=[("pe", pe_s), ("pe", probs_free[i4])],
                                        inc=[("se", 1)])
                                    s_free[i4] = [("se", P.val("se"))]
                                se_probs = P.val("se")
                                for b in range(2):
                                    i4 = 2 * b + par
                                    P.do("tensor", lambda t, hc=hc, j=j, b=b, i4=i4:
                                         t.matmul(pv_ps[b][:],
                                                  v_g[j][:, 128 * hc:128 * (hc + 1)],
                                                  probs[i4][:],
                                                  start=(j == 0), stop=(j == 15)),
                                         wait=([("se", se_probs)] +
                                               (pv_free[b] if j == 0 else [])))
                                    P.do("tensor", lambda t, j=j, b=b, i4=i4: t.matmul(
                                        sum_ps[b][:], ones_col[:], probs[i4][:],
                                        start=(j == 0), stop=(j == 15)),
                                        wait=(sum_free[b] if j == 0 else ()),
                                        inc=[("pe", 1)])
                                    probs_free[i4] = P.val("pe")
                                    pv_free[b] = []
                                    sum_free[b] = []
                            pe_pv = P.val("pe")
                            P.do("vector", lambda v, qh=qh: v.tensor_copy(
                                u_t[:, 512 * qh:512 * (qh + 1)], pv_ps[0][:]),
                                wait=[("pe", pe_pv)])
                            P.do("vector", lambda v, qh=qh: v.tensor_copy(
                                t2b[:, 512 * qh:512 * (qh + 1)], pv_ps[1][:]))
                            P.do("vector", lambda v, qh=qh: v.tensor_copy(
                                rows_f[0:1, 512 * qh:512 * (qh + 1)], sum_ps[0][:]))
                            P.do("vector", lambda v, qh=qh: v.tensor_copy(
                                rows_f[32:33, 512 * qh:512 * (qh + 1)], sum_ps[1][:]),
                                inc=[("ve", 1)])
                            ve_ev = P.val("ve")
                            for b in range(2):
                                pv_free[b] = [("ve", ve_ev)]
                                sum_free[b] = [("ve", ve_ev)]


                        P.do("vector", lambda v: v.reciprocal(rows_f[64:65, :],
                                                              rows_f[0:1, :]))
                        P.do("vector", lambda v: v.tensor_copy(rows_b[0:1, :],
                                                               rows_f[64:65, :]))
                        P.do("vector", lambda v: v.reciprocal(rows_f[64:65, :],
                                                              rows_f[32:33, :]))
                        P.do("vector", lambda v: v.tensor_scalar(
                            out=rows_b[32:33, :], in0=rows_f[64:65, :],
                            scalar1=lam_t[0:1, 0:1], scalar2=None, op0=ALU.mult),
                            inc=[("ve", 1)])
                        ve_rb = P.val("ve")
                        for half in range(2):
                            for b in range(2):
                                P.do("tensor", lambda t, half=half, b=b: t.matmul(
                                    s_ps[b][:], ones_rows[32 * b:32 * b + 1, :],
                                    rows_b[32 * b:32 * b + 1,
                                           512 * half:512 * (half + 1)],
                                    start=True, stop=True),
                                    wait=[("ve", ve_rb)] + s_free[b],
                                    inc=[("pe", 1)])
                                s_free[b] = []
                            pe_br = P.val("pe")
                            P.do("vector", lambda v, half=half: v.tensor_mul(
                                probs[2][:], u_t[:, 512 * half:512 * (half + 1)],
                                s_ps[0][:]),
                                wait=[("pe", pe_br)])
                            P.do("vector", lambda v, half=half: v.tensor_mul(
                                probs[3][:], t2b[:, 512 * half:512 * (half + 1)],
                                s_ps[1][:]),
                                inc=[("ve", 1)])
                            for b in range(2):
                                s_free[b] = [("ve", P.val("ve"))]
                            P.do("vector", lambda v, half=half: v.tensor_sub(
                                u_t[:, 512 * half:512 * (half + 1)], probs[2][:],
                                probs[3][:]))
                            P.do("vector", lambda v, half=half: v.tensor_mul(
                                probs[half][:], u_t[:, 512 * half:512 * (half + 1)],
                                u_t[:, 512 * half:512 * (half + 1)]),
                                inc=[("ve", 1)])
                        ve_uu = P.val("ve")
                        for half in range(2):
                            P.do("tensor", lambda t, half=half: t.matmul(
                                sum_ps[half][:], ones_col[:], probs[half][:],
                                start=True, stop=True),
                                wait=[("ve", ve_uu)] + sum_free[half],
                                inc=[("pe", 1)])
                            sum_free[half] = []
                        pe_ms = P.val("pe")
                        for half in range(2):
                            P.do("vector", lambda v, half=half: v.tensor_scalar(
                                out=rows_f[64:65, 512 * half:512 * (half + 1)],
                                in0=sum_ps[half][:], scalar1=1.0 / 128, scalar2=1e-6,
                                op0=ALU.mult, op1=ALU.add),
                                wait=[("pe", pe_ms)] if half == 0 else (),
                                inc=[("ve", 1)] if half == 1 else ())
                        ve_ms = P.val("ve")
                        for b in range(2):
                            sum_free[b] = [("ve", ve_ms)]
                        P.do("scalar", lambda s: s.activation(rows_f[64:65, :],
                                                              rows_f[64:65, :], AF.Ln),
                             wait=[("ve", ve_ms)])
                        P.do("scalar", lambda s: s.activation(rows_f[64:65, :],
                                                              rows_f[64:65, :],
                                                              AF.Exp, scale=-0.5),
                             inc=[("se", 1)])
                        se_rst = P.val("se")
                        P.do("vector", lambda v: v.tensor_scalar(
                            out=rows_b[64:65, :], in0=rows_f[64:65, :],
                            scalar1=(1.0 - LI), scalar2=None, op0=ALU.mult),
                            wait=[("se", se_rst)], inc=[("ve", 1)])
                        ve_rstb = P.val("ve")
                        for half in range(2):
                            P.do("tensor", lambda t, half=half: t.matmul(
                                s_ps[2][:], ones_rows[64:65, :],
                                rows_b[64:65, 512 * half:512 * (half + 1)],
                                start=True, stop=True),
                                wait=[("ve", ve_rstb)] + s_free[2], inc=[("pe", 1)])
                            s_free[2] = []
                            pe_brs = P.val("pe")
                            P.do("vector", lambda v, g=g, hc=hc, half=half:
                                 v.scalar_tensor_tensor(
                                     out=out_all[g * 8 + hc][:,
                                                             512 * half:512 * (half + 1)],
                                     in0=u_t[:, 512 * half:512 * (half + 1)],
                                     scalar=rms_col[:, 0:1], in1=s_ps[2][:],
                                     op0=ALU.mult, op1=ALU.mult),
                                 wait=[("pe", pe_brs), ("hl", hl_small)],
                                 inc=[("ve", 1)])
                            s_free[2] = [("ve", P.val("ve"))]
                        epi_ve = P.val("ve")
                        for b in range(2):
                            pv_free[b] = [("ve", epi_ve)]
                    se_prev_ctx = P.val("se")
                    ve_prev_ctx = P.val("ve")

        ln_stack.close()
        x1 = [sb(f"x1_{o}", [128, TOK]) for o in range(NF)]

        with ExitStack() as p4:
            sb4 = lambda n, s, d=F32: p4.enter_context(nc.sbuf_tensor(n, s, d))
            ps4 = lambda n, s, d=F32: p4.enter_context(nc.psum_tensor(n, s, d))
            xt4 = [sb4(f"p4_xt{p}", [128, D]) for p in range(2)]
            wpo = [sb4(f"p4_wp{p}", [128, D], BF16) for p in range(2)]
            xps = [ps4(f"p4_xp{p}", [128, 512]) for p in range(2)]
            pr_ps = [ps4(f"p4_pr{p}", [128, TOK]) for p in range(2)]
            wp_view = w_proj.rearrange("(f p) (o c) -> p f o c", p=128, o=16)

            pe_barrier = P.val("pe")
            ve_barrier = P.val("ve")
            se_barrier = P.val("se")
            sl_barrier = P.val("sl")
            xt4_free = [0, 0]
            xps_free = [[("ve", ve_prev_ctx), ("se", se_prev_ctx)] for _ in range(2)]
            prps_free = [[("ve", ve_prev_ctx), ("se", se_prev_ctx)] for _ in range(2)]

            for ti in range(8):
                par = ti % 2
                P.do("sync", lambda e, ti=ti, par=par: e.dma_start(
                    out=xt4[par][:], in_=x_own[128 * ti:128 * (ti + 1), :]),
                    wait=([("pe", xt4_free[par])] +
                          ([("pe", pe_barrier), ("ve", ve_barrier),
                            ("se", se_barrier), ("sl", sl_barrier)] if ti == 0 else [])),
                    inc=[("hl", 16)], dma=True)
                hl_x = P.val("hl")
                for grp in range(4):
                    gpar = grp % 2
                    for m in range(4):
                        o = grp * 4 + m
                        P.do("tensor", lambda t, par=par, gpar=gpar, m=m, o=o:
                             t.transpose(xps[gpar][:, 128 * m:128 * (m + 1)],
                                         xt4[par][:, 128 * o:128 * (o + 1)], ident32[:]),
                             wait=([("hl", hl_x)] + xps_free[gpar]) if m == 0 else (),
                             inc=[("pe", 1)] if m == 3 else ())
                    pe_x = P.val("pe")
                    if grp == 3:
                        xt4_free[par] = pe_x
                    for m in range(4):
                        o = grp * 4 + m
                        P.do("vector", lambda v, ti=ti, gpar=gpar, m=m, o=o: v.tensor_copy(
                            x1[o][:, 128 * ti:128 * (ti + 1)],
                            xps[gpar][:, 128 * m:128 * (m + 1)]),
                            wait=[("pe", pe_x)] if m == 0 else (),
                            inc=[("ve", 1)] if m == 3 else ())
                    xps_free[gpar] = [("ve", P.val("ve"))]
            ve_xinit = P.val("ve")

            for o in range(16):
                par = o % 2
                P.do("gpsimd", lambda e, o=o, par=par: e.dma_start(
                    out=wpo[par].ap().rearrange("p (f c) -> p f c", f=16),
                    in_=wp_view[:, :, o, :]),
                    wait=[("gw", P.val("gw")), ("ve", ve_barrier), ("se", se_barrier)]
                    + ([("pe", P.val("pe"))] if o >= 2 else []),
                    inc=[("gw", 16)], dma=True)
                gw_wp = P.val("gw")
                for f in range(16):
                    for half in range(2):
                        P.do("tensor", lambda t, o=o, par=par, f=f, half=half: t.matmul(
                            pr_ps[par][:, 512 * half:512 * (half + 1)],
                            wpo[par][:, 128 * f:128 * (f + 1)],
                            out_all[f][:, 512 * half:512 * (half + 1)],
                            start=(f == 0), stop=(f == 15)),
                            wait=(([("gw", gw_wp), ("ve", epi_ve)] + prps_free[par])
                                  if (f == 0 and half == 0) else ()),
                            inc=([("pe", 1)] if (f == 15 and half == 1) else ()))
                pe_pr = P.val("pe")
                P.do("vector", lambda v, o=o, par=par: v.tensor_add(
                    x1[o][:], x1[o][:], pr_ps[par][:]),
                    wait=[("pe", pe_pr), ("ve", ve_xinit)], inc=[("ve", 1)])
                prps_free[par] = [("ve", P.val("ve"))]
            ve_x1 = P.val("ve")

        if debug >= 3:
            for o in range(16):
                P.do("scalar", lambda s, o=o: s.dma_start(
                    out=dbg["x1a"][128 * o:128 * (o + 1), :], in_=x1[o][:]),
                    wait=[("ve", ve_x1)] if o == 0 else (), inc=[("sl", 16)], dma=True)

        h_fm = [sb(f"h_fm{o}", [128, TOK], BF16) for o in range(NF)]

        with ExitStack() as p5:
            sb5 = lambda n, s, d=F32: p5.enter_context(nc.sbuf_tensor(n, s, d))
            ps5 = lambda n, s, d=F32: p5.enter_context(nc.psum_tensor(n, s, d))
            xb = [sb5(f"p5_xb{p}", [128, TOK], BF16) for p in range(2)]
            xbsq = [sb5(f"p5_xq{p}", [128, TOK], BF16) for p in range(2)]
            nrm = sb5("p5_nrm", [128, TOK])
            mean_r = sb5("p5_mean", [1, TOK])
            var_r = sb5("p5_var", [1, TOK])
            tmp_r = sb5("p5_tmp", [1, TOK])
            meanb_r = sb5("p5_meanb", [1, TOK], BF16)
            rstdb_r = sb5("p5_rstdb", [1, TOK], BF16)
            sm_ps = ps5("p5_sm", [1, TOK])
            sq_ps = ps5("p5_sq", [1, TOK])
            bm_ps = ps5("p5_bm", [128, TOK])
            br_ps = ps5("p5_br", [128, TOK])

            p5_entry = [("ve", P.val("ve")), ("se", P.val("se")),
                        ("pe", P.val("pe"))]
            xb_free = [0, 0]
            for o in range(16):
                par = o % 2
                P.do("scalar", lambda s, o=o, par=par: s.copy(xb[par][:], x1[o][:]),
                     wait=[("ve", ve_x1), ("pe", xb_free[par])])
                P.do("vector", lambda v, par=par: v.tensor_mul(xbsq[par][:], xb[par][:],
                                                               xb[par][:]),
                     wait=[("se", P.val("se"))], inc=[("ve", 1)])
                ve_xb = P.val("ve")
                for half in range(2):
                    P.do("tensor", lambda t, par=par, half=half, o=o: t.matmul(
                        sm_ps[:, 512 * half:512 * (half + 1)], ones_col[:],
                        xb[par][:, 512 * half:512 * (half + 1)],
                        start=(o == 0), stop=(o == 15)),
                        wait=(([("ve", ve_xb)] + (p5_entry if o == 0 else []))
                              if half == 0 else ()))
                    P.do("tensor", lambda t, par=par, half=half, o=o: t.matmul(
                        sq_ps[:, 512 * half:512 * (half + 1)], ones_col[:],
                        xbsq[par][:, 512 * half:512 * (half + 1)],
                        start=(o == 0), stop=(o == 15)),
                        inc=([("pe", 1)] if half == 1 else ()))
                xb_free[par] = P.val("pe")
            pe_st5 = P.val("pe")
            P.do("vector", lambda v: v.tensor_scalar(
                out=mean_r[:], in0=sm_ps[:], scalar1=1.0 / D, scalar2=None,
                op0=ALU.mult), wait=[("pe", pe_st5)])
            P.do("vector", lambda v: v.tensor_scalar(
                out=var_r[:], in0=sq_ps[:], scalar1=1.0 / D, scalar2=None,
                op0=ALU.mult))
            P.do("vector", lambda v: v.tensor_mul(tmp_r[:], mean_r[:], mean_r[:]))
            P.do("vector", lambda v: v.tensor_sub(var_r[:], var_r[:], tmp_r[:]))
            P.do("vector", lambda v: v.tensor_scalar(
                out=var_r[:], in0=var_r[:], scalar1=1.0, scalar2=1e-5,
                op0=ALU.mult, op1=ALU.add))
            P.do("vector", lambda v: v.tensor_copy(meanb_r[:], mean_r[:]),
                 inc=[("ve", 1)])
            ve_r5 = P.val("ve")
            P.do("scalar", lambda s: s.activation(var_r[:], var_r[:],
                                                  AF.Ln), wait=[("ve", ve_r5)])
            P.do("scalar", lambda s: s.activation(var_r[:], var_r[:],
                                                  AF.Exp, scale=-0.5),
                 inc=[("se", 1)])
            se_r5 = P.val("se")
            P.do("vector", lambda v: v.tensor_copy(rstdb_r[:], var_r[:]),
                 wait=[("se", se_r5)], inc=[("ve", 1)])
            ve_r5b = P.val("ve")
            for half in range(2):
                P.do("tensor", lambda t, half=half: t.matmul(
                    bm_ps[:, 512 * half:512 * (half + 1)], ones_rows[0:1, :],
                    meanb_r[0:1, 512 * half:512 * (half + 1)], start=True, stop=True),
                    wait=[("ve", ve_r5b)] if half == 0 else ())
                P.do("tensor", lambda t, half=half: t.matmul(
                    br_ps[:, 512 * half:512 * (half + 1)], ones_rows[0:1, :],
                    rstdb_r[0:1, 512 * half:512 * (half + 1)], start=True, stop=True),
                    inc=[("pe", 1)] if half == 1 else ())
            pe_bc5 = P.val("pe")
            for o in range(16):
                P.do("vector", lambda v, o=o: v.tensor_sub(nrm[:], x1[o][:], bm_ps[:]),
                     wait=[("pe", pe_bc5)] if o == 0 else ())
                P.do("vector", lambda v, o=o: v.tensor_mul(h_fm[o][:], nrm[:], br_ps[:]),
                     inc=[("ve", 1)])
            ve_h = P.val("ve")

        if debug >= 3:
            for o in range(16):
                P.do("scalar", lambda s, o=o: s.dma_start(
                    out=dbg["h"][128 * o:128 * (o + 1), :], in_=h_fm[o][:]),
                    wait=[("ve", ve_h)] if o == 0 else (), inc=[("sl", 16)], dma=True)

        with ExitStack() as p6:
            sb6 = lambda n, s, d=F32: p6.enter_context(nc.sbuf_tensor(n, s, d))
            ps6 = lambda n, s, d=F32: p6.enter_context(nc.psum_tensor(n, s, d))
            wqr = sb6("p6_wqr", [128, 16 * 128], BF16)
            keysb = sb6("p6_keys", [64, 64], BF16)
            q_rb = sb6("p6_qrb", [64, BT], BF16)
            scr = [sb6(f"p6_scr{p}", [128, 64]) for p in range(2)]
            v8t = sb6("p6_v8", [128, 8])
            u8t = sb6("p6_u8", [128, 8], U32)
            contrib_sb = sb6("p6_contrib", [128, 256])
            qr_ps = ps6("p6_qps", [128, TOK])
            sc_ps = [ps6(f"p6_sps{p}", [128, 64]) for p in range(2)]

            p6_entry = [("ve", P.val("ve")), ("se", P.val("se")),
                        ("pe", P.val("pe"))]
            P.do("gpsimd", lambda e: e.dma_start(
                out=wqr.ap().rearrange("p (f c) -> p f c", f=16),
                in_=wq_r.rearrange("(f p) c -> p f c", p=128)),
                wait=[("gw", P.val("gw")), ("pe", P.val("pe"))] + p6_entry,
                inc=[("gw", 16)], dma=True)
            P.do("gpsimd", lambda e: e.dma_start(out=keysb[:], in_=keys_own),
                 wait=[("gw", P.val("gw"))] + p6_entry, inc=[("gw", 16)], dma=True)
            gw_p6 = P.val("gw")
            for f in range(16):
                for half in range(2):
                    P.do("tensor", lambda t, f=f, half=half: t.matmul(
                        qr_ps[:, 512 * half:512 * (half + 1)],
                        wqr[:, 128 * f:128 * (f + 1)],
                        h_fm[f][:, 512 * half:512 * (half + 1)],
                        start=(f == 0), stop=(f == 15)),
                        wait=(([("gw", gw_p6), ("ve", ve_h)] + p6_entry)
                              if (f == 0 and half == 0) else ()),
                        inc=([("pe", 1)] if (f == 15 and half == 1) else ()))
            pe_qr = P.val("pe")
            P.do("scalar", lambda s: s.copy(q_rb[:, 0:TOK], qr_ps[0:64, :]),
                 wait=[("pe", pe_qr)])
            P.do("scalar", lambda s: s.copy(q_rb[:, TOK:BT], qr_ps[64:128, :]),
                 inc=[("se", 1)])
            se_qrb = P.val("se")
            P.do("vector", lambda v: v.memset(contrib_sb[:], 0.0))
            scps_free = [[], []]
            scr_free = [0, 0]
            for i in range(16):
                par = i % 2
                P.do("tensor", lambda t, i=i, par=par: t.matmul(
                    sc_ps[par][:], q_rb[:, 128 * i:128 * (i + 1)], keysb[:],
                    start=True, stop=True),
                    wait=[("se", se_qrb)] + scps_free[par], inc=[("pe", 1)])
                pe_sc = P.val("pe")
                P.do("scalar", lambda s, par=par: s.copy(scr[par][:], sc_ps[par][:]),
                     wait=[("pe", pe_sc), ("ve", scr_free[par])], inc=[("se", 1)])
                se_scr = P.val("se")
                scps_free[par] = [("se", se_scr)]
                P.do("vector", lambda v, i=i, par=par: v.max(
                    out=contrib_sb[:, 16 * i:16 * i + 8], in_=scr[par][:]),
                    wait=[("se", se_scr)], inc=[("ve", 1)])
                ve_m8 = P.val("ve")
                P.do("vector", lambda v, i=i, par=par: v.max_index(
                    out=u8t[:], in_max=contrib_sb[:, 16 * i:16 * i + 8],
                    in_values=scr[par][:]),
                    wait=[("ve", ve_m8)], inc=[("ve", 1)])
                P.do("vector", lambda v, i=i: v.tensor_scalar(
                    out=contrib_sb[:, 16 * i + 8:16 * i + 16], in0=u8t[:],
                    scalar1=0.0, scalar2=None, op0=ALU.add),
                    inc=[("ve", 1)])
                scr_free[par] = P.val("ve")
            ve_contrib = P.val("ve")
            P.do("gpsimd", lambda e: e.dma_start(
                out=contrib.ap().rearrange("(i p) f -> p i f", p=128),
                in_=contrib_sb.ap().rearrange("p (i f) -> p i f", i=16)),
                wait=[("gg", P.val("gg")), ("ve", ve_contrib)],
                inc=[("gg", 16)], dma=True)
            gg_contrib = P.val("gg")
            P.do("gpsimd", lambda e: e.collective_compute(
                "AllGather", ALU.bypass, replica_groups=[list(range(8))],
                ins=[contrib.ap()], outs=[ag_out.ap()]),
                wait=[("gg", gg_contrib)], inc=[("cc", 1)], dma=True)
            cc_done = P.val("cc")

        NSL = 16
        with ExitStack() as p7:
            sb7 = lambda n, s, d=F32: p7.enter_context(nc.sbuf_tensor(n, s, d))
            ps7 = lambda n, s, d=F32: p7.enter_context(nc.psum_tensor(n, s, d))
            gwt = [sb7(f"p7_gw{p}", [128, 16 * 512], BF16) for p in range(2)]
            uwt = [sb7("p7_uw0", [128, 16 * 512], BF16)]
            dwt = [sb7("p7_dw0", [128, 4 * D], BF16)]
            m_g = sb7("p7_mg", [128, TOK], BF16)
            m_s = [sb7(f"p7_ms{i}", [128, TOK], BF16) for i in range(4)]
            g_ps = ps7("p7_g", [128, TOK])
            u_ps = ps7("p7_u", [128, TOK])
            y_ps = [ps7(f"p7_y{p}", [128, TOK]) for p in range(2)]

            gv = gate_w.rearrange("(f p) (sl c) -> p f sl c", p=128, sl=NSL)
            uv = up_w.rearrange("(f p) (sl c) -> p f sl c", p=128, sl=NSL)
            dv = down_w.rearrange("(kc p) c -> p kc c", p=128)

            p7_entry = [("ve", P.val("ve")), ("se", P.val("se")),
                        ("pe", P.val("pe")), ("sl", P.val("sl"))]
            gps_free = p7_entry
            ups_free = p7_entry
            yps_free = [p7_entry, p7_entry]
            wt_free = [0, 0]
            ms_free = [0] * 4
            mg_free = 0
            dw_free = 0
            uw_free = 0

            for sl in range(NSL):
                par = sl % 2
                P.do("gpsimd", lambda e, sl=sl, par=par: e.dma_start(
                    out=gwt[par].ap().rearrange("p (f c) -> p f c", f=16),
                    in_=gv[:, :, sl, :]),
                    wait=[("gw", P.val("gw")), ("pe", wt_free[par])] +
                         (p7_entry if sl < 2 else []),
                    inc=[("gw", 16)], dma=True)
                P.do("gpsimd", lambda e, sl=sl: e.dma_start(
                    out=uwt[0].ap().rearrange("p (f c) -> p f c", f=16),
                    in_=uv[:, :, sl, :]),
                    wait=[("gw", P.val("gw")), ("pe", uw_free)],
                    inc=[("gw", 16)], dma=True)
                P.do("gpsimd", lambda e, sl=sl: e.dma_start(
                    out=dwt[0].ap().rearrange("p (kc c) -> p kc c", kc=4),
                    in_=dv[:, 4 * sl:4 * (sl + 1), :]),
                    wait=[("gw", P.val("gw")), ("pe", dw_free)],
                    inc=[("gw", 16)], dma=True)
                gw_sl = P.val("gw")
                for ic in range(4):
                    for f in range(16):
                        for half in range(2):
                            P.do("tensor", lambda t, par=par, ic=ic, f=f, half=half:
                                 t.matmul(g_ps[:, 512 * half:512 * (half + 1)],
                                          gwt[par][:, f * 512 + 128 * ic:
                                                   f * 512 + 128 * (ic + 1)],
                                          h_fm[f][:, 512 * half:512 * (half + 1)],
                                          start=(f == 0), stop=(f == 15)),
                                 wait=(([("gw", gw_sl), ("ve", ve_h)] + gps_free)
                                       if (f == 0 and half == 0) else ()),
                                 inc=([("pe", 1)] if (f == 15 and half == 1) else ()))
                    pe_g = P.val("pe")
                    P.do("scalar", lambda s: s.activation(m_g[:], g_ps[:], AF.Silu),
                         wait=[("pe", pe_g), ("ve", mg_free)], inc=[("se", 1)])
                    gps_free = [("se", P.val("se"))]
                    for f in range(16):
                        for half in range(2):
                            P.do("tensor", lambda t, par=par, ic=ic, f=f, half=half:
                                 t.matmul(u_ps[:, 512 * half:512 * (half + 1)],
                                          uwt[0][:, f * 512 + 128 * ic:
                                                 f * 512 + 128 * (ic + 1)],
                                          h_fm[f][:, 512 * half:512 * (half + 1)],
                                          start=(f == 0), stop=(f == 15)),
                                 wait=(([("gw", gw_sl)] + ups_free)
                                       if (f == 0 and half == 0) else ()),
                                 inc=([("pe", 1)] if (f == 15 and half == 1) else ()))
                    pe_u = P.val("pe")
                    if ic == 3:
                        pe_up_last = P.val("pe")
                    P.do("vector", lambda v, ic=ic: v.tensor_mul(
                        m_s[ic][:], m_g[:], u_ps[:]),
                        wait=[("pe", pe_u), ("se", P.val("se")),
                              ("pe", ms_free[ic])],
                        inc=[("ve", 1)])
                    ups_free = [("ve", P.val("ve"))]
                    mg_free = P.val("ve")
                for o in range(16):
                    ypar = o % 2
                    for kc in range(4):
                        for half in range(2):
                            P.do("tensor", lambda t, par=par, o=o, kc=kc, half=half,
                                 ypar=ypar: t.matmul(
                                     y_ps[ypar][:, 512 * half:512 * (half + 1)],
                                     dwt[0][:, kc * D + 128 * o:
                                             kc * D + 128 * (o + 1)],
                                     m_s[kc][:, 512 * half:512 * (half + 1)],
                                     start=(kc == 0), stop=(kc == 3)),
                                 wait=(([("ve", P.val("ve"))] + yps_free[ypar])
                                       if (kc == 0 and half == 0) else ()),
                                 inc=([("pe", 1)] if (kc == 3 and half == 1) else ()))
                    pe_y = P.val("pe")
                    P.do("vector", lambda v, o=o, ypar=ypar: v.tensor_add(
                        x1[o][:], x1[o][:], y_ps[ypar][:]),
                        wait=[("pe", pe_y)], inc=[("ve", 1)])
                    yps_free[ypar] = [("ve", P.val("ve"))]
                pe_down = P.val("pe")
                for ic in range(4):
                    ms_free[ic] = pe_down
                wt_free[par] = pe_down
                dw_free = pe_down
                uw_free = pe_up_last
            ve_x2m = P.val("ve")

        h_tm = [sb(f"h_tm{i}", [128, D], BF16) for i in range(8)]
        with ExitStack() as p8t:
            tp2_ps = [p8t.enter_context(
                nc.psum_tensor(f"p8_tp{p}", [128, 512], BF16)) for p in range(2)]
            p8_entry = [("ve", P.val("ve")), ("se", P.val("se"))]
            tp2_free = [0, 0]
            for nt in range(8):
                for grp in range(4):
                    gpar = grp % 2
                    for m in range(4):
                        o = grp * 4 + m
                        P.do("tensor", lambda t, nt=nt, gpar=gpar, m=m, o=o:
                             t.transpose(tp2_ps[gpar][:, 128 * m:128 * (m + 1)],
                                         h_fm[o][:, 128 * nt:128 * (nt + 1)],
                                         identbf[:]),
                             wait=(([("se", tp2_free[gpar])] +
                                    (p8_entry if nt == 0 and grp == 0 else []))
                                   if m == 0 else ()),
                             inc=[("pe", 1)] if m == 3 else ())
                    pe_t8 = P.val("pe")
                    for m in range(4):
                        o = grp * 4 + m
                        P.do("scalar", lambda s, nt=nt, gpar=gpar, m=m, o=o: s.copy(
                            h_tm[nt][:, 128 * o:128 * (o + 1)],
                            tp2_ps[gpar][:, 128 * m:128 * (m + 1)]),
                            wait=[("pe", pe_t8)] if m == 0 else (),
                            inc=[("se", 1)] if m == 3 else ())
                    tp2_free[gpar] = P.val("se")
            se_htm = P.val("se")

        with ExitStack() as p8:
            sb8 = lambda n, s, d=F32: p8.enter_context(nc.sbuf_tensor(n, s, d))
            ps8 = lambda n, s, d=F32: p8.enter_context(nc.psum_tensor(n, s, d))
            agidx_sb = sb8("p8_agidx", [128, 16], I32)
            sxy = sb8("p8_sxy", [128, 32])
            comb = sb8("p8_comb", [128, 64])
            emat = sb8("p8_emat", [128, 64])
            ix64 = sb8("p8_ix64", [128, 8])
            v8p = sb8("p8_v8", [128, 8])
            p8u = sb8("p8_p8u", [128, 8], U32)
            p8f = sb8("p8_p8f", [128, 8])
            dummy64 = sb8("p8_dummy", [128, 64])
            e8 = sb8("p8_e8", [128, 8])
            mxr = sb8("p8_mx", [128, 1])
            ssum = sb8("p8_ss", [128, 1])
            pe8 = sb8("p8_pe8", [128, 8])
            prall = sb8("p8_prall", [128, 64])
            eidxall = sb8("p8_eidx", [128, 64], I32)
            ewr = sb8("p8_ewr", [128, 8])
            ewf = sb8("p8_ewf", [128, 8])
            de_t = [sb8(f"p8_de{p}", [128, D], BF16) for p in range(2)]
            ue_t = [sb8(f"p8_ue{p}", [128, D], BF16) for p in range(2)]
            prod = sb8("p8_prod", [128, D])
            acc = [sb8(f"p8_acc{p}", [128, D]) for p in range(2)]
            tp3_ps = [ps8(f"p8_tp3{p}", [128, 512]) for p in range(2)]

            p8_bar = [("ve", P.val("ve")), ("se", P.val("se")), ("pe", P.val("pe"))]
            P.do("sync", lambda e: e.dma_start(out=agidx_sb[:], in_=ag_idx),
                 wait=p8_bar, inc=[("hl", 16)], dma=True)
            hl_agidx = P.val("hl")

            sxy_free = 0
            for nt in range(8):
                for which in range(2):
                    P.do("gpsimd", lambda e, nt=nt, which=which: e.indirect_dma_start(
                        out=sxy[:, 16 * which:16 * (which + 1)], out_offset=None,
                        in_=ag_out.ap(),
                        in_offset=bass.IndirectOffsetOnAxis(
                            agidx_sb[:, 8 * which + nt:8 * which + nt + 1], 0)),
                        wait=[("gg", P.val("gg")), ("cc", cc_done),
                              ("hl", hl_agidx), ("ve", sxy_free),
                              ("sl", P.val("sl"))] +
                             (p8_bar if (nt == 0 and which == 0) else []),
                        inc=[("gg", 16)], dma=True)
                gg_sxy = P.val("gg")
                P.do("vector", lambda v: v.tensor_scalar(
                    out=ix64[:], in0=sxy[:, 8:16], scalar1=64.0, scalar2=None,
                    op0=ALU.mult), wait=[("gg", gg_sxy)])
                for i in range(8):
                    P.do("vector", lambda v, i=i: v.tensor_scalar(
                        out=comb[:, 8 * i:8 * (i + 1)], in0=sxy[:, 16:24],
                        scalar1=sxy[:, i:i + 1], scalar2=None, op0=ALU.add))
                    P.do("vector", lambda v, i=i: v.tensor_scalar(
                        out=emat[:, 8 * i:8 * (i + 1)], in0=sxy[:, 24:32],
                        scalar1=ix64[:, i:i + 1], scalar2=None, op0=ALU.add))
                P.do("vector", lambda v: v.max(out=v8p[:], in_=comb[:]),
                     inc=[("ve", 1)])
                ve_v8 = P.val("ve")
                sxy_free = ve_v8
                P.do("vector", lambda v: v.max_index(out=p8u[:], in_max=v8p[:],
                                                     in_values=comb[:]),
                     wait=[("ve", ve_v8)], inc=[("ve", 1)])
                ve_p8 = P.val("ve")
                P.do("vector", lambda v: v.tensor_scalar(
                    out=p8f[:], in0=p8u[:], scalar1=0.0, scalar2=None, op0=ALU.add),
                    wait=[("ve", ve_p8)])
                for s8 in range(8):
                    P.do("vector", lambda v, s8=s8: v.scalar_tensor_tensor(
                        out=dummy64[:], in0=iota64[:], scalar=p8f[:, s8:s8 + 1],
                        in1=emat[:], op0=ALU.is_equal, op1=ALU.mult,
                        accum_out=e8[:, s8:s8 + 1]))
                P.do("vector", lambda v, nt=nt: v.tensor_scalar(
                    out=eidxall[:, 8 * nt:8 * (nt + 1)], in0=e8[:], scalar1=0.0,
                    scalar2=None, op0=ALU.add))
                P.do("vector", lambda v: v.reduce_max(mxr[:], v8p[:], axis=AX.X))
                P.do("vector", lambda v: v.tensor_scalar(
                    out=mxr[:], in0=mxr[:], scalar1=-1.0, scalar2=None, op0=ALU.mult),
                    inc=[("ve", 1)])
                ve_mx = P.val("ve")
                P.do("scalar", lambda s: s.activation(pe8[:], v8p[:], AF.Exp,
                                                      bias=mxr[:, 0:1],
                                                      accum_out=ssum[:, 0:1]),
                     wait=[("ve", ve_mx)], inc=[("se", 1)])
                se_pe8 = P.val("se")
                P.do("vector", lambda v: v.reciprocal(mxr[:], ssum[:]),
                     wait=[("se", se_pe8)])
                P.do("vector", lambda v, nt=nt: v.tensor_scalar(
                    out=prall[:, 8 * nt:8 * (nt + 1)], in0=pe8[:],
                    scalar1=mxr[:, 0:1], scalar2=None, op0=ALU.mult),
                    inc=[("ve", 1)])
                if debug >= 4 and nt == 0:
                    P.do("scalar", lambda s: s.dma_start(out=dbg["sxy0"], in_=sxy[:]),
                         wait=[("ve", P.val("ve"))], inc=[("sl", 16)], dma=True)
                    sxy_free = max(sxy_free, 0)
            ve_peerA = P.val("ve")
            if debug >= 4:
                P.do("scalar", lambda s: s.dma_start(out=dbg["eidx"], in_=eidxall[:]),
                     wait=[("ve", ve_peerA)], inc=[("sl", 16)], dma=True)
                P.do("scalar", lambda s: s.dma_start(out=dbg["prall"], in_=prall[:]),
                     inc=[("sl", 16)], dma=True)

            de_free = [0, 0]
            for nt in range(8):
                for s8 in range(8):
                    par = s8 % 2
                    P.do("gpsimd", lambda e, nt=nt, s8=s8, par=par:
                         e.indirect_dma_start(
                             out=de_t[par][:], out_offset=None, in_=down_embed,
                             in_offset=bass.IndirectOffsetOnAxis(
                                 eidxall[:, 8 * nt + s8:8 * nt + s8 + 1], 0)),
                         wait=[("gg", P.val("gg")), ("ve", de_free[par]),
                               ("sl", P.val("sl"))],
                         inc=[("gg", 16)], dma=True)
                    gg_de = P.val("gg")
                    P.do("vector", lambda v, nt=nt, par=par: v.tensor_mul(
                        prod[:], de_t[par][:], h_tm[nt][:]),
                        wait=[("gg", gg_de), ("se", se_htm)])
                    de_free[par] = P.val("ve")
                    P.do("vector", lambda v, s8=s8: v.reduce_sum(
                        ewr[:, s8:s8 + 1], prod[:], axis=AX.X))
                P.do("vector", lambda v: v.engine_nop(), inc=[("ve", 1)])
                ve_ewr = P.val("ve")
                P.do("scalar", lambda s: s.activation(pe8[:], ewr[:], AF.Silu),
                     wait=[("ve", ve_ewr)], inc=[("se", 1)])
                se_silu = P.val("se")
                P.do("vector", lambda v, nt=nt: v.tensor_mul(
                    ewf[:], pe8[:], prall[:, 8 * nt:8 * (nt + 1)]),
                    wait=[("se", se_silu)], inc=[("ve", 1)])
                if debug >= 4 and nt == 0:
                    P.do("scalar", lambda s: s.dma_start(out=dbg["ewr0"], in_=ewr[:]),
                         wait=[("ve", P.val("ve"))], inc=[("sl", 16)], dma=True)
                    P.do("scalar", lambda s: s.dma_start(out=dbg["ewf0"], in_=ewf[:]),
                         inc=[("sl", 16)], dma=True)
                for s8 in range(8):
                    par = s8 % 2
                    P.do("gpsimd", lambda e, nt=nt, s8=s8, par=par:
                         e.indirect_dma_start(
                             out=ue_t[par][:], out_offset=None, in_=up_embed,
                             in_offset=bass.IndirectOffsetOnAxis(
                                 eidxall[:, 8 * nt + s8:8 * nt + s8 + 1], 0)),
                         wait=[("gg", P.val("gg")), ("ve", de_free[par])],
                         inc=[("gg", 16)], dma=True)
                    gg_ue = P.val("gg")
                    if s8 == 0:
                        P.do("vector", lambda v, s8=s8, par=par: v.tensor_scalar(
                            out=acc[0][:], in0=ue_t[par][:],
                            scalar1=ewf[:, s8:s8 + 1], scalar2=None, op0=ALU.mult),
                            wait=[("gg", gg_ue)])
                    else:
                        P.do("vector", lambda v, s8=s8, par=par: v.scalar_tensor_tensor(
                            out=acc[s8 % 2][:], in0=ue_t[par][:],
                            scalar=ewf[:, s8:s8 + 1], in1=acc[(s8 - 1) % 2][:],
                            op0=ALU.mult, op1=ALU.add),
                            wait=[("gg", gg_ue)])
                    de_free[par] = P.val("ve")
                P.do("vector", lambda v: v.engine_nop(), inc=[("ve", 1)])
                ve_acc = P.val("ve")
                accf = acc[7 % 2]
                if debug >= 4 and nt == 0:
                    P.do("scalar", lambda s, accf=accf: s.dma_start(
                        out=dbg["acc0"], in_=accf[:]),
                        wait=[("ve", ve_acc)], inc=[("sl", 16)], dma=True)
                for grp in range(4):
                    gpar = grp % 2
                    for m in range(4):
                        o = grp * 4 + m
                        P.do("tensor", lambda t, gpar=gpar, m=m, o=o, accf=accf:
                             t.transpose(tp3_ps[gpar][:, 128 * m:128 * (m + 1)],
                                         accf[:, 128 * o:128 * (o + 1)], ident32[:]),
                             wait=(([("ve", ve_acc)] if grp == 0 else
                                    [("ve", P.val("ve"))]) if m == 0 else []),
                             inc=[("pe", 1)] if m == 3 else ())
                    pe_t3 = P.val("pe")
                    for m in range(4):
                        o = grp * 4 + m
                        P.do("vector", lambda v, nt=nt, gpar=gpar, m=m, o=o:
                             v.tensor_add(x1[o][:, 128 * nt:128 * (nt + 1)],
                                          x1[o][:, 128 * nt:128 * (nt + 1)],
                                          tp3_ps[gpar][:, 128 * m:128 * (m + 1)]),
                             wait=[("pe", pe_t3)] if m == 0 else (),
                             inc=[("ve", 1)])
            ve_x2 = P.val("ve")

        with ExitStack() as p9:
            sb9 = lambda n, s, d=F32: p9.enter_context(nc.sbuf_tensor(n, s, d))
            ps9 = lambda n, s, d=F32: p9.enter_context(nc.psum_tensor(n, s, d))
            ot = [sb9(f"p9_ot{p}", [128, D]) for p in range(2)]
            xop = [ps9(f"p9_xp{p}", [128, 512]) for p in range(2)]
            p9_entry = [("ve", P.val("ve")), ("se", P.val("se"))]
            xop_free = [p9_entry, p9_entry]
            ot_free = [0, 0]
            for nt in range(8):
                par = nt % 2
                for grp in range(4):
                    gpar = grp % 2
                    for m in range(4):
                        o = grp * 4 + m
                        P.do("tensor", lambda t, nt=nt, gpar=gpar, m=m, o=o:
                             t.transpose(xop[gpar][:, 128 * m:128 * (m + 1)],
                                         x1[o][:, 128 * nt:128 * (nt + 1)],
                                         ident32[:]),
                             wait=(([("ve", ve_x2)] + xop_free[gpar]) if m == 0
                                   else ()),
                             inc=[("pe", 1)] if m == 3 else ())
                    pe_x9 = P.val("pe")
                    xop_free[gpar] = []
                    for m in range(4):
                        o = grp * 4 + m
                        P.do("vector", lambda v, par=par, gpar=gpar, m=m, o=o:
                             v.tensor_copy(ot[par][:, 128 * o:128 * (o + 1)],
                                           xop[gpar][:, 128 * m:128 * (m + 1)]),
                             wait=(([("pe", pe_x9)] +
                                    ([("sl", ot_free[par])] if grp == 0 else []))
                                   if m == 0 else []),
                             inc=[("ve", 1)])
                    xop_free[gpar] = [("ve", P.val("ve"))]
                ve_ot = P.val("ve")
                P.do("scalar", lambda s, nt=nt, par=par: s.dma_start(
                    out=out[128 * nt:128 * (nt + 1), :], in_=ot[par][:]),
                    wait=[("ve", ve_ot)], inc=[("sl", 16)], dma=True)
                ot_free[par] = P.val("sl")
            sl_out = P.val("sl")

        if debug:
            for o in range(16):
                P.do("scalar", lambda s, o=o: s.dma_start(
                    out=dbg["x1"][128 * o:128 * (o + 1), :], in_=x1[o][:]),
                    wait=[("ve", ve_x2)] if o == 0 else (), inc=[("sl", 16)], dma=True)

        P.do("sync", lambda e: e.nop(), wait=[("sl", P.val("sl"))])

        block = top.enter_context(nc.Block())
        P.replay(block)

    return nc


_NC_CACHE = {}


def _get_nc():
    if "nc" not in _NC_CACHE:
        _NC_CACHE["nc"] = build(debug=0)
    return _NC_CACHE["nc"]


def _ag_idx_for_core(c):
    idx = np.zeros((128, 16), np.int32)
    c1 = c // 2
    for nt in range(8):
        p = np.arange(128)
        n = 1024 * c + 128 * nt + p
        tok_local = n // 2 - 1024 * c1
        half = n % 2
        idx[:, nt] = c1 * 2048 + half * 1024 + tok_local
        idx[:, 8 + nt] = (4 + c1) * 2048 + half * 1024 + tok_local
    return idx


def kernel(L0_attn_x, L0_attn_w_qkv, L0_attn_w_proj, L0_attn_lambda_q1,
           L0_attn_lambda_k1, L0_attn_lambda_q2, L0_attn_lambda_k2,
           L0_attn_rms_scale, L0_ffn_W_q, L0_ffn_keys, L0_ffn_down_embed,
           L0_ffn_up_embed, L0_ffn_gate_w, L0_ffn_up_w, L0_ffn_down_w):
    x = np.ascontiguousarray(np.asarray(L0_attn_x, dtype=np.float32))
    keys = np.ascontiguousarray(np.asarray(L0_ffn_keys, dtype=np.float32))
    common = {
        "w_qkv": np.ascontiguousarray(L0_attn_w_qkv, dtype=np.float32),
        "w_proj": np.ascontiguousarray(L0_attn_w_proj, dtype=np.float32),
        "lam_q1": np.ascontiguousarray(L0_attn_lambda_q1, dtype=np.float32),
        "lam_k1": np.ascontiguousarray(L0_attn_lambda_k1, dtype=np.float32),
        "lam_q2": np.ascontiguousarray(L0_attn_lambda_q2, dtype=np.float32),
        "lam_k2": np.ascontiguousarray(L0_attn_lambda_k2, dtype=np.float32),
        "rms_scale": np.ascontiguousarray(L0_attn_rms_scale, dtype=np.float32),
        "wq_r": np.ascontiguousarray(L0_ffn_W_q, dtype=np.float32),
        "down_embed": np.ascontiguousarray(L0_ffn_down_embed, dtype=np.float32),
        "up_embed": np.ascontiguousarray(L0_ffn_up_embed, dtype=np.float32),
        "gate_w": np.ascontiguousarray(L0_ffn_gate_w, dtype=np.float32),
        "up_w": np.ascontiguousarray(L0_ffn_up_w, dtype=np.float32),
        "down_w": np.ascontiguousarray(L0_ffn_down_w, dtype=np.float32),
    }
    in_maps = []
    for c in range(8):
        b, half = c // 2, c % 2
        im = dict(common)
        im["x_own"] = np.ascontiguousarray(x[b, 1024 * half:1024 * (half + 1)])
        im["x_partner"] = np.ascontiguousarray(
            x[b, 1024 * (1 - half):1024 * (2 - half)])
        im["keys_own"] = np.ascontiguousarray(keys[c // 4])
        im["ag_idx"] = _ag_idx_for_core(c)
        in_maps.append(im)
    nc = _get_nc()
    res = run_bass_kernel_spmd(nc, in_maps, core_ids=list(range(8)))
    outp = np.empty((4, 2048, 2048), np.float32)
    for c in range(8):
        b, half = c // 2, c % 2
        outp[b, 1024 * half:1024 * (half + 1)] = np.asarray(res.results[c]["out"])
    return outp


# version 03
# speedup vs baseline: 1.2509x; 1.2509x over previous
import math
import numpy as np
from contextlib import ExitStack

import concourse.bass as bass
import concourse.mybir as mybir
from concourse.bass_utils import run_bass_kernel_spmd

F32 = mybir.dt.float32
BF16 = mybir.dt.bfloat16
I32 = mybir.dt.int32
U32 = mybir.dt.uint32
AF = mybir.ActivationFunctionType
ALU = mybir.AluOpType
AX = mybir.AxisListType

LI = 0.8 - 0.6 * math.exp(-0.3)
D = 2048
TOK = 1024
BT = 2048
NF = 16

ENGS = ("sync", "vector", "scalar", "tensor", "gpsimd")


class Prog:

    def __init__(self, nc, ctx):
        self.nc = nc
        self.ctx = ctx
        self.q = {e: [] for e in ENGS}
        self.sems = {}

    def sem(self, name):
        h = self.ctx.enter_context(self.nc.semaphore(name))
        self.sems[name] = [h, 0]
        return name

    def val(self, name):
        return self.sems[name][1]

    CHAIN = {"vector": "ve", "scalar": "se", "gpsimd": "gp"}

    def do(self, eng, fn, wait=(), inc=(), dma=False):
        wait = list(wait)
        inc = list(inc)
        ch = self.CHAIN.get(eng)
        if ch and not dma:
            wait.append((ch, self.sems[ch][1]))
            if not any(s == ch for s, _ in inc):
                inc.append((ch, 1))
        if dma:
            for s, n in inc:
                wait.append((s, self.sems[s][1]))
        incs = []
        for s, n in inc:
            self.sems[s][1] += n
            incs.append((self.sems[s][0], n))
        waits = [(self.sems[s][0], v) for s, v in wait if v > 0]

        def run(e, fn=fn, waits=waits, incs=incs):
            for hh, v in waits:
                e.wait_ge(hh, v)
            r = fn(e)
            for hh, n in incs:
                r.then_inc(hh, n)

        self.q[eng].append(run)

    def replay(self, block):
        def make(lst):
            def f(e):
                for fn in lst:
                    fn(e)
            return f
        block.sync(make(self.q["sync"]))
        block.vector(make(self.q["vector"]))
        block.scalar(make(self.q["scalar"]))
        block.tensor(make(self.q["tensor"]))
        block.gpsimd(make(self.q["gpsimd"]))


def build(debug=0):
    nc = bass.Bass("TRN2", target_bir_lowering=False)

    x_own = nc.dram_tensor("x_own", [TOK, D], F32, kind="ExternalInput").ap()
    x_partner = nc.dram_tensor("x_partner", [TOK, D], F32, kind="ExternalInput").ap()
    w_qkv = nc.dram_tensor("w_qkv", [D, 3 * D], F32, kind="ExternalInput").ap()
    w_proj = nc.dram_tensor("w_proj", [D, D], F32, kind="ExternalInput").ap()
    lam_q1 = nc.dram_tensor("lam_q1", [64], F32, kind="ExternalInput").ap()
    lam_k1 = nc.dram_tensor("lam_k1", [64], F32, kind="ExternalInput").ap()
    lam_q2 = nc.dram_tensor("lam_q2", [64], F32, kind="ExternalInput").ap()
    lam_k2 = nc.dram_tensor("lam_k2", [64], F32, kind="ExternalInput").ap()
    rms_scale = nc.dram_tensor("rms_scale", [128], F32, kind="ExternalInput").ap()
    wq_r = nc.dram_tensor("wq_r", [D, 128], F32, kind="ExternalInput").ap()
    keys_own = nc.dram_tensor("keys_own", [64, 64], F32, kind="ExternalInput").ap()
    down_embed = nc.dram_tensor("down_embed", [4096, D], F32, kind="ExternalInput").ap()
    up_embed = nc.dram_tensor("up_embed", [4096, D], F32, kind="ExternalInput").ap()
    gate_w = nc.dram_tensor("gate_w", [D, 8192], F32, kind="ExternalInput").ap()
    up_w = nc.dram_tensor("up_w", [D, 8192], F32, kind="ExternalInput").ap()
    down_w = nc.dram_tensor("down_w", [8192, D], F32, kind="ExternalInput").ap()
    ag_idx = nc.dram_tensor("ag_idx", [128, 16], I32, kind="ExternalInput").ap()
    contrib = nc.dram_tensor("contrib", [2048, 16], F32)
    ag_out = nc.dram_tensor("ag_out", [8 * 2048, 16], F32)

    out = nc.dram_tensor("out", [TOK, D], F32, kind="ExternalOutput").ap()
    dbg = {}
    if debug:
        dbg["x1"] = nc.dram_tensor("dbg_x1", [D, TOK], F32, kind="ExternalOutput").ap()
    if debug >= 3:
        dbg["x1a"] = nc.dram_tensor("dbg_x1a", [D, TOK], F32, kind="ExternalOutput").ap()
        dbg["h"] = nc.dram_tensor("dbg_h", [D, TOK], BF16, kind="ExternalOutput").ap()
    if debug >= 4:
        dbg["eidx"] = nc.dram_tensor("dbg_eidx", [128, 64], I32, kind="ExternalOutput").ap()
        dbg["prall"] = nc.dram_tensor("dbg_prall", [128, 64], F32, kind="ExternalOutput").ap()
        dbg["ewr0"] = nc.dram_tensor("dbg_ewr0", [128, 8], F32, kind="ExternalOutput").ap()
        dbg["ewf0"] = nc.dram_tensor("dbg_ewf0", [128, 8], F32, kind="ExternalOutput").ap()
        dbg["acc0"] = nc.dram_tensor("dbg_acc0", [128, D], F32, kind="ExternalOutput").ap()
        dbg["sxy0"] = nc.dram_tensor("dbg_sxy0", [128, 32], F32, kind="ExternalOutput").ap()
    if debug >= 2:
        dbg["negB"] = nc.dram_tensor("dbg_negB", [2, 128, 1], F32, kind="ExternalOutput").ap()
        dbg["q"] = nc.dram_tensor("dbg_q", [2, 128, 128], BF16, kind="ExternalOutput").ap()
        dbg["k"] = nc.dram_tensor("dbg_k", [2, 128, 128], BF16, kind="ExternalOutput").ap()
        dbg["rows"] = nc.dram_tensor("dbg_rows", [2, 3, TOK], F32, kind="ExternalOutput").ap()
        dbg["pr"] = nc.dram_tensor("dbg_pr", [2, 128, 512], BF16, kind="ExternalOutput").ap()

    with ExitStack() as top:
        P = Prog(nc, top)
        sb = lambda n, s, d=F32: top.enter_context(nc.sbuf_tensor(n, s, d))

        for s in ("hl", "sl", "gw", "gg", "cc", "pe", "ve", "se", "gp"):
            P.sem(s)

        iota64 = sb("iota64", [128, 64])
        ident32 = sb("ident32", [128, 128])
        identbf = sb("identbf", [128, 128], BF16)
        ones_col = sb("ones_col", [128, 1], BF16)
        ones_rows = sb("ones_rows", [65, 128], BF16)
        eps_ln = sb("eps_ln", [128, 1])
        ones_sq = sb("ones_sq", [128, 128], BF16)
        ones_row32 = sb("ones_row32", [1, 128])
        lam_col = sb("lam_col", [128, 1])
        li_col = sb("li_col", [128, 1])
        rms_col = sb("rms_col", [128, 1])
        lam_t = sb("lam_t", [1, 1])
        sc_a = sb("sc_a", [1, 1])
        sc_b = sb("sc_b", [1, 1])
        sc_bb = sb("sc_bb", [1, 1], BF16)

        P.do("gpsimd", lambda g: g.iota(iota64[:], pattern=[[1, 64]], base=0,
                                        channel_multiplier=0,
                                        allow_small_or_imprecise_dtypes=True),
             inc=[("gp", 1)])
        for fn in (
            lambda g: g.memset(ident32[:], 0.0),
            lambda g: g.affine_select(out=ident32[:], in_=ident32[:],
                                      compare_op=ALU.not_equal, fill=1.0, base=0,
                                      pattern=[[-1, 128]], channel_multiplier=1),
            lambda g: g.memset(identbf[:], 0.0),
            lambda g: g.affine_select(out=identbf[:], in_=identbf[:],
                                      compare_op=ALU.not_equal, fill=1.0, base=0,
                                      pattern=[[-1, 128]], channel_multiplier=1),
            lambda g: g.memset(ones_col[:], 1.0),
            lambda g: g.memset(ones_rows[:], 1.0),
            lambda g: g.memset(eps_ln[:], 1e-5),
            lambda g: g.memset(ones_sq[:], 1.0),
            lambda g: g.memset(ones_row32[:], 1.0),
            lambda g: g.memset(li_col[:], math.log(1.0 - LI)),
        ):
            P.do("gpsimd", fn, wait=[("gp", P.val("gp"))], inc=[("gp", 1)])
        gp_consts = P.val("gp")

        p0ctx = ExitStack()
        lamrows = p0ctx.enter_context(nc.sbuf_tensor("lamrows", [1, 256], F32))
        rowtmp = p0ctx.enter_context(nc.sbuf_tensor("rowtmp", [1, 64], F32))
        rowtmp2 = p0ctx.enter_context(nc.sbuf_tensor("rowtmp2", [1, 64], F32))
        P.do("sync", lambda e: e.dma_start(
            out=rms_col[:], in_=rms_scale.rearrange("(p o) -> p o", o=1),
            allow_slow_non_contiguous=True), inc=[("hl", 16)], dma=True)
        for i, t in enumerate((lam_q1, lam_k1, lam_q2, lam_k2)):
            P.do("sync", lambda e, i=i, t=t: e.dma_start(
                out=lamrows[0:1, 64 * i:64 * (i + 1)], in_=t.rearrange("(o f) -> o f", o=1)),
                inc=[("hl", 16)], dma=True)
        hl_small = P.val("hl")

        P.do("vector", lambda v: v.tensor_mul(rowtmp[:], lamrows[0:1, 0:64],
                                              lamrows[0:1, 64:128]),
             wait=[("hl", hl_small)])
        P.do("vector", lambda v: v.tensor_mul(rowtmp2[:], lamrows[0:1, 128:192],
                                              lamrows[0:1, 192:256]))
        P.do("vector", lambda v: v.reduce_sum(sc_a[:], rowtmp[:], axis=AX.X))
        P.do("vector", lambda v: v.reduce_sum(sc_b[:], rowtmp2[:], axis=AX.X),
             inc=[("ve", 1)])
        ve_ls = P.val("ve")
        P.do("scalar", lambda s: s.activation(sc_a[:], sc_a[:], AF.Exp),
             wait=[("ve", ve_ls)])
        P.do("scalar", lambda s: s.activation(sc_b[:], sc_b[:], AF.Exp),
             inc=[("se", 1)])
        se_le = P.val("se")
        P.do("vector", lambda v: v.tensor_sub(lam_t[:], sc_a[:], sc_b[:]),
             wait=[("se", se_le)])
        P.do("vector", lambda v: v.tensor_scalar(out=lam_t[:], in0=lam_t[:],
                                                 scalar1=LI, scalar2=None, op0=ALU.add),
             inc=[("ve", 1)])
        ve_lam = P.val("ve")
        with nc.psum_tensor("p0_lamps", [128, 1], F32) as lam_ps:
            P.do("tensor", lambda t: t.matmul(lam_ps[:], ones_row32[:], lam_t[:],
                                              start=True, stop=True),
                 wait=[("ve", ve_lam), ("gp", gp_consts)], inc=[("pe", 1)])
            pe_lamb = P.val("pe")
            P.do("scalar", lambda s: s.copy(lam_col[:], lam_ps[:]),
                 wait=[("pe", pe_lamb)], inc=[("se", 1)])
        se_lamcol = P.val("se")
        p0ctx.close()

        out_all = [sb(f"oa{j}", [128, TOK], BF16) for j in range(NF)]

        ln_stack = ExitStack()
        ln_fm = [ln_stack.enter_context(nc.sbuf_tensor(f"ln_fm{j}", [128, BT], BF16))
                 for j in range(NF)]

        with ExitStack() as p1:
            sb1 = lambda n, s, d=F32: p1.enter_context(nc.sbuf_tensor(n, s, d))
            ps1 = lambda n, s, d=F32: p1.enter_context(nc.psum_tensor(n, s, d))
            xt = [sb1(f"p1_xt{p}", [128, D]) for p in range(2)]
            lnt = [sb1(f"p1_lnt{p}", [128, D], BF16) for p in range(2)]
            bnbuf = [sb1(f"p1_bn{p}", [128, 4, 6]) for p in range(2)]
            bnagg = [sb1(f"p1_ag{p}", [128, 2]) for p in range(2)]
            mrow = [sb1(f"p1_m{p}", [128, 1]) for p in range(2)]
            rrow = [sb1(f"p1_r{p}", [128, 1]) for p in range(2)]
            tp_ps = [ps1(f"p1_tp{p}", [128, 512], BF16) for p in range(2)]

            xt_free = [0, 0]
            lnt_free = [0, 0]
            tp_free = [0, 0]

            for i in range(16):
                par = i % 2
                src = x_own if i < 8 else x_partner
                r0 = (i % 8) * 128
                P.do("sync", lambda e, src=src, r0=r0, par=par: e.dma_start(
                    out=xt[par][:], in_=src[r0:r0 + 128, :]),
                    wait=[("ve", xt_free[par])], inc=[("hl", 16)], dma=True)
                hl_i = P.val("hl")
                for cth in range(4):
                    P.do("vector", lambda v, par=par, cth=cth: v.bn_stats(
                        bnbuf[par][:, cth, :], xt[par][:, 512 * cth:512 * (cth + 1)]),
                        wait=[("hl", hl_i)] if cth == 0 else ())
                P.do("vector", lambda v, par=par: v.bn_aggr(
                    bnagg[par][:], bnbuf[par].ap().rearrange("p a b -> p (a b)")))
                P.do("vector", lambda v, par=par: v.tensor_copy(mrow[par][:],
                                                                bnagg[par][:, 0:1]))
                P.do("vector", lambda v, par=par: v.tensor_add(
                    rrow[par][:], bnagg[par][:, 1:2], eps_ln[:]),
                    wait=[("gp", gp_consts)], inc=[("ve", 1)])
                ve_var = P.val("ve")
                P.do("scalar", lambda s, par=par: s.activation(rrow[par][:], rrow[par][:],
                                                               AF.Ln),
                     wait=[("ve", ve_var)])
                P.do("scalar", lambda s, par=par: s.activation(rrow[par][:], rrow[par][:],
                                                               AF.Exp, scale=-0.5),
                     inc=[("se", 1)])
                se_rstd = P.val("se")
                P.do("vector", lambda v, par=par: v.tensor_scalar(
                    out=lnt[par][:], in0=xt[par][:], scalar1=mrow[par][:, 0:1],
                    scalar2=rrow[par][:, 0:1], op0=ALU.subtract, op1=ALU.mult),
                    wait=[("se", se_rstd), ("pe", lnt_free[par])], inc=[("ve", 1)])
                ve_ln = P.val("ve")
                xt_free[par] = ve_ln
                for grp in range(4):
                    gpar = grp % 2
                    for m in range(4):
                        jj = grp * 4 + m
                        P.do("tensor", lambda t, par=par, gpar=gpar, m=m, jj=jj:
                             t.transpose(tp_ps[gpar][:, 128 * m:128 * (m + 1)],
                                         lnt[par][:, 128 * jj:128 * (jj + 1)], identbf[:]),
                             wait=([("ve", ve_ln), ("se", tp_free[gpar]),
                                    ("gp", gp_consts)] if m == 0 else ()),
                             inc=[("pe", 1)] if m == 3 else ())
                    pe_tp = P.val("pe")
                    if grp == 3:
                        lnt_free[par] = pe_tp
                    for m in range(4):
                        jj = grp * 4 + m
                        P.do("scalar", lambda s, i=i, gpar=gpar, m=m, jj=jj: s.copy(
                            ln_fm[jj][:, 128 * i:128 * (i + 1)],
                            tp_ps[gpar][:, 128 * m:128 * (m + 1)]),
                            wait=[("pe", pe_tp)] if m == 0 else (),
                            inc=[("se", 1)] if m == 3 else ())
                    tp_free[gpar] = P.val("se")
            se_ln_done = P.val("se")


        wq_view = w_qkv.rearrange("(f p) (blk grp c) -> p f blk grp c",
                                  p=128, blk=6, grp=2)

        epi_ve = 0
        se_prev_ctx = se_ln_done
        ve_prev_ctx = 0
        for g in range(2):
            with ExitStack() as pg:
                sbg = lambda n, s, d=F32: pg.enter_context(nc.sbuf_tensor(n, s, d))
                q_g = [sbg(f"g{g}_q{h}", [128, TOK], BF16) for h in range(8)]
                k_g = [sbg(f"g{g}_k{h}", [128, BT], BF16) for h in range(8)]
                v_g = [sbg(f"g{g}_v{i}", [128, TOK], BF16) for i in range(16)]
                negB8 = sbg(f"g{g}_negB8", [128, 8])
                wstack = ExitStack()
                wband = wstack.enter_context(
                    nc.sbuf_tensor(f"g{g}_wband", [128, 16 * 512], BF16))
                wband_4d = wband.ap().rearrange("p (f hcs b c) -> p f hcs b c",
                                                f=16, hcs=4, b=2)
                wband_vv = wband.ap().rearrange("p (f c) -> p f c", f=16)

                def qk_lhsT(f, hc4):
                    return wband[:, f * 512 + hc4 * 128: f * 512 + (hc4 + 1) * 128]

                with ExitStack() as pq:
                    q_ps = [pq.enter_context(
                        nc.psum_tensor(f"g{g}_qps{p}", [128, TOK], F32)) for p in range(2)]
                    qps_free = [se_prev_ctx, se_prev_ctx]
                    qps_extra = [("ve", ve_prev_ctx)]
                    for hb in range(2):
                        for bnd in range(2):
                            for hc4 in range(4):
                                P.do("gpsimd", lambda e, g=g, hb=hb, bnd=bnd, hc4=hc4:
                                     e.dma_start(
                                         out=wband_4d[:, :, hc4, bnd, :],
                                         in_=wq_view[:, :, bnd, g,
                                                     256 * hb + 64 * hc4:
                                                     256 * hb + 64 * (hc4 + 1)]),
                                     wait=[("gw", P.val("gw")), ("pe", P.val("pe")),
                                           ("ve", P.val("ve")), ("se", P.val("se")),
                                           ("sl", P.val("sl"))],
                                     inc=[("gw", 16)], dma=True)
                        gw_q = P.val("gw")
                        for hc4 in range(4):
                            hc = 4 * hb + hc4
                            par = hc % 2
                            for f in range(16):
                                for half in range(2):
                                    P.do("tensor",
                                         lambda t, hc4=hc4, f=f, half=half, par=par:
                                         t.matmul(q_ps[par][:, 512 * half:512 * (half + 1)],
                                                  qk_lhsT(f, hc4),
                                                  ln_fm[f][:, 512 * half:512 * (half + 1)],
                                                  start=(f == 0), stop=(f == 15)),
                                         wait=(([("gw", gw_q), ("se", qps_free[par]),
                                                 ("se", se_ln_done)] + qps_extra)
                                               if (f == 0 and half == 0) else ()),
                                         inc=([("pe", 1)] if (f == 15 and half == 1)
                                              else ()))
                            pe_q = P.val("pe")
                            P.do("scalar", lambda s, hc=hc, par=par: s.copy(
                                q_g[hc][:], q_ps[par][:]),
                                wait=[("pe", pe_q)], inc=[("se", 1)])
                            qps_free[par] = P.val("se")
                    se_q_end = P.val("se")

                with ExitStack() as pk:
                    k_ps = [pk.enter_context(
                        nc.psum_tensor(f"g{g}_kps{p}", [128, 512], F32)) for p in range(8)]
                    kps_free = [se_q_end] * 2
                    for hb in range(2):
                        for bnd in range(2):
                            for hc4 in range(4):
                                P.do("gpsimd", lambda e, g=g, hb=hb, bnd=bnd, hc4=hc4:
                                     e.dma_start(
                                         out=wband_4d[:, :, hc4, bnd, :],
                                         in_=wq_view[:, :, 2 + bnd, g,
                                                     256 * hb + 64 * hc4:
                                                     256 * hb + 64 * (hc4 + 1)]),
                                     wait=[("gw", P.val("gw")), ("pe", P.val("pe")),
                                           ("ve", P.val("ve")), ("se", P.val("se")),
                                           ("sl", P.val("sl"))],
                                     inc=[("gw", 16)], dma=True)
                        gw_k = P.val("gw")
                        for hc4 in range(4):
                            hc = 4 * hb + hc4
                            base = (hc % 2) * 4
                            for f in range(16):
                                for ts4 in range(4):
                                    P.do("tensor",
                                         lambda t, hc4=hc4, f=f, ts4=ts4, base=base:
                                         t.matmul(k_ps[base + ts4][:],
                                                  qk_lhsT(f, hc4),
                                                  ln_fm[f][:, 512 * ts4:512 * (ts4 + 1)],
                                                  start=(f == 0), stop=(f == 15)),
                                         wait=([("gw", gw_k), ("se", kps_free[hc % 2])]
                                               if (f == 0 and ts4 == 0) else ()),
                                         inc=([("pe", 1)] if (f == 15 and ts4 == 3)
                                              else ()))
                            pe_k = P.val("pe")
                            for ts4 in range(4):
                                P.do("scalar", lambda s, hc=hc, ts4=ts4, base=base: s.copy(
                                    k_g[hc][:, 512 * ts4:512 * (ts4 + 1)],
                                    k_ps[base + ts4][:]),
                                    wait=[("pe", pe_k)] if ts4 == 0 else (),
                                    inc=[("se", 1)] if ts4 == 3 else ())
                            kps_free[hc % 2] = P.val("se")
                    se_k_end = P.val("se")

                with ExitStack() as pv:
                    v_ps = [pv.enter_context(
                        nc.psum_tensor(f"g{g}_vps{p}", [128, 512], F32)) for p in range(2)]
                    vps_free = [se_k_end, se_k_end]
                    vsrc = wq_view[:, :, 4 + g, :, :].rearrange("p f grp c -> p f (grp c)")
                    for hb in range(2):
                        P.do("gpsimd", lambda e, hb=hb, vsrc=vsrc: e.dma_start(
                            out=wband_vv, in_=vsrc[:, :, 512 * hb:512 * (hb + 1)]),
                            wait=[("gw", P.val("gw")), ("pe", P.val("pe")),
                                  ("ve", P.val("ve")), ("se", P.val("se")),
                                  ("sl", P.val("sl"))],
                            inc=[("gw", 16)], dma=True)
                        gw_v = P.val("gw")
                        for ti in range(16):
                            par = ti % 2
                            for f in range(16):
                                P.do("tensor", lambda t, ti=ti, f=f, par=par: t.matmul(
                                    v_ps[par][:],
                                    ln_fm[f][:, 128 * ti:128 * (ti + 1)],
                                    wband_vv[:, f, :],
                                    start=(f == 0), stop=(f == 15)),
                                    wait=([("gw", gw_v), ("se", vps_free[par])]
                                          if f == 0 else ()),
                                    inc=([("pe", 1)] if f == 15 else ()))
                            pe_v = P.val("pe")
                            P.do("scalar", lambda s, ti=ti, hb=hb, par=par: s.copy(
                                v_g[ti][:, 512 * hb:512 * (hb + 1)], v_ps[par][:]),
                                wait=[("pe", pe_v)], inc=[("se", 1)])
                            vps_free[par] = P.val("se")
                    se_v_end = P.val("se")

                with ExitStack() as pb:
                    sqt = pb.enter_context(nc.sbuf_tensor(f"g{g}_bbsq", [128, BT], BF16))
                    qn_ps = pb.enter_context(nc.psum_tensor(f"g{g}_bbqn", [1, TOK], F32))
                    kn_ps = pb.enter_context(nc.psum_tensor(f"g{g}_bbkn", [1, BT], F32))
                    bps = pb.enter_context(nc.psum_tensor(f"g{g}_bbps", [128, 1], F32))
                    nps_free = []
                    bps_free = []
                    for hc in range(8):
                        P.do("vector", lambda v, hc=hc: v.tensor_mul(
                            sqt[:, 0:TOK], q_g[hc][:], q_g[hc][:]),
                            wait=([("se", se_v_end), ("ve", ve_lam)] if hc == 0 else
                                  [("pe", P.val("pe"))]),
                            inc=[("ve", 1)])
                        ve_sq = P.val("ve")
                        for half in range(2):
                            P.do("tensor", lambda t, half=half: t.matmul(
                                qn_ps[:, 512 * half:512 * (half + 1)], ones_col[:],
                                sqt[:, 512 * half:512 * (half + 1)],
                                start=True, stop=True),
                                wait=(([("ve", ve_sq), ("se", se_v_end)] + nps_free)
                                      if half == 0 else ()),
                                inc=[("pe", 1)] if half == 1 else ())
                        pe_qn = P.val("pe")
                        P.do("vector", lambda v, hc=hc: v.tensor_mul(
                            sqt[:], k_g[hc][:], k_g[hc][:]),
                            wait=[("pe", pe_qn)], inc=[("ve", 1)])
                        ve_sk = P.val("ve")
                        for qtr in range(4):
                            P.do("tensor", lambda t, qtr=qtr: t.matmul(
                                kn_ps[:, 512 * qtr:512 * (qtr + 1)], ones_col[:],
                                sqt[:, 512 * qtr:512 * (qtr + 1)],
                                start=True, stop=True),
                                wait=[("ve", ve_sk)] if qtr == 0 else (),
                                inc=[("pe", 1)] if qtr == 3 else ())
                        pe_kn = P.val("pe")
                        P.do("vector", lambda v: v.reduce_max(sc_a[:], qn_ps[:],
                                                              axis=AX.X),
                             wait=[("pe", pe_kn)])
                        P.do("vector", lambda v: v.reduce_max(sc_b[:], kn_ps[:],
                                                              axis=AX.X))
                        P.do("vector", lambda v: v.tensor_mul(sc_a[:], sc_a[:],
                                                              sc_b[:]),
                             inc=[("ve", 1)])
                        ve_bb = P.val("ve")
                        nps_free = [("ve", ve_bb)]
                        P.do("scalar", lambda s: s.activation(sc_a[:], sc_a[:], AF.Ln),
                             wait=[("ve", ve_bb)])
                        P.do("scalar", lambda s: s.activation(sc_a[:], sc_a[:], AF.Exp,
                                                              scale=0.5),
                             inc=[("se", 1)])
                        se_bb = P.val("se")
                        P.do("vector", lambda v: v.tensor_scalar(
                            out=sc_a[:], in0=sc_a[:], scalar1=-0.125 * 1.05,
                            scalar2=None, op0=ALU.mult), wait=[("se", se_bb)])
                        P.do("vector", lambda v: v.tensor_copy(sc_bb[:], sc_a[:]),
                             inc=[("ve", 1)])
                        ve_bB = P.val("ve")
                        P.do("tensor", lambda t: t.matmul(bps[:], ones_rows[0:1, :],
                                                          sc_bb[:],
                                                          start=True, stop=True),
                             wait=[("ve", ve_bB)] + bps_free, inc=[("pe", 1)])
                        pe_bb = P.val("pe")
                        P.do("scalar", lambda s, hc=hc: s.copy(negB8[:, hc:hc + 1],
                                                               bps[:]),
                             wait=[("pe", pe_bb)], inc=[("se", 1)])
                        bps_free = [("se", P.val("se"))]
                    se_negB = P.val("se")
                    if debug >= 2:
                        P.do("scalar", lambda s, g=g: s.dma_start(
                            out=dbg["negB"][g, :, 0:1], in_=negB8[:, 0:1]),
                            wait=[("se", se_negB)], inc=[("sl", 16)], dma=True)
                        P.do("scalar", lambda s, g=g: s.dma_start(
                            out=dbg["q"][g, :, :], in_=q_g[0][:, 0:128]),
                            inc=[("sl", 16)], dma=True)
                        P.do("scalar", lambda s, g=g: s.dma_start(
                            out=dbg["k"][g, :, :], in_=k_g[0][:, 0:128]),
                            inc=[("sl", 16)], dma=True)

                wstack.close()

                with ExitStack() as pa:
                    sba = lambda n, s, d=F32: pa.enter_context(nc.sbuf_tensor(n, s, d))
                    psa = lambda n, s, d=F32: pa.enter_context(nc.psum_tensor(n, s, d))
                    s_pair = [psa(f"g{g}_at_s{p}", [128, 1024]) for p in range(2)]
                    pv_ps = [psa(f"g{g}_at_pv{b}", [128, 512]) for b in range(2)]
                    sum_ps = [psa(f"g{g}_at_sm{b}", [128, 512]) for b in range(2)]
                    ppair = [sba(f"g{g}_at_pr{p}", [128, 1024], BF16) for p in range(2)]
                    u_t = sba(f"g{g}_at_u", [128, TOK], BF16)
                    t2b = sba(f"g{g}_at_t2", [128, TOK], BF16)
                    s1b = sba(f"g{g}_at_s1b", [128, TOK])
                    s2b = sba(f"g{g}_at_s2b", [128, TOK])

                    s_free = [[("se", se_negB)] for _ in range(2)]
                    probs_free = [[] for _ in range(2)]
                    sum_free = [[("se", se_negB)] for _ in range(2)]
                    pv_free = [[("se", se_negB)] for _ in range(2)]
                    pe_s_at = [0, 0]
                    se_exp_at = [0] * 16

                    for hc in range(8):
                        for qh in range(2):
                            def emit_scores(j, hc=hc, qh=qh):
                                par = j % 2
                                for b2 in range(2):
                                    lo, hi = (0, 64) if b2 == 0 else (64, 128)
                                    tp = (0, 0) if b2 == 0 else (64, 0)
                                    P.do("tensor",
                                         lambda t, hc=hc, qh=qh, j=j, b2=b2, lo=lo,
                                         hi=hi, tp=tp, par=par: t.matmul(
                                             s_pair[par][:, 512 * b2:512 * (b2 + 1)],
                                             k_g[hc][lo:hi, 128 * j:128 * (j + 1)],
                                             q_g[hc][lo:hi, 512 * qh:512 * (qh + 1)],
                                             start=True, stop=True, tile_position=tp),
                                         wait=s_free[par] if b2 == 0 else (),
                                         inc=[("pe", 1)] if b2 == 1 else ())
                                s_free[par] = []
                                pe_s_at[par] = P.val("pe")

                            def emit_exp(j, hc=hc):
                                par = j % 2
                                P.do("scalar", lambda s, par=par, hc=hc: s.activation(
                                    ppair[par][:], s_pair[par][:], AF.Exp,
                                    bias=negB8[:, hc:hc + 1], scale=0.125),
                                    wait=[("pe", pe_s_at[par])] + probs_free[par],
                                    inc=[("se", 1)])
                                probs_free[par] = []
                                s_free[par] = [("se", P.val("se"))]
                                se_exp_at[j] = P.val("se")

                            def emit_pv(j, hc=hc):
                                par = j % 2
                                for b2 in range(2):
                                    P.do("tensor", lambda t, hc=hc, j=j, b2=b2,
                                         par=par: t.matmul(
                                             pv_ps[b2][:],
                                             v_g[j][:, 128 * hc:128 * (hc + 1)],
                                             ppair[par][:, 512 * b2:512 * (b2 + 1)],
                                             start=(j == 0), stop=(j == 15)),
                                         wait=([("se", se_exp_at[j])] +
                                               (pv_free[b2] if j == 0 else [])))
                                    pv_free[b2] = []
                                for b2 in range(2):
                                    P.do("tensor", lambda t, j=j, b2=b2, par=par:
                                         t.matmul(
                                             sum_ps[b2][:], ones_sq[:],
                                             ppair[par][:, 512 * b2:512 * (b2 + 1)],
                                             start=(j == 0), stop=(j == 15)),
                                         wait=(sum_free[b2] if j == 0 else ()),
                                         inc=[("pe", 1)] if b2 == 1 else ())
                                    sum_free[b2] = []
                                probs_free[par] = [("pe", P.val("pe"))]

                            emit_scores(0)
                            emit_exp(0)
                            emit_scores(1)
                            emit_exp(1)
                            for j in range(16):
                                emit_pv(j)
                                if j + 2 < 16:
                                    emit_scores(j + 2)
                                    emit_exp(j + 2)
                            pe_pv = P.val("pe")
                            P.do("vector", lambda v, qh=qh: v.tensor_copy(
                                u_t[:, 512 * qh:512 * (qh + 1)], pv_ps[0][:]),
                                wait=[("pe", pe_pv)])
                            P.do("vector", lambda v, qh=qh: v.tensor_copy(
                                t2b[:, 512 * qh:512 * (qh + 1)], pv_ps[1][:]))
                            P.do("vector", lambda v, qh=qh: v.tensor_copy(
                                s1b[:, 512 * qh:512 * (qh + 1)], sum_ps[0][:]))
                            P.do("vector", lambda v, qh=qh: v.tensor_copy(
                                s2b[:, 512 * qh:512 * (qh + 1)], sum_ps[1][:]),
                                inc=[("ve", 1)])
                            ve_ev = P.val("ve")
                            for b2 in range(2):
                                pv_free[b2] = [("ve", ve_ev)]
                                sum_free[b2] = [("ve", ve_ev)]

                        P.do("vector", lambda v: v.tensor_mul(ppair[1][:], s1b[:],
                                                              s1b[:]),
                             wait=[("ve", ve_ev)] + probs_free[1])
                        probs_free[1] = []
                        P.do("vector", lambda v: v.tensor_scalar(
                            out=ppair[1][:], in0=ppair[1][:], scalar1=1e-6,
                            scalar2=None, op0=ALU.mult), inc=[("ve", 1)])
                        ve_eps = P.val("ve")
                        P.do("scalar", lambda s: s.activation(s1b[:], s1b[:], AF.Ln),
                             wait=[("se", se_lamcol), ("ve", ve_eps)])
                        P.do("scalar", lambda s: s.activation(s2b[:], s2b[:], AF.Ln),
                             inc=[("se", 1)])
                        se_ln12 = P.val("se")
                        P.do("vector", lambda v: v.tensor_sub(s1b[:], s1b[:], s2b[:]),
                             wait=[("se", se_ln12)], inc=[("ve", 1)])
                        ve_d12 = P.val("ve")
                        P.do("scalar", lambda s: s.activation(s1b[:], s1b[:], AF.Exp),
                             wait=[("ve", ve_d12)], inc=[("se", 1)])
                        se_rho = P.val("se")
                        P.do("vector", lambda v: v.scalar_tensor_tensor(
                            out=ppair[0][:], in0=t2b[:], scalar=lam_col[:, 0:1],
                            in1=s1b[:], op0=ALU.mult, op1=ALU.mult),
                            wait=[("se", se_rho)] + probs_free[0])
                        probs_free[0] = []
                        P.do("vector", lambda v: v.tensor_sub(t2b[:], u_t[:],
                                                              ppair[0][:]))
                        P.do("vector", lambda v: v.tensor_mul(ppair[0][:], t2b[:],
                                                              t2b[:]),
                             inc=[("ve", 1)])
                        ve_uu = P.val("ve")
                        for half in range(2):
                            P.do("tensor", lambda t, half=half: t.matmul(
                                sum_ps[half][:], ones_sq[:],
                                ppair[0][:, 512 * half:512 * (half + 1)],
                                start=True, stop=True),
                                wait=[("ve", ve_uu)] + sum_free[half],
                                inc=[("pe", 1)])
                            sum_free[half] = []
                        pe_ms = P.val("pe")
                        for half in range(2):
                            P.do("vector", lambda v, half=half: v.scalar_tensor_tensor(
                                out=s2b[:, 512 * half:512 * (half + 1)],
                                in0=sum_ps[half][:], scalar=1.0 / 128,
                                in1=ppair[1][:, 512 * half:512 * (half + 1)],
                                op0=ALU.mult, op1=ALU.add),
                                wait=[("pe", pe_ms)] if half == 0 else (),
                                inc=[("ve", 1)] if half == 1 else ())
                        ve_ms = P.val("ve")
                        probs_free[0] = [("ve", ve_ms)]
                        probs_free[1] = [("ve", ve_ms)]
                        for b2 in range(2):
                            sum_free[b2] = [("ve", ve_ms)]
                        P.do("scalar", lambda s: s.activation(s2b[:], s2b[:], AF.Ln),
                             wait=[("ve", ve_ms)])
                        P.do("scalar", lambda s: s.activation(s2b[:], s2b[:], AF.Exp,
                                                              scale=-0.5,
                                                              bias=li_col[:, 0:1]),
                             inc=[("se", 1)])
                        se_rst = P.val("se")
                        P.do("vector", lambda v, g=g, hc=hc: v.scalar_tensor_tensor(
                            out=out_all[g * 8 + hc][:], in0=t2b[:],
                            scalar=rms_col[:, 0:1], in1=s2b[:],
                            op0=ALU.mult, op1=ALU.mult),
                            wait=[("se", se_rst), ("hl", hl_small)], inc=[("ve", 1)])
                        epi_ve = P.val("ve")
                    se_prev_ctx = P.val("se")
                    ve_prev_ctx = P.val("ve")

        ln_stack.close()
        x1 = [sb(f"x1_{o}", [128, TOK]) for o in range(NF)]

        with ExitStack() as p4:
            sb4 = lambda n, s, d=F32: p4.enter_context(nc.sbuf_tensor(n, s, d))
            ps4 = lambda n, s, d=F32: p4.enter_context(nc.psum_tensor(n, s, d))
            xt4 = [sb4(f"p4_xt{p}", [128, D]) for p in range(2)]
            wpo = [sb4(f"p4_wp{p}", [128, D], BF16) for p in range(2)]
            xps = [ps4(f"p4_xp{p}", [128, 512]) for p in range(2)]
            pr_ps = [ps4(f"p4_pr{p}", [128, TOK]) for p in range(2)]
            wp_view = w_proj.rearrange("(f p) (o c) -> p f o c", p=128, o=16)

            pe_barrier = P.val("pe")
            ve_barrier = P.val("ve")
            se_barrier = P.val("se")
            sl_barrier = P.val("sl")
            xt4_free = [0, 0]
            xps_free = [[("ve", ve_prev_ctx), ("se", se_prev_ctx)] for _ in range(2)]
            prps_free = [[("ve", ve_prev_ctx), ("se", se_prev_ctx)] for _ in range(2)]

            for ti in range(8):
                par = ti % 2
                P.do("sync", lambda e, ti=ti, par=par: e.dma_start(
                    out=xt4[par][:], in_=x_own[128 * ti:128 * (ti + 1), :]),
                    wait=([("pe", xt4_free[par])] +
                          ([("pe", pe_barrier), ("ve", ve_barrier),
                            ("se", se_barrier), ("sl", sl_barrier)] if ti == 0 else [])),
                    inc=[("hl", 16)], dma=True)
                hl_x = P.val("hl")
                for grp in range(4):
                    gpar = grp % 2
                    for m in range(4):
                        o = grp * 4 + m
                        P.do("tensor", lambda t, par=par, gpar=gpar, m=m, o=o:
                             t.transpose(xps[gpar][:, 128 * m:128 * (m + 1)],
                                         xt4[par][:, 128 * o:128 * (o + 1)], ident32[:]),
                             wait=([("hl", hl_x)] + xps_free[gpar]) if m == 0 else (),
                             inc=[("pe", 1)] if m == 3 else ())
                    pe_x = P.val("pe")
                    if grp == 3:
                        xt4_free[par] = pe_x
                    for m in range(4):
                        o = grp * 4 + m
                        P.do("vector", lambda v, ti=ti, gpar=gpar, m=m, o=o: v.tensor_copy(
                            x1[o][:, 128 * ti:128 * (ti + 1)],
                            xps[gpar][:, 128 * m:128 * (m + 1)]),
                            wait=[("pe", pe_x)] if m == 0 else (),
                            inc=[("ve", 1)] if m == 3 else ())
                    xps_free[gpar] = [("ve", P.val("ve"))]
            ve_xinit = P.val("ve")

            for o in range(16):
                par = o % 2
                P.do("gpsimd", lambda e, o=o, par=par: e.dma_start(
                    out=wpo[par].ap().rearrange("p (f c) -> p f c", f=16),
                    in_=wp_view[:, :, o, :]),
                    wait=[("gw", P.val("gw")), ("ve", ve_barrier), ("se", se_barrier)]
                    + ([("pe", P.val("pe"))] if o >= 2 else []),
                    inc=[("gw", 16)], dma=True)
                gw_wp = P.val("gw")
                for f in range(16):
                    for half in range(2):
                        P.do("tensor", lambda t, o=o, par=par, f=f, half=half: t.matmul(
                            pr_ps[par][:, 512 * half:512 * (half + 1)],
                            wpo[par][:, 128 * f:128 * (f + 1)],
                            out_all[f][:, 512 * half:512 * (half + 1)],
                            start=(f == 0), stop=(f == 15)),
                            wait=(([("gw", gw_wp), ("ve", epi_ve)] + prps_free[par])
                                  if (f == 0 and half == 0) else ()),
                            inc=([("pe", 1)] if (f == 15 and half == 1) else ()))
                pe_pr = P.val("pe")
                P.do("vector", lambda v, o=o, par=par: v.tensor_add(
                    x1[o][:], x1[o][:], pr_ps[par][:]),
                    wait=[("pe", pe_pr), ("ve", ve_xinit)], inc=[("ve", 1)])
                prps_free[par] = [("ve", P.val("ve"))]
            ve_x1 = P.val("ve")

        if debug >= 3:
            for o in range(16):
                P.do("scalar", lambda s, o=o: s.dma_start(
                    out=dbg["x1a"][128 * o:128 * (o + 1), :], in_=x1[o][:]),
                    wait=[("ve", ve_x1)] if o == 0 else (), inc=[("sl", 16)], dma=True)

        h_fm = [sb(f"h_fm{o}", [128, TOK], BF16) for o in range(NF)]

        with ExitStack() as p5:
            sb5 = lambda n, s, d=F32: p5.enter_context(nc.sbuf_tensor(n, s, d))
            ps5 = lambda n, s, d=F32: p5.enter_context(nc.psum_tensor(n, s, d))
            xb = [sb5(f"p5_xb{p}", [128, TOK], BF16) for p in range(2)]
            xbsq = [sb5(f"p5_xq{p}", [128, TOK], BF16) for p in range(2)]
            nrm = sb5("p5_nrm", [128, TOK])
            mean_r = sb5("p5_mean", [1, TOK])
            var_r = sb5("p5_var", [1, TOK])
            tmp_r = sb5("p5_tmp", [1, TOK])
            meanb_r = sb5("p5_meanb", [1, TOK], BF16)
            rstdb_r = sb5("p5_rstdb", [1, TOK], BF16)
            sm_ps = ps5("p5_sm", [1, TOK])
            sq_ps = ps5("p5_sq", [1, TOK])
            bm_ps = ps5("p5_bm", [128, TOK])
            br_ps = ps5("p5_br", [128, TOK])

            p5_entry = [("ve", P.val("ve")), ("se", P.val("se")),
                        ("pe", P.val("pe"))]
            xb_free = [0, 0]
            for o in range(16):
                par = o % 2
                P.do("scalar", lambda s, o=o, par=par: s.copy(xb[par][:], x1[o][:]),
                     wait=[("ve", ve_x1), ("pe", xb_free[par])])
                P.do("vector", lambda v, par=par: v.tensor_mul(xbsq[par][:], xb[par][:],
                                                               xb[par][:]),
                     wait=[("se", P.val("se"))], inc=[("ve", 1)])
                ve_xb = P.val("ve")
                for half in range(2):
                    P.do("tensor", lambda t, par=par, half=half, o=o: t.matmul(
                        sm_ps[:, 512 * half:512 * (half + 1)], ones_col[:],
                        xb[par][:, 512 * half:512 * (half + 1)],
                        start=(o == 0), stop=(o == 15)),
                        wait=(([("ve", ve_xb)] + (p5_entry if o == 0 else []))
                              if half == 0 else ()))
                    P.do("tensor", lambda t, par=par, half=half, o=o: t.matmul(
                        sq_ps[:, 512 * half:512 * (half + 1)], ones_col[:],
                        xbsq[par][:, 512 * half:512 * (half + 1)],
                        start=(o == 0), stop=(o == 15)),
                        inc=([("pe", 1)] if half == 1 else ()))
                xb_free[par] = P.val("pe")
            pe_st5 = P.val("pe")
            P.do("vector", lambda v: v.tensor_scalar(
                out=mean_r[:], in0=sm_ps[:], scalar1=1.0 / D, scalar2=None,
                op0=ALU.mult), wait=[("pe", pe_st5)])
            P.do("vector", lambda v: v.tensor_scalar(
                out=var_r[:], in0=sq_ps[:], scalar1=1.0 / D, scalar2=None,
                op0=ALU.mult))
            P.do("vector", lambda v: v.tensor_mul(tmp_r[:], mean_r[:], mean_r[:]))
            P.do("vector", lambda v: v.tensor_sub(var_r[:], var_r[:], tmp_r[:]))
            P.do("vector", lambda v: v.tensor_scalar(
                out=var_r[:], in0=var_r[:], scalar1=1.0, scalar2=1e-5,
                op0=ALU.mult, op1=ALU.add))
            P.do("vector", lambda v: v.tensor_copy(meanb_r[:], mean_r[:]),
                 inc=[("ve", 1)])
            ve_r5 = P.val("ve")
            P.do("scalar", lambda s: s.activation(var_r[:], var_r[:],
                                                  AF.Ln), wait=[("ve", ve_r5)])
            P.do("scalar", lambda s: s.activation(var_r[:], var_r[:],
                                                  AF.Exp, scale=-0.5),
                 inc=[("se", 1)])
            se_r5 = P.val("se")
            P.do("vector", lambda v: v.tensor_copy(rstdb_r[:], var_r[:]),
                 wait=[("se", se_r5)], inc=[("ve", 1)])
            ve_r5b = P.val("ve")
            for half in range(2):
                P.do("tensor", lambda t, half=half: t.matmul(
                    bm_ps[:, 512 * half:512 * (half + 1)], ones_rows[0:1, :],
                    meanb_r[0:1, 512 * half:512 * (half + 1)], start=True, stop=True),
                    wait=[("ve", ve_r5b)] if half == 0 else ())
                P.do("tensor", lambda t, half=half: t.matmul(
                    br_ps[:, 512 * half:512 * (half + 1)], ones_rows[0:1, :],
                    rstdb_r[0:1, 512 * half:512 * (half + 1)], start=True, stop=True),
                    inc=[("pe", 1)] if half == 1 else ())
            pe_bc5 = P.val("pe")
            for o in range(16):
                P.do("vector", lambda v, o=o: v.tensor_sub(nrm[:], x1[o][:], bm_ps[:]),
                     wait=[("pe", pe_bc5)] if o == 0 else ())
                P.do("vector", lambda v, o=o: v.tensor_mul(h_fm[o][:], nrm[:], br_ps[:]),
                     inc=[("ve", 1)])
            ve_h = P.val("ve")

        if debug >= 3:
            for o in range(16):
                P.do("scalar", lambda s, o=o: s.dma_start(
                    out=dbg["h"][128 * o:128 * (o + 1), :], in_=h_fm[o][:]),
                    wait=[("ve", ve_h)] if o == 0 else (), inc=[("sl", 16)], dma=True)

        with ExitStack() as p6:
            sb6 = lambda n, s, d=F32: p6.enter_context(nc.sbuf_tensor(n, s, d))
            ps6 = lambda n, s, d=F32: p6.enter_context(nc.psum_tensor(n, s, d))
            wqr = sb6("p6_wqr", [128, 16 * 128], BF16)
            keysb = sb6("p6_keys", [64, 64], BF16)
            q_rb = sb6("p6_qrb", [64, BT], BF16)
            scr = [sb6(f"p6_scr{p}", [128, 64]) for p in range(2)]
            v8t = sb6("p6_v8", [128, 8])
            u8t = sb6("p6_u8", [128, 8], U32)
            contrib_sb = sb6("p6_contrib", [128, 256])
            qr_ps = ps6("p6_qps", [128, TOK])
            sc_ps = [ps6(f"p6_sps{p}", [128, 64]) for p in range(2)]

            p6_entry = [("ve", P.val("ve")), ("se", P.val("se")),
                        ("pe", P.val("pe"))]
            P.do("gpsimd", lambda e: e.dma_start(
                out=wqr.ap().rearrange("p (f c) -> p f c", f=16),
                in_=wq_r.rearrange("(f p) c -> p f c", p=128)),
                wait=[("gw", P.val("gw")), ("pe", P.val("pe"))] + p6_entry,
                inc=[("gw", 16)], dma=True)
            P.do("gpsimd", lambda e: e.dma_start(out=keysb[:], in_=keys_own),
                 wait=[("gw", P.val("gw"))] + p6_entry, inc=[("gw", 16)], dma=True)
            gw_p6 = P.val("gw")
            for f in range(16):
                for half in range(2):
                    P.do("tensor", lambda t, f=f, half=half: t.matmul(
                        qr_ps[:, 512 * half:512 * (half + 1)],
                        wqr[:, 128 * f:128 * (f + 1)],
                        h_fm[f][:, 512 * half:512 * (half + 1)],
                        start=(f == 0), stop=(f == 15)),
                        wait=(([("gw", gw_p6), ("ve", ve_h)] + p6_entry)
                              if (f == 0 and half == 0) else ()),
                        inc=([("pe", 1)] if (f == 15 and half == 1) else ()))
            pe_qr = P.val("pe")
            P.do("scalar", lambda s: s.copy(q_rb[:, 0:TOK], qr_ps[0:64, :]),
                 wait=[("pe", pe_qr)])
            P.do("scalar", lambda s: s.copy(q_rb[:, TOK:BT], qr_ps[64:128, :]),
                 inc=[("se", 1)])
            se_qrb = P.val("se")
            P.do("vector", lambda v: v.memset(contrib_sb[:], 0.0))
            scps_free = [[], []]
            scr_free = [0, 0]
            for i in range(16):
                par = i % 2
                P.do("tensor", lambda t, i=i, par=par: t.matmul(
                    sc_ps[par][:], q_rb[:, 128 * i:128 * (i + 1)], keysb[:],
                    start=True, stop=True),
                    wait=[("se", se_qrb)] + scps_free[par], inc=[("pe", 1)])
                pe_sc = P.val("pe")
                P.do("scalar", lambda s, par=par: s.copy(scr[par][:], sc_ps[par][:]),
                     wait=[("pe", pe_sc), ("ve", scr_free[par])], inc=[("se", 1)])
                se_scr = P.val("se")
                scps_free[par] = [("se", se_scr)]
                P.do("vector", lambda v, i=i, par=par: v.max(
                    out=contrib_sb[:, 16 * i:16 * i + 8], in_=scr[par][:]),
                    wait=[("se", se_scr)], inc=[("ve", 1)])
                ve_m8 = P.val("ve")
                P.do("vector", lambda v, i=i, par=par: v.max_index(
                    out=u8t[:], in_max=contrib_sb[:, 16 * i:16 * i + 8],
                    in_values=scr[par][:]),
                    wait=[("ve", ve_m8)], inc=[("ve", 1)])
                P.do("vector", lambda v, i=i: v.tensor_scalar(
                    out=contrib_sb[:, 16 * i + 8:16 * i + 16], in0=u8t[:],
                    scalar1=0.0, scalar2=None, op0=ALU.add),
                    inc=[("ve", 1)])
                scr_free[par] = P.val("ve")
            ve_contrib = P.val("ve")
            P.do("gpsimd", lambda e: e.dma_start(
                out=contrib.ap().rearrange("(i p) f -> p i f", p=128),
                in_=contrib_sb.ap().rearrange("p (i f) -> p i f", i=16)),
                wait=[("gg", P.val("gg")), ("ve", ve_contrib)],
                inc=[("gg", 16)], dma=True)
            gg_contrib = P.val("gg")
            P.do("gpsimd", lambda e: e.collective_compute(
                "AllGather", ALU.bypass, replica_groups=[list(range(8))],
                ins=[contrib.ap()], outs=[ag_out.ap()]),
                wait=[("gg", gg_contrib)], inc=[("cc", 1)], dma=True)
            cc_done = P.val("cc")

        NSL = 16
        with ExitStack() as p7:
            sb7 = lambda n, s, d=F32: p7.enter_context(nc.sbuf_tensor(n, s, d))
            ps7 = lambda n, s, d=F32: p7.enter_context(nc.psum_tensor(n, s, d))
            gwt = [sb7(f"p7_gw{p}", [128, 16 * 512], BF16) for p in range(2)]
            uwt = [sb7("p7_uw0", [128, 16 * 512], BF16)]
            dwt = [sb7("p7_dw0", [128, 4 * D], BF16)]
            m_g = sb7("p7_mg", [128, TOK], BF16)
            m_s = [sb7(f"p7_ms{i}", [128, TOK], BF16) for i in range(4)]
            g_ps = ps7("p7_g", [128, TOK])
            u_ps = ps7("p7_u", [128, TOK])
            y_ps = [ps7(f"p7_y{p}", [128, TOK]) for p in range(2)]

            gv = gate_w.rearrange("(f p) (sl c) -> p f sl c", p=128, sl=NSL)
            uv = up_w.rearrange("(f p) (sl c) -> p f sl c", p=128, sl=NSL)
            dv = down_w.rearrange("(kc p) c -> p kc c", p=128)

            p7_entry = [("ve", P.val("ve")), ("se", P.val("se")),
                        ("pe", P.val("pe")), ("sl", P.val("sl"))]
            gps_free = p7_entry
            ups_free = p7_entry
            yps_free = [p7_entry, p7_entry]
            wt_free = [0, 0]
            ms_free = [0] * 4
            mg_free = 0
            dw_free = 0
            uw_free = 0

            for sl in range(NSL):
                par = sl % 2
                P.do("gpsimd", lambda e, sl=sl, par=par: e.dma_start(
                    out=gwt[par].ap().rearrange("p (f c) -> p f c", f=16),
                    in_=gv[:, :, sl, :]),
                    wait=[("gw", P.val("gw")), ("pe", wt_free[par])] +
                         (p7_entry if sl < 2 else []),
                    inc=[("gw", 16)], dma=True)
                P.do("gpsimd", lambda e, sl=sl: e.dma_start(
                    out=uwt[0].ap().rearrange("p (f c) -> p f c", f=16),
                    in_=uv[:, :, sl, :]),
                    wait=[("gw", P.val("gw")), ("pe", uw_free)],
                    inc=[("gw", 16)], dma=True)
                P.do("gpsimd", lambda e, sl=sl: e.dma_start(
                    out=dwt[0].ap().rearrange("p (kc c) -> p kc c", kc=4),
                    in_=dv[:, 4 * sl:4 * (sl + 1), :]),
                    wait=[("gw", P.val("gw")), ("pe", dw_free)],
                    inc=[("gw", 16)], dma=True)
                gw_sl = P.val("gw")
                for ic in range(4):
                    for f in range(16):
                        for half in range(2):
                            P.do("tensor", lambda t, par=par, ic=ic, f=f, half=half:
                                 t.matmul(g_ps[:, 512 * half:512 * (half + 1)],
                                          gwt[par][:, f * 512 + 128 * ic:
                                                   f * 512 + 128 * (ic + 1)],
                                          h_fm[f][:, 512 * half:512 * (half + 1)],
                                          start=(f == 0), stop=(f == 15)),
                                 wait=(([("gw", gw_sl), ("ve", ve_h)] + gps_free)
                                       if (f == 0 and half == 0) else ()),
                                 inc=([("pe", 1)] if (f == 15 and half == 1) else ()))
                    pe_g = P.val("pe")
                    P.do("scalar", lambda s: s.activation(m_g[:], g_ps[:], AF.Silu),
                         wait=[("pe", pe_g), ("ve", mg_free)], inc=[("se", 1)])
                    gps_free = [("se", P.val("se"))]
                    for f in range(16):
                        for half in range(2):
                            P.do("tensor", lambda t, par=par, ic=ic, f=f, half=half:
                                 t.matmul(u_ps[:, 512 * half:512 * (half + 1)],
                                          uwt[0][:, f * 512 + 128 * ic:
                                                 f * 512 + 128 * (ic + 1)],
                                          h_fm[f][:, 512 * half:512 * (half + 1)],
                                          start=(f == 0), stop=(f == 15)),
                                 wait=(([("gw", gw_sl)] + ups_free)
                                       if (f == 0 and half == 0) else ()),
                                 inc=([("pe", 1)] if (f == 15 and half == 1) else ()))
                    pe_u = P.val("pe")
                    if ic == 3:
                        pe_up_last = P.val("pe")
                    P.do("vector", lambda v, ic=ic: v.tensor_mul(
                        m_s[ic][:], m_g[:], u_ps[:]),
                        wait=[("pe", pe_u), ("se", P.val("se")),
                              ("pe", ms_free[ic])],
                        inc=[("ve", 1)])
                    ups_free = [("ve", P.val("ve"))]
                    mg_free = P.val("ve")
                for o in range(16):
                    ypar = o % 2
                    for kc in range(4):
                        for half in range(2):
                            P.do("tensor", lambda t, par=par, o=o, kc=kc, half=half,
                                 ypar=ypar: t.matmul(
                                     y_ps[ypar][:, 512 * half:512 * (half + 1)],
                                     dwt[0][:, kc * D + 128 * o:
                                             kc * D + 128 * (o + 1)],
                                     m_s[kc][:, 512 * half:512 * (half + 1)],
                                     start=(kc == 0), stop=(kc == 3)),
                                 wait=(([("ve", P.val("ve"))] + yps_free[ypar])
                                       if (kc == 0 and half == 0) else ()),
                                 inc=([("pe", 1)] if (kc == 3 and half == 1) else ()))
                    pe_y = P.val("pe")
                    P.do("vector", lambda v, o=o, ypar=ypar: v.tensor_add(
                        x1[o][:], x1[o][:], y_ps[ypar][:]),
                        wait=[("pe", pe_y)], inc=[("ve", 1)])
                    yps_free[ypar] = [("ve", P.val("ve"))]
                pe_down = P.val("pe")
                for ic in range(4):
                    ms_free[ic] = pe_down
                wt_free[par] = pe_down
                dw_free = pe_down
                uw_free = pe_up_last
            ve_x2m = P.val("ve")

        h_tm = [sb(f"h_tm{i}", [128, D], BF16) for i in range(8)]
        with ExitStack() as p8t:
            tp2_ps = [p8t.enter_context(
                nc.psum_tensor(f"p8_tp{p}", [128, 512], BF16)) for p in range(2)]
            p8_entry = [("ve", P.val("ve")), ("se", P.val("se"))]
            tp2_free = [0, 0]
            for nt in range(8):
                for grp in range(4):
                    gpar = grp % 2
                    for m in range(4):
                        o = grp * 4 + m
                        P.do("tensor", lambda t, nt=nt, gpar=gpar, m=m, o=o:
                             t.transpose(tp2_ps[gpar][:, 128 * m:128 * (m + 1)],
                                         h_fm[o][:, 128 * nt:128 * (nt + 1)],
                                         identbf[:]),
                             wait=(([("se", tp2_free[gpar])] +
                                    (p8_entry if nt == 0 and grp == 0 else []))
                                   if m == 0 else ()),
                             inc=[("pe", 1)] if m == 3 else ())
                    pe_t8 = P.val("pe")
                    for m in range(4):
                        o = grp * 4 + m
                        P.do("scalar", lambda s, nt=nt, gpar=gpar, m=m, o=o: s.copy(
                            h_tm[nt][:, 128 * o:128 * (o + 1)],
                            tp2_ps[gpar][:, 128 * m:128 * (m + 1)]),
                            wait=[("pe", pe_t8)] if m == 0 else (),
                            inc=[("se", 1)] if m == 3 else ())
                    tp2_free[gpar] = P.val("se")
            se_htm = P.val("se")

        with ExitStack() as p8:
            sb8 = lambda n, s, d=F32: p8.enter_context(nc.sbuf_tensor(n, s, d))
            ps8 = lambda n, s, d=F32: p8.enter_context(nc.psum_tensor(n, s, d))
            agidx_sb = sb8("p8_agidx", [128, 16], I32)
            sxy = sb8("p8_sxy", [128, 32])
            comb = sb8("p8_comb", [128, 64])
            emat = sb8("p8_emat", [128, 64])
            ix64 = sb8("p8_ix64", [128, 8])
            v8p = sb8("p8_v8", [128, 8])
            p8u = sb8("p8_p8u", [128, 8], U32)
            p8f = sb8("p8_p8f", [128, 8])
            dummy64 = sb8("p8_dummy", [128, 64])
            e8 = sb8("p8_e8", [128, 8])
            mxr = sb8("p8_mx", [128, 1])
            ssum = sb8("p8_ss", [128, 1])
            pe8 = sb8("p8_pe8", [128, 8])
            prall = sb8("p8_prall", [128, 64])
            eidxall = sb8("p8_eidx", [128, 64], I32)
            ewr = sb8("p8_ewr", [128, 8])
            ewf = sb8("p8_ewf", [128, 8])
            de_t = [sb8(f"p8_de{p}", [128, D], BF16) for p in range(2)]
            ue_t = [sb8(f"p8_ue{p}", [128, D], BF16) for p in range(2)]
            prod = sb8("p8_prod", [128, D])
            acc = [sb8(f"p8_acc{p}", [128, D]) for p in range(2)]
            tp3_ps = [ps8(f"p8_tp3{p}", [128, 512]) for p in range(2)]

            p8_bar = [("ve", P.val("ve")), ("se", P.val("se")), ("pe", P.val("pe"))]
            P.do("sync", lambda e: e.dma_start(out=agidx_sb[:], in_=ag_idx),
                 wait=p8_bar, inc=[("hl", 16)], dma=True)
            hl_agidx = P.val("hl")

            sxy_free = 0
            for nt in range(8):
                for which in range(2):
                    P.do("gpsimd", lambda e, nt=nt, which=which: e.indirect_dma_start(
                        out=sxy[:, 16 * which:16 * (which + 1)], out_offset=None,
                        in_=ag_out.ap(),
                        in_offset=bass.IndirectOffsetOnAxis(
                            agidx_sb[:, 8 * which + nt:8 * which + nt + 1], 0)),
                        wait=[("gg", P.val("gg")), ("cc", cc_done),
                              ("hl", hl_agidx), ("ve", sxy_free),
                              ("sl", P.val("sl"))] +
                             (p8_bar if (nt == 0 and which == 0) else []),
                        inc=[("gg", 16)], dma=True)
                gg_sxy = P.val("gg")
                P.do("vector", lambda v: v.tensor_scalar(
                    out=ix64[:], in0=sxy[:, 8:16], scalar1=64.0, scalar2=None,
                    op0=ALU.mult), wait=[("gg", gg_sxy)])
                for i in range(8):
                    P.do("vector", lambda v, i=i: v.tensor_scalar(
                        out=comb[:, 8 * i:8 * (i + 1)], in0=sxy[:, 16:24],
                        scalar1=sxy[:, i:i + 1], scalar2=None, op0=ALU.add))
                    P.do("vector", lambda v, i=i: v.tensor_scalar(
                        out=emat[:, 8 * i:8 * (i + 1)], in0=sxy[:, 24:32],
                        scalar1=ix64[:, i:i + 1], scalar2=None, op0=ALU.add))
                P.do("vector", lambda v: v.max(out=v8p[:], in_=comb[:]),
                     inc=[("ve", 1)])
                ve_v8 = P.val("ve")
                sxy_free = ve_v8
                P.do("vector", lambda v: v.max_index(out=p8u[:], in_max=v8p[:],
                                                     in_values=comb[:]),
                     wait=[("ve", ve_v8)], inc=[("ve", 1)])
                ve_p8 = P.val("ve")
                P.do("vector", lambda v: v.tensor_scalar(
                    out=p8f[:], in0=p8u[:], scalar1=0.0, scalar2=None, op0=ALU.add),
                    wait=[("ve", ve_p8)])
                for s8 in range(8):
                    P.do("vector", lambda v, s8=s8: v.scalar_tensor_tensor(
                        out=dummy64[:], in0=iota64[:], scalar=p8f[:, s8:s8 + 1],
                        in1=emat[:], op0=ALU.is_equal, op1=ALU.mult,
                        accum_out=e8[:, s8:s8 + 1]))
                P.do("vector", lambda v, nt=nt: v.tensor_scalar(
                    out=eidxall[:, 8 * nt:8 * (nt + 1)], in0=e8[:], scalar1=0.0,
                    scalar2=None, op0=ALU.add))
                P.do("vector", lambda v: v.reduce_max(mxr[:], v8p[:], axis=AX.X))
                P.do("vector", lambda v: v.tensor_scalar(
                    out=mxr[:], in0=mxr[:], scalar1=-1.0, scalar2=None, op0=ALU.mult),
                    inc=[("ve", 1)])
                ve_mx = P.val("ve")
                P.do("scalar", lambda s: s.activation(pe8[:], v8p[:], AF.Exp,
                                                      bias=mxr[:, 0:1],
                                                      accum_out=ssum[:, 0:1]),
                     wait=[("ve", ve_mx)], inc=[("se", 1)])
                se_pe8 = P.val("se")
                P.do("vector", lambda v: v.reciprocal(mxr[:], ssum[:]),
                     wait=[("se", se_pe8)])
                P.do("vector", lambda v, nt=nt: v.tensor_scalar(
                    out=prall[:, 8 * nt:8 * (nt + 1)], in0=pe8[:],
                    scalar1=mxr[:, 0:1], scalar2=None, op0=ALU.mult),
                    inc=[("ve", 1)])
                if debug >= 4 and nt == 0:
                    P.do("scalar", lambda s: s.dma_start(out=dbg["sxy0"], in_=sxy[:]),
                         wait=[("ve", P.val("ve"))], inc=[("sl", 16)], dma=True)
                    sxy_free = max(sxy_free, 0)
            ve_peerA = P.val("ve")
            if debug >= 4:
                P.do("scalar", lambda s: s.dma_start(out=dbg["eidx"], in_=eidxall[:]),
                     wait=[("ve", ve_peerA)], inc=[("sl", 16)], dma=True)
                P.do("scalar", lambda s: s.dma_start(out=dbg["prall"], in_=prall[:]),
                     inc=[("sl", 16)], dma=True)

            de_free = [0, 0]
            for nt in range(8):
                for s8 in range(8):
                    par = s8 % 2
                    P.do("gpsimd", lambda e, nt=nt, s8=s8, par=par:
                         e.indirect_dma_start(
                             out=de_t[par][:], out_offset=None, in_=down_embed,
                             in_offset=bass.IndirectOffsetOnAxis(
                                 eidxall[:, 8 * nt + s8:8 * nt + s8 + 1], 0)),
                         wait=[("gg", P.val("gg")), ("ve", de_free[par]),
                               ("sl", P.val("sl"))],
                         inc=[("gg", 16)], dma=True)
                    gg_de = P.val("gg")
                    P.do("vector", lambda v, nt=nt, par=par: v.tensor_mul(
                        prod[:], de_t[par][:], h_tm[nt][:]),
                        wait=[("gg", gg_de), ("se", se_htm)])
                    de_free[par] = P.val("ve")
                    P.do("vector", lambda v, s8=s8: v.reduce_sum(
                        ewr[:, s8:s8 + 1], prod[:], axis=AX.X))
                P.do("vector", lambda v: v.engine_nop(), inc=[("ve", 1)])
                ve_ewr = P.val("ve")
                P.do("scalar", lambda s: s.activation(pe8[:], ewr[:], AF.Silu),
                     wait=[("ve", ve_ewr)], inc=[("se", 1)])
                se_silu = P.val("se")
                P.do("vector", lambda v, nt=nt: v.tensor_mul(
                    ewf[:], pe8[:], prall[:, 8 * nt:8 * (nt + 1)]),
                    wait=[("se", se_silu)], inc=[("ve", 1)])
                if debug >= 4 and nt == 0:
                    P.do("scalar", lambda s: s.dma_start(out=dbg["ewr0"], in_=ewr[:]),
                         wait=[("ve", P.val("ve"))], inc=[("sl", 16)], dma=True)
                    P.do("scalar", lambda s: s.dma_start(out=dbg["ewf0"], in_=ewf[:]),
                         inc=[("sl", 16)], dma=True)
                for s8 in range(8):
                    par = s8 % 2
                    P.do("gpsimd", lambda e, nt=nt, s8=s8, par=par:
                         e.indirect_dma_start(
                             out=ue_t[par][:], out_offset=None, in_=up_embed,
                             in_offset=bass.IndirectOffsetOnAxis(
                                 eidxall[:, 8 * nt + s8:8 * nt + s8 + 1], 0)),
                         wait=[("gg", P.val("gg")), ("ve", de_free[par])],
                         inc=[("gg", 16)], dma=True)
                    gg_ue = P.val("gg")
                    if s8 == 0:
                        P.do("vector", lambda v, s8=s8, par=par: v.tensor_scalar(
                            out=acc[0][:], in0=ue_t[par][:],
                            scalar1=ewf[:, s8:s8 + 1], scalar2=None, op0=ALU.mult),
                            wait=[("gg", gg_ue)])
                    else:
                        P.do("vector", lambda v, s8=s8, par=par: v.scalar_tensor_tensor(
                            out=acc[s8 % 2][:], in0=ue_t[par][:],
                            scalar=ewf[:, s8:s8 + 1], in1=acc[(s8 - 1) % 2][:],
                            op0=ALU.mult, op1=ALU.add),
                            wait=[("gg", gg_ue)])
                    de_free[par] = P.val("ve")
                P.do("vector", lambda v: v.engine_nop(), inc=[("ve", 1)])
                ve_acc = P.val("ve")
                accf = acc[7 % 2]
                if debug >= 4 and nt == 0:
                    P.do("scalar", lambda s, accf=accf: s.dma_start(
                        out=dbg["acc0"], in_=accf[:]),
                        wait=[("ve", ve_acc)], inc=[("sl", 16)], dma=True)
                for grp in range(4):
                    gpar = grp % 2
                    for m in range(4):
                        o = grp * 4 + m
                        P.do("tensor", lambda t, gpar=gpar, m=m, o=o, accf=accf:
                             t.transpose(tp3_ps[gpar][:, 128 * m:128 * (m + 1)],
                                         accf[:, 128 * o:128 * (o + 1)], ident32[:]),
                             wait=(([("ve", ve_acc)] if grp == 0 else
                                    [("ve", P.val("ve"))]) if m == 0 else []),
                             inc=[("pe", 1)] if m == 3 else ())
                    pe_t3 = P.val("pe")
                    for m in range(4):
                        o = grp * 4 + m
                        P.do("vector", lambda v, nt=nt, gpar=gpar, m=m, o=o:
                             v.tensor_add(x1[o][:, 128 * nt:128 * (nt + 1)],
                                          x1[o][:, 128 * nt:128 * (nt + 1)],
                                          tp3_ps[gpar][:, 128 * m:128 * (m + 1)]),
                             wait=[("pe", pe_t3)] if m == 0 else (),
                             inc=[("ve", 1)])
            ve_x2 = P.val("ve")

        with ExitStack() as p9:
            sb9 = lambda n, s, d=F32: p9.enter_context(nc.sbuf_tensor(n, s, d))
            ps9 = lambda n, s, d=F32: p9.enter_context(nc.psum_tensor(n, s, d))
            ot = [sb9(f"p9_ot{p}", [128, D]) for p in range(2)]
            xop = [ps9(f"p9_xp{p}", [128, 512]) for p in range(2)]
            p9_entry = [("ve", P.val("ve")), ("se", P.val("se"))]
            xop_free = [p9_entry, p9_entry]
            ot_free = [0, 0]
            for nt in range(8):
                par = nt % 2
                for grp in range(4):
                    gpar = grp % 2
                    for m in range(4):
                        o = grp * 4 + m
                        P.do("tensor", lambda t, nt=nt, gpar=gpar, m=m, o=o:
                             t.transpose(xop[gpar][:, 128 * m:128 * (m + 1)],
                                         x1[o][:, 128 * nt:128 * (nt + 1)],
                                         ident32[:]),
                             wait=(([("ve", ve_x2)] + xop_free[gpar]) if m == 0
                                   else ()),
                             inc=[("pe", 1)] if m == 3 else ())
                    pe_x9 = P.val("pe")
                    xop_free[gpar] = []
                    for m in range(4):
                        o = grp * 4 + m
                        P.do("vector", lambda v, par=par, gpar=gpar, m=m, o=o:
                             v.tensor_copy(ot[par][:, 128 * o:128 * (o + 1)],
                                           xop[gpar][:, 128 * m:128 * (m + 1)]),
                             wait=(([("pe", pe_x9)] +
                                    ([("sl", ot_free[par])] if grp == 0 else []))
                                   if m == 0 else []),
                             inc=[("ve", 1)])
                    xop_free[gpar] = [("ve", P.val("ve"))]
                ve_ot = P.val("ve")
                P.do("scalar", lambda s, nt=nt, par=par: s.dma_start(
                    out=out[128 * nt:128 * (nt + 1), :], in_=ot[par][:]),
                    wait=[("ve", ve_ot)], inc=[("sl", 16)], dma=True)
                ot_free[par] = P.val("sl")
            sl_out = P.val("sl")

        if debug:
            for o in range(16):
                P.do("scalar", lambda s, o=o: s.dma_start(
                    out=dbg["x1"][128 * o:128 * (o + 1), :], in_=x1[o][:]),
                    wait=[("ve", ve_x2)] if o == 0 else (), inc=[("sl", 16)], dma=True)

        P.do("sync", lambda e: e.nop(), wait=[("sl", P.val("sl"))])

        block = top.enter_context(nc.Block())
        P.replay(block)

    return nc


_NC_CACHE = {}


def _get_nc():
    if "nc" not in _NC_CACHE:
        _NC_CACHE["nc"] = build(debug=0)
    return _NC_CACHE["nc"]


def _ag_idx_for_core(c):
    idx = np.zeros((128, 16), np.int32)
    c1 = c // 2
    for nt in range(8):
        p = np.arange(128)
        n = 1024 * c + 128 * nt + p
        tok_local = n // 2 - 1024 * c1
        half = n % 2
        idx[:, nt] = c1 * 2048 + half * 1024 + tok_local
        idx[:, 8 + nt] = (4 + c1) * 2048 + half * 1024 + tok_local
    return idx


def kernel(L0_attn_x, L0_attn_w_qkv, L0_attn_w_proj, L0_attn_lambda_q1,
           L0_attn_lambda_k1, L0_attn_lambda_q2, L0_attn_lambda_k2,
           L0_attn_rms_scale, L0_ffn_W_q, L0_ffn_keys, L0_ffn_down_embed,
           L0_ffn_up_embed, L0_ffn_gate_w, L0_ffn_up_w, L0_ffn_down_w):
    x = np.ascontiguousarray(np.asarray(L0_attn_x, dtype=np.float32))
    keys = np.ascontiguousarray(np.asarray(L0_ffn_keys, dtype=np.float32))
    common = {
        "w_qkv": np.ascontiguousarray(L0_attn_w_qkv, dtype=np.float32),
        "w_proj": np.ascontiguousarray(L0_attn_w_proj, dtype=np.float32),
        "lam_q1": np.ascontiguousarray(L0_attn_lambda_q1, dtype=np.float32),
        "lam_k1": np.ascontiguousarray(L0_attn_lambda_k1, dtype=np.float32),
        "lam_q2": np.ascontiguousarray(L0_attn_lambda_q2, dtype=np.float32),
        "lam_k2": np.ascontiguousarray(L0_attn_lambda_k2, dtype=np.float32),
        "rms_scale": np.ascontiguousarray(L0_attn_rms_scale, dtype=np.float32),
        "wq_r": np.ascontiguousarray(L0_ffn_W_q, dtype=np.float32),
        "down_embed": np.ascontiguousarray(L0_ffn_down_embed, dtype=np.float32),
        "up_embed": np.ascontiguousarray(L0_ffn_up_embed, dtype=np.float32),
        "gate_w": np.ascontiguousarray(L0_ffn_gate_w, dtype=np.float32),
        "up_w": np.ascontiguousarray(L0_ffn_up_w, dtype=np.float32),
        "down_w": np.ascontiguousarray(L0_ffn_down_w, dtype=np.float32),
    }
    in_maps = []
    for c in range(8):
        b, half = c // 2, c % 2
        im = dict(common)
        im["x_own"] = np.ascontiguousarray(x[b, 1024 * half:1024 * (half + 1)])
        im["x_partner"] = np.ascontiguousarray(
            x[b, 1024 * (1 - half):1024 * (2 - half)])
        im["keys_own"] = np.ascontiguousarray(keys[c // 4])
        im["ag_idx"] = _ag_idx_for_core(c)
        in_maps.append(im)
    nc = _get_nc()
    res = run_bass_kernel_spmd(nc, in_maps, core_ids=list(range(8)))
    outp = np.empty((4, 2048, 2048), np.float32)
    for c in range(8):
        b, half = c // 2, c % 2
        outp[b, 1024 * half:1024 * (half + 1)] = np.asarray(res.results[c]["out"])
    return outp


# version 04
# speedup vs baseline: 1.3113x; 1.0483x over previous
import math
import numpy as np
from contextlib import ExitStack

import concourse.bass as bass
import concourse.mybir as mybir
from concourse.bass_utils import run_bass_kernel_spmd

F32 = mybir.dt.float32
BF16 = mybir.dt.bfloat16
I32 = mybir.dt.int32
U32 = mybir.dt.uint32
AF = mybir.ActivationFunctionType
ALU = mybir.AluOpType
AX = mybir.AxisListType

LI = 0.8 - 0.6 * math.exp(-0.3)
D = 2048
TOK = 1024
BT = 2048
NF = 16

ENGS = ("sync", "vector", "scalar", "tensor", "gpsimd")


class Prog:

    def __init__(self, nc, ctx):
        self.nc = nc
        self.ctx = ctx
        self.q = {e: [] for e in ENGS}
        self.sems = {}

    def sem(self, name):
        h = self.ctx.enter_context(self.nc.semaphore(name))
        self.sems[name] = [h, 0]
        return name

    def val(self, name):
        return self.sems[name][1]

    CHAIN = {"vector": "ve", "scalar": "se", "gpsimd": "gp"}

    def do(self, eng, fn, wait=(), inc=(), dma=False):
        wait = list(wait)
        inc = list(inc)
        ch = self.CHAIN.get(eng)
        if ch and not dma:
            wait.append((ch, self.sems[ch][1]))
            if not any(s == ch for s, _ in inc):
                inc.append((ch, 1))
        if dma:
            for s, n in inc:
                wait.append((s, self.sems[s][1]))
        incs = []
        for s, n in inc:
            self.sems[s][1] += n
            incs.append((self.sems[s][0], n))
        waits = [(self.sems[s][0], v) for s, v in wait if v > 0]

        def run(e, fn=fn, waits=waits, incs=incs):
            for hh, v in waits:
                e.wait_ge(hh, v)
            r = fn(e)
            for hh, n in incs:
                r.then_inc(hh, n)

        self.q[eng].append(run)

    def replay(self, block):
        def make(lst):
            def f(e):
                for fn in lst:
                    fn(e)
            return f
        block.sync(make(self.q["sync"]))
        block.vector(make(self.q["vector"]))
        block.scalar(make(self.q["scalar"]))
        block.tensor(make(self.q["tensor"]))
        block.gpsimd(make(self.q["gpsimd"]))


def build(debug=0):
    nc = bass.Bass("TRN2", target_bir_lowering=False)

    x_own = nc.dram_tensor("x_own", [TOK, D], F32, kind="ExternalInput").ap()
    x_partner = nc.dram_tensor("x_partner", [TOK, D], F32, kind="ExternalInput").ap()
    w_qkv = nc.dram_tensor("w_qkv", [D, 3 * D], F32, kind="ExternalInput").ap()
    w_proj = nc.dram_tensor("w_proj", [D, D], F32, kind="ExternalInput").ap()
    lam_q1 = nc.dram_tensor("lam_q1", [64], F32, kind="ExternalInput").ap()
    lam_k1 = nc.dram_tensor("lam_k1", [64], F32, kind="ExternalInput").ap()
    lam_q2 = nc.dram_tensor("lam_q2", [64], F32, kind="ExternalInput").ap()
    lam_k2 = nc.dram_tensor("lam_k2", [64], F32, kind="ExternalInput").ap()
    rms_scale = nc.dram_tensor("rms_scale", [128], F32, kind="ExternalInput").ap()
    wq_r = nc.dram_tensor("wq_r", [D, 128], F32, kind="ExternalInput").ap()
    keys_own = nc.dram_tensor("keys_own", [64, 64], F32, kind="ExternalInput").ap()
    down_embed = nc.dram_tensor("down_embed", [4096, D], F32, kind="ExternalInput").ap()
    up_embed = nc.dram_tensor("up_embed", [4096, D], F32, kind="ExternalInput").ap()
    gate_w = nc.dram_tensor("gate_w", [D, 8192], F32, kind="ExternalInput").ap()
    up_w = nc.dram_tensor("up_w", [D, 8192], F32, kind="ExternalInput").ap()
    down_w = nc.dram_tensor("down_w", [8192, D], F32, kind="ExternalInput").ap()
    ag_idx = nc.dram_tensor("ag_idx", [128, 16], I32, kind="ExternalInput").ap()
    contrib = nc.dram_tensor("contrib", [2048, 16], F32)
    ag_out = nc.dram_tensor("ag_out", [8 * 2048, 16], F32)

    out = nc.dram_tensor("out", [TOK, D], F32, kind="ExternalOutput").ap()
    dbg = {}
    if debug:
        dbg["x1"] = nc.dram_tensor("dbg_x1", [D, TOK], F32, kind="ExternalOutput").ap()
    if debug >= 3:
        dbg["x1a"] = nc.dram_tensor("dbg_x1a", [D, TOK], F32, kind="ExternalOutput").ap()
        dbg["h"] = nc.dram_tensor("dbg_h", [D, TOK], BF16, kind="ExternalOutput").ap()
    if debug >= 4:
        dbg["eidx"] = nc.dram_tensor("dbg_eidx", [128, 64], I32, kind="ExternalOutput").ap()
        dbg["prall"] = nc.dram_tensor("dbg_prall", [128, 64], F32, kind="ExternalOutput").ap()
        dbg["ewr0"] = nc.dram_tensor("dbg_ewr0", [128, 8], F32, kind="ExternalOutput").ap()
        dbg["ewf0"] = nc.dram_tensor("dbg_ewf0", [128, 8], F32, kind="ExternalOutput").ap()
        dbg["acc0"] = nc.dram_tensor("dbg_acc0", [128, D], F32, kind="ExternalOutput").ap()
        dbg["sxy0"] = nc.dram_tensor("dbg_sxy0", [128, 32], F32, kind="ExternalOutput").ap()
        dbg["v0"] = nc.dram_tensor("dbg_v0", [2, 128, TOK], BF16, kind="ExternalOutput").ap()
    if debug >= 2:
        dbg["negB"] = nc.dram_tensor("dbg_negB", [2, 128, 1], F32, kind="ExternalOutput").ap()
        dbg["q"] = nc.dram_tensor("dbg_q", [2, 128, 128], BF16, kind="ExternalOutput").ap()
        dbg["k"] = nc.dram_tensor("dbg_k", [2, 128, 128], BF16, kind="ExternalOutput").ap()
        dbg["rows"] = nc.dram_tensor("dbg_rows", [2, 3, TOK], F32, kind="ExternalOutput").ap()
        dbg["pr"] = nc.dram_tensor("dbg_pr", [2, 128, 512], BF16, kind="ExternalOutput").ap()

    with ExitStack() as top:
        P = Prog(nc, top)
        sb = lambda n, s, d=F32: top.enter_context(nc.sbuf_tensor(n, s, d))

        for s in ("hl", "sl", "gw", "gg", "cc", "pe", "ve", "se", "gp"):
            P.sem(s)

        iota64 = sb("iota64", [128, 64])
        ident32 = sb("ident32", [128, 128])
        identbf = sb("identbf", [128, 128], BF16)
        ones_col = sb("ones_col", [128, 1], BF16)
        ones_rows = sb("ones_rows", [65, 128], BF16)
        eps_ln = sb("eps_ln", [128, 1])
        ones_sq = sb("ones_sq", [128, 128], BF16)
        ones_row32 = sb("ones_row32", [1, 128])
        lam_col = sb("lam_col", [128, 1])
        li_col = sb("li_col", [128, 1])
        rms_col = sb("rms_col", [128, 1])
        lam_t = sb("lam_t", [1, 1])
        sc_a = sb("sc_a", [1, 1])
        sc_b = sb("sc_b", [1, 1])
        sc_bb = sb("sc_bb", [1, 1], BF16)

        P.do("gpsimd", lambda g: g.iota(iota64[:], pattern=[[1, 64]], base=0,
                                        channel_multiplier=0,
                                        allow_small_or_imprecise_dtypes=True),
             inc=[("gp", 1)])
        for fn in (
            lambda g: g.memset(ident32[:], 0.0),
            lambda g: g.affine_select(out=ident32[:], in_=ident32[:],
                                      compare_op=ALU.not_equal, fill=1.0, base=0,
                                      pattern=[[-1, 128]], channel_multiplier=1),
            lambda g: g.memset(identbf[:], 0.0),
            lambda g: g.affine_select(out=identbf[:], in_=identbf[:],
                                      compare_op=ALU.not_equal, fill=1.0, base=0,
                                      pattern=[[-1, 128]], channel_multiplier=1),
            lambda g: g.memset(ones_col[:], 1.0),
            lambda g: g.memset(ones_rows[:], 1.0),
            lambda g: g.memset(eps_ln[:], 1e-5),
            lambda g: g.memset(ones_sq[:], 1.0),
            lambda g: g.memset(ones_row32[:], 1.0),
            lambda g: g.memset(li_col[:], math.log(1.0 - LI)),
        ):
            P.do("gpsimd", fn, wait=[("gp", P.val("gp"))], inc=[("gp", 1)])
        gp_consts = P.val("gp")

        p0ctx = ExitStack()
        lamrows = p0ctx.enter_context(nc.sbuf_tensor("lamrows", [1, 256], F32))
        rowtmp = p0ctx.enter_context(nc.sbuf_tensor("rowtmp", [1, 64], F32))
        rowtmp2 = p0ctx.enter_context(nc.sbuf_tensor("rowtmp2", [1, 64], F32))
        P.do("sync", lambda e: e.dma_start(
            out=rms_col[:], in_=rms_scale.rearrange("(p o) -> p o", o=1),
            allow_slow_non_contiguous=True), inc=[("hl", 16)], dma=True)
        for i, t in enumerate((lam_q1, lam_k1, lam_q2, lam_k2)):
            P.do("sync", lambda e, i=i, t=t: e.dma_start(
                out=lamrows[0:1, 64 * i:64 * (i + 1)], in_=t.rearrange("(o f) -> o f", o=1)),
                inc=[("hl", 16)], dma=True)
        hl_small = P.val("hl")

        P.do("vector", lambda v: v.tensor_mul(rowtmp[:], lamrows[0:1, 0:64],
                                              lamrows[0:1, 64:128]),
             wait=[("hl", hl_small)])
        P.do("vector", lambda v: v.tensor_mul(rowtmp2[:], lamrows[0:1, 128:192],
                                              lamrows[0:1, 192:256]))
        P.do("vector", lambda v: v.reduce_sum(sc_a[:], rowtmp[:], axis=AX.X))
        P.do("vector", lambda v: v.reduce_sum(sc_b[:], rowtmp2[:], axis=AX.X),
             inc=[("ve", 1)])
        ve_ls = P.val("ve")
        P.do("scalar", lambda s: s.activation(sc_a[:], sc_a[:], AF.Exp),
             wait=[("ve", ve_ls)])
        P.do("scalar", lambda s: s.activation(sc_b[:], sc_b[:], AF.Exp),
             inc=[("se", 1)])
        se_le = P.val("se")
        P.do("vector", lambda v: v.tensor_sub(lam_t[:], sc_a[:], sc_b[:]),
             wait=[("se", se_le)])
        P.do("vector", lambda v: v.tensor_scalar(out=lam_t[:], in0=lam_t[:],
                                                 scalar1=LI, scalar2=None, op0=ALU.add),
             inc=[("ve", 1)])
        ve_lam = P.val("ve")
        with nc.psum_tensor("p0_lamps", [128, 1], F32) as lam_ps:
            P.do("tensor", lambda t: t.matmul(lam_ps[:], ones_row32[:], lam_t[:],
                                              start=True, stop=True),
                 wait=[("ve", ve_lam), ("gp", gp_consts)], inc=[("pe", 1)])
            pe_lamb = P.val("pe")
            P.do("scalar", lambda s: s.copy(lam_col[:], lam_ps[:]),
                 wait=[("pe", pe_lamb)], inc=[("se", 1)])
        se_lamcol = P.val("se")
        p0ctx.close()

        out_all = [sb(f"oa{j}", [128, TOK], BF16) for j in range(NF)]

        ln_stack = ExitStack()
        ln_fm = [ln_stack.enter_context(nc.sbuf_tensor(f"ln_fm{j}", [128, BT], BF16))
                 for j in range(NF)]

        with ExitStack() as p1:
            sb1 = lambda n, s, d=F32: p1.enter_context(nc.sbuf_tensor(n, s, d))
            ps1 = lambda n, s, d=F32: p1.enter_context(nc.psum_tensor(n, s, d))
            xt = [sb1(f"p1_xt{p}", [128, D]) for p in range(2)]
            lnt = [sb1(f"p1_lnt{p}", [128, D], BF16) for p in range(2)]
            bnbuf = [sb1(f"p1_bn{p}", [128, 4, 6]) for p in range(2)]
            bnagg = [sb1(f"p1_ag{p}", [128, 2]) for p in range(2)]
            mrow = [sb1(f"p1_m{p}", [128, 1]) for p in range(2)]
            rrow = [sb1(f"p1_r{p}", [128, 1]) for p in range(2)]
            tp_ps = [ps1(f"p1_tp{p}", [128, 512], BF16) for p in range(2)]

            xt_free = [0, 0]
            lnt_free = [0, 0]
            tp_free = [0, 0]

            for i in range(16):
                par = i % 2
                src = x_own if i < 8 else x_partner
                r0 = (i % 8) * 128
                P.do("sync", lambda e, src=src, r0=r0, par=par: e.dma_start(
                    out=xt[par][:], in_=src[r0:r0 + 128, :]),
                    wait=[("ve", xt_free[par])], inc=[("hl", 16)], dma=True)
                hl_i = P.val("hl")
                for cth in range(4):
                    P.do("vector", lambda v, par=par, cth=cth: v.bn_stats(
                        bnbuf[par][:, cth, :], xt[par][:, 512 * cth:512 * (cth + 1)]),
                        wait=[("hl", hl_i)] if cth == 0 else ())
                P.do("vector", lambda v, par=par: v.bn_aggr(
                    bnagg[par][:], bnbuf[par].ap().rearrange("p a b -> p (a b)")))
                P.do("vector", lambda v, par=par: v.tensor_copy(mrow[par][:],
                                                                bnagg[par][:, 0:1]))
                P.do("vector", lambda v, par=par: v.tensor_add(
                    rrow[par][:], bnagg[par][:, 1:2], eps_ln[:]),
                    wait=[("gp", gp_consts)], inc=[("ve", 1)])
                ve_var = P.val("ve")
                P.do("scalar", lambda s, par=par: s.activation(rrow[par][:], rrow[par][:],
                                                               AF.Ln),
                     wait=[("ve", ve_var)])
                P.do("scalar", lambda s, par=par: s.activation(rrow[par][:], rrow[par][:],
                                                               AF.Exp, scale=-0.5),
                     inc=[("se", 1)])
                se_rstd = P.val("se")
                P.do("vector", lambda v, par=par: v.tensor_scalar(
                    out=lnt[par][:], in0=xt[par][:], scalar1=mrow[par][:, 0:1],
                    scalar2=rrow[par][:, 0:1], op0=ALU.subtract, op1=ALU.mult),
                    wait=[("se", se_rstd), ("pe", lnt_free[par])], inc=[("ve", 1)])
                ve_ln = P.val("ve")
                xt_free[par] = ve_ln
                for grp in range(4):
                    gpar = grp % 2
                    for m in range(4):
                        jj = grp * 4 + m
                        P.do("tensor", lambda t, par=par, gpar=gpar, m=m, jj=jj:
                             t.transpose(tp_ps[gpar][:, 128 * m:128 * (m + 1)],
                                         lnt[par][:, 128 * jj:128 * (jj + 1)], identbf[:]),
                             wait=([("ve", ve_ln), ("se", tp_free[gpar]),
                                    ("gp", gp_consts)] if m == 0 else ()),
                             inc=[("pe", 1)] if m == 3 else ())
                    pe_tp = P.val("pe")
                    if grp == 3:
                        lnt_free[par] = pe_tp
                    for m in range(4):
                        jj = grp * 4 + m
                        P.do("scalar", lambda s, i=i, gpar=gpar, m=m, jj=jj: s.copy(
                            ln_fm[jj][:, 128 * i:128 * (i + 1)],
                            tp_ps[gpar][:, 128 * m:128 * (m + 1)]),
                            wait=[("pe", pe_tp)] if m == 0 else (),
                            inc=[("se", 1)] if m == 3 else ())
                    tp_free[gpar] = P.val("se")
            se_ln_done = P.val("se")


        wq_view = w_qkv.rearrange("(f p) (blk grp c) -> p f blk grp c",
                                  p=128, blk=6, grp=2)

        epi_ve = 0
        se_prev_ctx = se_ln_done
        ve_prev_ctx = 0
        for g in range(2):
            with ExitStack() as pg:
                sbg = lambda n, s, d=F32: pg.enter_context(nc.sbuf_tensor(n, s, d))
                q_g = [sbg(f"g{g}_q{h}", [128, TOK], BF16) for h in range(8)]
                k_g = [sbg(f"g{g}_k{h}", [128, BT], BF16) for h in range(8)]
                v_g = [sbg(f"g{g}_v{i}", [128, TOK], BF16) for i in range(16)]
                negB8 = sbg(f"g{g}_negB8", [128, 8])
                wstack = ExitStack()
                wband2 = [wstack.enter_context(
                    nc.sbuf_tensor(f"g{g}_wband{p}", [128, 16 * 256], BF16))
                    for p in range(2)]

                def qk_lhsT(wb, f, hc2):
                    return wb[:, f * 256 + hc2 * 128: f * 256 + (hc2 + 1) * 128]

                batches = ([("q", hb) for hb in range(4)] +
                           [("k", hb) for hb in range(4)] +
                           [("v", hb) for hb in range(4)])
                vsrc = wq_view[:, :, 4 + g, :, :].rearrange("p f grp c -> p f (grp c)")

                def issue_band(bi, wait_extra):
                    kind, hb = batches[bi]
                    wb = wband2[bi % 2]
                    if kind in ("q", "k"):
                        blk0 = 0 if kind == "q" else 2
                        wb4 = wb.ap().rearrange("p (f hcs b c) -> p f hcs b c",
                                                f=16, hcs=2, b=2)
                        for bnd in range(2):
                            for hc2 in range(2):
                                P.do("gpsimd", lambda e, g=g, hb=hb, bnd=bnd,
                                     hc2=hc2, wb4=wb4, blk0=blk0: e.dma_start(
                                         out=wb4[:, :, hc2, bnd, :],
                                         in_=wq_view[:, :, blk0 + bnd, g,
                                                     128 * hb + 64 * hc2:
                                                     128 * hb + 64 * (hc2 + 1)]),
                                     wait=[("gw", P.val("gw"))] + wait_extra,
                                     inc=[("gw", 16)], dma=True)
                                wait_extra = []
                    else:
                        P.do("gpsimd", lambda e, hb=hb, wb=wb, vsrc=vsrc: e.dma_start(
                            out=wb.ap().rearrange("p (f c) -> p f c", f=16),
                            in_=vsrc[:, :, 256 * hb:256 * (hb + 1)]),
                            wait=[("gw", P.val("gw"))] + wait_extra,
                            inc=[("gw", 16)], dma=True)
                    return P.val("gw")

                with ExitStack() as pq:
                    q_ps = [pq.enter_context(
                        nc.psum_tensor(f"g{g}_qps{p}", [128, TOK], F32)) for p in range(2)]
                    k_ps = [pq.enter_context(
                        nc.psum_tensor(f"g{g}_kps{p}", [128, 512], F32)) for p in range(4)]
                    qps_free = [se_prev_ctx, se_prev_ctx]
                    qps_extra = [("ve", ve_prev_ctx)]
                    kps_free = [0, 0]
                    vps_free = [0, 0]

                    g_entry = [("pe", P.val("pe")), ("ve", P.val("ve")),
                               ("se", P.val("se")), ("sl", P.val("sl"))]
                    gw_done = []
                    pe_batch_end = [0, 0]
                    gw_done.append(issue_band(0, g_entry))

                    for bi, (kind, hb) in enumerate(batches):
                        par = bi % 2
                        wb = wband2[par]
                        if bi + 1 < len(batches):
                            gw_done.append(issue_band(
                                bi + 1, [("pe", pe_batch_end[(bi + 1) % 2])]))
                        gw_b = gw_done[bi]
                        if kind == "q":
                            for hc4 in range(2):
                                hc = 2 * hb + hc4
                                qpar = hc % 2
                                for f in range(16):
                                    for half in range(2):
                                        P.do("tensor",
                                             lambda t, wb=wb, hc4=hc4, f=f, half=half,
                                             qpar=qpar: t.matmul(
                                                 q_ps[qpar][:, 512 * half:512 * (half + 1)],
                                                 qk_lhsT(wb, f, hc4),
                                                 ln_fm[f][:, 512 * half:512 * (half + 1)],
                                                 start=(f == 0), stop=(f == 15)),
                                             wait=(([("gw", gw_b),
                                                     ("se", qps_free[qpar]),
                                                     ("se", se_ln_done)] + qps_extra)
                                                   if (f == 0 and half == 0) else ()),
                                             inc=([("pe", 1)]
                                                  if (f == 15 and half == 1) else ()))
                                pe_q = P.val("pe")
                                P.do("scalar", lambda s, hc=hc, qpar=qpar: s.copy(
                                    q_g[hc][:], q_ps[qpar][:]),
                                    wait=[("pe", pe_q)], inc=[("se", 1)])
                                qps_free[qpar] = P.val("se")
                        elif kind == "k":
                            for hc4 in range(2):
                                hc = 2 * hb + hc4
                                base = (hc % 2) * 2
                                for f in range(16):
                                    for ts4 in range(2):
                                        P.do("tensor",
                                             lambda t, wb=wb, hc4=hc4, f=f, ts4=ts4,
                                             base=base: t.matmul(
                                                 k_ps[base + ts4][:],
                                                 qk_lhsT(wb, f, hc4),
                                                 ln_fm[f][:, 512 * ts4:512 * (ts4 + 1)],
                                                 start=(f == 0), stop=(f == 15)),
                                             wait=([("gw", gw_b),
                                                    ("se", kps_free[hc % 2])]
                                                   if (f == 0 and ts4 == 0) else ()),
                                             inc=([("pe", 1)]
                                                  if (f == 15 and ts4 == 1) else ()))
                                pe_k = P.val("pe")
                                for ts4 in range(2):
                                    P.do("scalar", lambda s, hc=hc, ts4=ts4, base=base:
                                         s.copy(k_g[hc][:, 512 * ts4:512 * (ts4 + 1)],
                                                k_ps[base + ts4][:]),
                                         wait=[("pe", pe_k)] if ts4 == 0 else (),
                                         inc=[("se", 1)] if ts4 == 1 else ())
                                kps_free[hc % 2] = P.val("se")
                                for f in range(16):
                                    for ts4 in range(2):
                                        P.do("tensor",
                                             lambda t, wb=wb, hc4=hc4, f=f, ts4=ts4,
                                             base=base: t.matmul(
                                                 k_ps[base + ts4][:],
                                                 qk_lhsT(wb, f, hc4),
                                                 ln_fm[f][:, 1024 + 512 * ts4:
                                                           1024 + 512 * (ts4 + 1)],
                                                 start=(f == 0), stop=(f == 15)),
                                             wait=([("se", kps_free[hc % 2])]
                                                   if (f == 0 and ts4 == 0) else ()),
                                             inc=([("pe", 1)]
                                                  if (f == 15 and ts4 == 1) else ()))
                                pe_k = P.val("pe")
                                for ts4 in range(2):
                                    P.do("scalar", lambda s, hc=hc, ts4=ts4, base=base:
                                         s.copy(k_g[hc][:, 1024 + 512 * ts4:
                                                        1024 + 512 * (ts4 + 1)],
                                                k_ps[base + ts4][:]),
                                         wait=[("pe", pe_k)] if ts4 == 0 else (),
                                         inc=[("se", 1)] if ts4 == 1 else ())
                                kps_free[hc % 2] = P.val("se")
                        else:
                            if vps_free == [0, 0]:
                                vps_free = [kps_free[0], kps_free[0]]
                            for ti in range(16):
                                vpar = ti % 2
                                for f in range(16):
                                    P.do("tensor", lambda t, wb=wb, ti=ti, f=f,
                                         vpar=vpar: t.matmul(
                                             k_ps[vpar][:, 0:256],
                                             ln_fm[f][:, 128 * ti:128 * (ti + 1)],
                                             wb[:, f * 256:(f + 1) * 256],
                                             start=(f == 0), stop=(f == 15)),
                                         wait=([("gw", gw_b), ("se", vps_free[vpar])]
                                               if f == 0 else ()),
                                         inc=([("pe", 1)] if f == 15 else ()))
                                pe_v = P.val("pe")
                                P.do("scalar", lambda s, ti=ti, hb=hb, vpar=vpar:
                                     s.copy(v_g[ti][:, 256 * hb:256 * (hb + 1)],
                                            k_ps[vpar][:, 0:256]),
                                     wait=[("pe", pe_v)], inc=[("se", 1)])
                                vps_free[vpar] = P.val("se")
                        pe_batch_end[par] = P.val("pe")
                    se_v_end = P.val("se")


                with ExitStack() as pb:
                    sqt = pb.enter_context(nc.sbuf_tensor(f"g{g}_bbsq", [128, BT], BF16))
                    qn_ps = pb.enter_context(nc.psum_tensor(f"g{g}_bbqn", [1, TOK], F32))
                    kn_ps = pb.enter_context(nc.psum_tensor(f"g{g}_bbkn", [1, BT], F32))
                    bps = pb.enter_context(nc.psum_tensor(f"g{g}_bbps", [128, 1], F32))
                    nps_free = []
                    bps_free = []
                    for hc in range(8):
                        P.do("vector", lambda v, hc=hc: v.tensor_mul(
                            sqt[:, 0:TOK], q_g[hc][:], q_g[hc][:]),
                            wait=([("se", se_v_end), ("ve", ve_lam)] if hc == 0 else
                                  [("pe", P.val("pe"))]),
                            inc=[("ve", 1)])
                        ve_sq = P.val("ve")
                        for half in range(2):
                            P.do("tensor", lambda t, half=half: t.matmul(
                                qn_ps[:, 512 * half:512 * (half + 1)], ones_col[:],
                                sqt[:, 512 * half:512 * (half + 1)],
                                start=True, stop=True),
                                wait=(([("ve", ve_sq), ("se", se_v_end)] + nps_free)
                                      if half == 0 else ()),
                                inc=[("pe", 1)] if half == 1 else ())
                        pe_qn = P.val("pe")
                        P.do("vector", lambda v, hc=hc: v.tensor_mul(
                            sqt[:], k_g[hc][:], k_g[hc][:]),
                            wait=[("pe", pe_qn)], inc=[("ve", 1)])
                        ve_sk = P.val("ve")
                        for qtr in range(4):
                            P.do("tensor", lambda t, qtr=qtr: t.matmul(
                                kn_ps[:, 512 * qtr:512 * (qtr + 1)], ones_col[:],
                                sqt[:, 512 * qtr:512 * (qtr + 1)],
                                start=True, stop=True),
                                wait=[("ve", ve_sk)] if qtr == 0 else (),
                                inc=[("pe", 1)] if qtr == 3 else ())
                        pe_kn = P.val("pe")
                        P.do("vector", lambda v: v.reduce_max(sc_a[:], qn_ps[:],
                                                              axis=AX.X),
                             wait=[("pe", pe_kn)])
                        P.do("vector", lambda v: v.reduce_max(sc_b[:], kn_ps[:],
                                                              axis=AX.X))
                        P.do("vector", lambda v: v.tensor_mul(sc_a[:], sc_a[:],
                                                              sc_b[:]),
                             inc=[("ve", 1)])
                        ve_bb = P.val("ve")
                        nps_free = [("ve", ve_bb)]
                        P.do("scalar", lambda s: s.activation(sc_a[:], sc_a[:], AF.Ln),
                             wait=[("ve", ve_bb)])
                        P.do("scalar", lambda s: s.activation(sc_a[:], sc_a[:], AF.Exp,
                                                              scale=0.5),
                             inc=[("se", 1)])
                        se_bb = P.val("se")
                        P.do("vector", lambda v: v.tensor_scalar(
                            out=sc_a[:], in0=sc_a[:], scalar1=-0.125 * 1.05,
                            scalar2=None, op0=ALU.mult), wait=[("se", se_bb)])
                        P.do("vector", lambda v: v.tensor_copy(sc_bb[:], sc_a[:]),
                             inc=[("ve", 1)])
                        ve_bB = P.val("ve")
                        P.do("tensor", lambda t: t.matmul(bps[:], ones_rows[0:1, :],
                                                          sc_bb[:],
                                                          start=True, stop=True),
                             wait=[("ve", ve_bB)] + bps_free, inc=[("pe", 1)])
                        pe_bb = P.val("pe")
                        P.do("scalar", lambda s, hc=hc: s.copy(negB8[:, hc:hc + 1],
                                                               bps[:]),
                             wait=[("pe", pe_bb)], inc=[("se", 1)])
                        bps_free = [("se", P.val("se"))]
                    se_negB = P.val("se")
                    if debug >= 2:
                        P.do("scalar", lambda s, g=g: s.dma_start(
                            out=dbg["negB"][g, :, 0:1], in_=negB8[:, 0:1]),
                            wait=[("se", se_negB)], inc=[("sl", 16)], dma=True)
                        P.do("scalar", lambda s, g=g: s.dma_start(
                            out=dbg["q"][g, :, :], in_=q_g[0][:, 0:128]),
                            inc=[("sl", 16)], dma=True)
                        P.do("scalar", lambda s, g=g: s.dma_start(
                            out=dbg["k"][g, :, :], in_=k_g[0][:, 0:128]),
                            inc=[("sl", 16)], dma=True)
                        P.do("scalar", lambda s, g=g: s.dma_start(
                            out=dbg["v0"][g, :, :], in_=v_g[0][:]),
                            inc=[("sl", 16)], dma=True)

                wstack.close()

                with ExitStack() as pa:
                    sba = lambda n, s, d=F32: pa.enter_context(nc.sbuf_tensor(n, s, d))
                    psa = lambda n, s, d=F32: pa.enter_context(nc.psum_tensor(n, s, d))
                    s_pair = [psa(f"g{g}_at_s{p}", [128, 1024]) for p in range(2)]
                    pv_ps = [psa(f"g{g}_at_pv{b}", [128, 512]) for b in range(2)]
                    sum_ps = [psa(f"g{g}_at_sm{b}", [128, 512]) for b in range(2)]
                    ppair = [sba(f"g{g}_at_pr{p}", [128, 1024], BF16) for p in range(2)]
                    u_t = sba(f"g{g}_at_u", [128, TOK], BF16)
                    t2b = sba(f"g{g}_at_t2", [128, TOK], BF16)
                    s1b = sba(f"g{g}_at_s1b", [128, TOK])
                    s2b = sba(f"g{g}_at_s2b", [128, TOK])

                    s_free = [[("se", se_negB)] for _ in range(2)]
                    probs_free = [[] for _ in range(2)]
                    sum_free = [[("se", se_negB)] for _ in range(2)]
                    pv_free = [[("se", se_negB)] for _ in range(2)]
                    pe_s_at = [0, 0]
                    se_exp_at = [0] * 16

                    for hc in range(8):
                        for qh in range(2):
                            def emit_scores(j, hc=hc, qh=qh):
                                par = j % 2
                                for b2 in range(2):
                                    lo, hi = (0, 64) if b2 == 0 else (64, 128)
                                    tp = (0, 0) if b2 == 0 else (64, 0)
                                    P.do("tensor",
                                         lambda t, hc=hc, qh=qh, j=j, b2=b2, lo=lo,
                                         hi=hi, tp=tp, par=par: t.matmul(
                                             s_pair[par][:, 512 * b2:512 * (b2 + 1)],
                                             k_g[hc][lo:hi, 128 * j:128 * (j + 1)],
                                             q_g[hc][lo:hi, 512 * qh:512 * (qh + 1)],
                                             start=True, stop=True, tile_position=tp),
                                         wait=s_free[par] if b2 == 0 else (),
                                         inc=[("pe", 1)] if b2 == 1 else ())
                                s_free[par] = []
                                pe_s_at[par] = P.val("pe")

                            def emit_exp(j, hc=hc):
                                par = j % 2
                                P.do("scalar", lambda s, par=par, hc=hc: s.activation(
                                    ppair[par][:], s_pair[par][:], AF.Exp,
                                    bias=negB8[:, hc:hc + 1], scale=0.125),
                                    wait=[("pe", pe_s_at[par])] + probs_free[par],
                                    inc=[("se", 1)])
                                probs_free[par] = []
                                s_free[par] = [("se", P.val("se"))]
                                se_exp_at[j] = P.val("se")

                            def emit_pv(j, hc=hc):
                                par = j % 2
                                for b2 in range(2):
                                    P.do("tensor", lambda t, hc=hc, j=j, b2=b2,
                                         par=par: t.matmul(
                                             pv_ps[b2][:],
                                             v_g[j][:, 128 * hc:128 * (hc + 1)],
                                             ppair[par][:, 512 * b2:512 * (b2 + 1)],
                                             start=(j == 0), stop=(j == 15)),
                                         wait=([("se", se_exp_at[j])] +
                                               (pv_free[b2] if j == 0 else [])))
                                    pv_free[b2] = []
                                for b2 in range(2):
                                    P.do("tensor", lambda t, j=j, b2=b2, par=par:
                                         t.matmul(
                                             sum_ps[b2][:], ones_sq[:],
                                             ppair[par][:, 512 * b2:512 * (b2 + 1)],
                                             start=(j == 0), stop=(j == 15)),
                                         wait=(sum_free[b2] if j == 0 else ()),
                                         inc=[("pe", 1)] if b2 == 1 else ())
                                    sum_free[b2] = []
                                probs_free[par] = [("pe", P.val("pe"))]

                            emit_scores(0)
                            emit_exp(0)
                            emit_scores(1)
                            emit_exp(1)
                            for j in range(16):
                                emit_pv(j)
                                if j + 2 < 16:
                                    emit_scores(j + 2)
                                    emit_exp(j + 2)
                            pe_pv = P.val("pe")
                            P.do("vector", lambda v, qh=qh: v.tensor_copy(
                                u_t[:, 512 * qh:512 * (qh + 1)], pv_ps[0][:]),
                                wait=[("pe", pe_pv)])
                            P.do("vector", lambda v, qh=qh: v.tensor_copy(
                                t2b[:, 512 * qh:512 * (qh + 1)], pv_ps[1][:]))
                            P.do("vector", lambda v, qh=qh: v.tensor_copy(
                                s1b[:, 512 * qh:512 * (qh + 1)], sum_ps[0][:]))
                            P.do("vector", lambda v, qh=qh: v.tensor_copy(
                                s2b[:, 512 * qh:512 * (qh + 1)], sum_ps[1][:]),
                                inc=[("ve", 1)])
                            ve_ev = P.val("ve")
                            for b2 in range(2):
                                pv_free[b2] = [("ve", ve_ev)]
                                sum_free[b2] = [("ve", ve_ev)]

                        P.do("vector", lambda v: v.tensor_mul(ppair[1][:], s1b[:],
                                                              s1b[:]),
                             wait=[("ve", ve_ev)] + probs_free[1])
                        probs_free[1] = []
                        P.do("vector", lambda v: v.tensor_scalar(
                            out=ppair[1][:], in0=ppair[1][:], scalar1=1e-6,
                            scalar2=None, op0=ALU.mult), inc=[("ve", 1)])
                        ve_eps = P.val("ve")
                        P.do("scalar", lambda s: s.activation(s1b[:], s1b[:], AF.Ln),
                             wait=[("se", se_lamcol), ("ve", ve_eps)])
                        P.do("scalar", lambda s: s.activation(s2b[:], s2b[:], AF.Ln),
                             inc=[("se", 1)])
                        se_ln12 = P.val("se")
                        P.do("vector", lambda v: v.tensor_sub(s1b[:], s1b[:], s2b[:]),
                             wait=[("se", se_ln12)], inc=[("ve", 1)])
                        ve_d12 = P.val("ve")
                        P.do("scalar", lambda s: s.activation(s1b[:], s1b[:], AF.Exp),
                             wait=[("ve", ve_d12)], inc=[("se", 1)])
                        se_rho = P.val("se")
                        P.do("vector", lambda v: v.scalar_tensor_tensor(
                            out=ppair[0][:], in0=t2b[:], scalar=lam_col[:, 0:1],
                            in1=s1b[:], op0=ALU.mult, op1=ALU.mult),
                            wait=[("se", se_rho)] + probs_free[0])
                        probs_free[0] = []
                        P.do("vector", lambda v: v.tensor_sub(t2b[:], u_t[:],
                                                              ppair[0][:]))
                        P.do("vector", lambda v: v.tensor_mul(ppair[0][:], t2b[:],
                                                              t2b[:]),
                             inc=[("ve", 1)])
                        ve_uu = P.val("ve")
                        for half in range(2):
                            P.do("tensor", lambda t, half=half: t.matmul(
                                sum_ps[half][:], ones_sq[:],
                                ppair[0][:, 512 * half:512 * (half + 1)],
                                start=True, stop=True),
                                wait=[("ve", ve_uu)] + sum_free[half],
                                inc=[("pe", 1)])
                            sum_free[half] = []
                        pe_ms = P.val("pe")
                        for half in range(2):
                            P.do("vector", lambda v, half=half: v.scalar_tensor_tensor(
                                out=s2b[:, 512 * half:512 * (half + 1)],
                                in0=sum_ps[half][:], scalar=1.0 / 128,
                                in1=ppair[1][:, 512 * half:512 * (half + 1)],
                                op0=ALU.mult, op1=ALU.add),
                                wait=[("pe", pe_ms)] if half == 0 else (),
                                inc=[("ve", 1)] if half == 1 else ())
                        ve_ms = P.val("ve")
                        probs_free[0] = [("ve", ve_ms)]
                        probs_free[1] = [("ve", ve_ms)]
                        for b2 in range(2):
                            sum_free[b2] = [("ve", ve_ms)]
                        P.do("scalar", lambda s: s.activation(s2b[:], s2b[:], AF.Ln),
                             wait=[("ve", ve_ms)])
                        P.do("scalar", lambda s: s.activation(s2b[:], s2b[:], AF.Exp,
                                                              scale=-0.5,
                                                              bias=li_col[:, 0:1]),
                             inc=[("se", 1)])
                        se_rst = P.val("se")
                        P.do("vector", lambda v, g=g, hc=hc: v.scalar_tensor_tensor(
                            out=out_all[g * 8 + hc][:], in0=t2b[:],
                            scalar=rms_col[:, 0:1], in1=s2b[:],
                            op0=ALU.mult, op1=ALU.mult),
                            wait=[("se", se_rst), ("hl", hl_small)], inc=[("ve", 1)])
                        epi_ve = P.val("ve")
                    se_prev_ctx = P.val("se")
                    ve_prev_ctx = P.val("ve")

        ln_stack.close()
        x1 = [sb(f"x1_{o}", [128, TOK]) for o in range(NF)]
        wqr = sb("p6_wqr", [128, 16 * 128], BF16)
        keysb = sb("p6_keys", [64, 64], BF16)
        P.do("gpsimd", lambda e: e.dma_start(
            out=wqr.ap().rearrange("p (f c) -> p f c", f=16),
            in_=wq_r.rearrange("(f p) c -> p f c", p=128)),
            wait=[("gw", P.val("gw")), ("pe", P.val("pe")), ("ve", P.val("ve")),
                  ("se", P.val("se"))], inc=[("gw", 16)], dma=True)
        P.do("gpsimd", lambda e: e.dma_start(out=keysb[:], in_=keys_own),
             wait=[("gw", P.val("gw"))], inc=[("gw", 16)], dma=True)
        gw_p6pre = P.val("gw")

        with ExitStack() as p4:
            sb4 = lambda n, s, d=F32: p4.enter_context(nc.sbuf_tensor(n, s, d))
            ps4 = lambda n, s, d=F32: p4.enter_context(nc.psum_tensor(n, s, d))
            xt4 = [sb4(f"p4_xt{p}", [128, D]) for p in range(2)]
            wpo = [sb4(f"p4_wp{p}", [128, D], BF16) for p in range(2)]
            xps = [ps4(f"p4_xp{p}", [128, 512]) for p in range(2)]
            pr_ps = [ps4(f"p4_pr{p}", [128, TOK]) for p in range(2)]
            wp_view = w_proj.rearrange("(f p) (o c) -> p f o c", p=128, o=16)

            pe_barrier = P.val("pe")
            ve_barrier = P.val("ve")
            se_barrier = P.val("se")
            sl_barrier = P.val("sl")
            xt4_free = [0, 0]
            xps_free = [[("ve", ve_prev_ctx), ("se", se_prev_ctx)] for _ in range(2)]
            prps_free = [[("ve", ve_prev_ctx), ("se", se_prev_ctx)] for _ in range(2)]

            for ti in range(8):
                par = ti % 2
                P.do("sync", lambda e, ti=ti, par=par: e.dma_start(
                    out=xt4[par][:], in_=x_own[128 * ti:128 * (ti + 1), :]),
                    wait=([("pe", xt4_free[par])] +
                          ([("pe", pe_barrier), ("ve", ve_barrier),
                            ("se", se_barrier), ("sl", sl_barrier)] if ti == 0 else [])),
                    inc=[("hl", 16)], dma=True)
                hl_x = P.val("hl")
                for grp in range(4):
                    gpar = grp % 2
                    for m in range(4):
                        o = grp * 4 + m
                        P.do("tensor", lambda t, par=par, gpar=gpar, m=m, o=o:
                             t.transpose(xps[gpar][:, 128 * m:128 * (m + 1)],
                                         xt4[par][:, 128 * o:128 * (o + 1)], ident32[:]),
                             wait=([("hl", hl_x)] + xps_free[gpar]) if m == 0 else (),
                             inc=[("pe", 1)] if m == 3 else ())
                    pe_x = P.val("pe")
                    if grp == 3:
                        xt4_free[par] = pe_x
                    for m in range(4):
                        o = grp * 4 + m
                        P.do("vector", lambda v, ti=ti, gpar=gpar, m=m, o=o: v.tensor_copy(
                            x1[o][:, 128 * ti:128 * (ti + 1)],
                            xps[gpar][:, 128 * m:128 * (m + 1)]),
                            wait=[("pe", pe_x)] if m == 0 else (),
                            inc=[("ve", 1)] if m == 3 else ())
                    xps_free[gpar] = [("ve", P.val("ve"))]
            ve_xinit = P.val("ve")

            for o in range(16):
                par = o % 2
                P.do("gpsimd", lambda e, o=o, par=par: e.dma_start(
                    out=wpo[par].ap().rearrange("p (f c) -> p f c", f=16),
                    in_=wp_view[:, :, o, :]),
                    wait=[("gw", P.val("gw")), ("ve", ve_barrier), ("se", se_barrier)]
                    + ([("pe", P.val("pe"))] if o >= 2 else []),
                    inc=[("gw", 16)], dma=True)
                gw_wp = P.val("gw")
                for f in range(16):
                    for half in range(2):
                        P.do("tensor", lambda t, o=o, par=par, f=f, half=half: t.matmul(
                            pr_ps[par][:, 512 * half:512 * (half + 1)],
                            wpo[par][:, 128 * f:128 * (f + 1)],
                            out_all[f][:, 512 * half:512 * (half + 1)],
                            start=(f == 0), stop=(f == 15)),
                            wait=(([("gw", gw_wp), ("ve", epi_ve)] + prps_free[par])
                                  if (f == 0 and half == 0) else ()),
                            inc=([("pe", 1)] if (f == 15 and half == 1) else ()))
                pe_pr = P.val("pe")
                P.do("vector", lambda v, o=o, par=par: v.tensor_add(
                    x1[o][:], x1[o][:], pr_ps[par][:]),
                    wait=[("pe", pe_pr), ("ve", ve_xinit)], inc=[("ve", 1)])
                prps_free[par] = [("ve", P.val("ve"))]
            ve_x1 = P.val("ve")

        if debug >= 3:
            for o in range(16):
                P.do("scalar", lambda s, o=o: s.dma_start(
                    out=dbg["x1a"][128 * o:128 * (o + 1), :], in_=x1[o][:]),
                    wait=[("ve", ve_x1)] if o == 0 else (), inc=[("sl", 16)], dma=True)

        h_fm = [sb(f"h_fm{o}", [128, TOK], BF16) for o in range(NF)]

        with ExitStack() as p5:
            sb5 = lambda n, s, d=F32: p5.enter_context(nc.sbuf_tensor(n, s, d))
            ps5 = lambda n, s, d=F32: p5.enter_context(nc.psum_tensor(n, s, d))
            xb = [sb5(f"p5_xb{p}", [128, TOK], BF16) for p in range(2)]
            xbsq = [sb5(f"p5_xq{p}", [128, TOK], BF16) for p in range(2)]
            nrm = sb5("p5_nrm", [128, TOK])
            mean_r = sb5("p5_mean", [1, TOK])
            var_r = sb5("p5_var", [1, TOK])
            tmp_r = sb5("p5_tmp", [1, TOK])
            meanb_r = sb5("p5_meanb", [1, TOK], BF16)
            rstdb_r = sb5("p5_rstdb", [1, TOK], BF16)
            sm_ps = ps5("p5_sm", [1, TOK])
            sq_ps = ps5("p5_sq", [1, TOK])
            bm_ps = ps5("p5_bm", [128, TOK])
            br_ps = ps5("p5_br", [128, TOK])

            p5_entry = [("ve", P.val("ve")), ("se", P.val("se")),
                        ("pe", P.val("pe"))]
            xb_free = [0, 0]
            for o in range(16):
                par = o % 2
                P.do("scalar", lambda s, o=o, par=par: s.copy(xb[par][:], x1[o][:]),
                     wait=[("ve", ve_x1), ("pe", xb_free[par])])
                P.do("vector", lambda v, par=par: v.tensor_mul(xbsq[par][:], xb[par][:],
                                                               xb[par][:]),
                     wait=[("se", P.val("se"))], inc=[("ve", 1)])
                ve_xb = P.val("ve")
                for half in range(2):
                    P.do("tensor", lambda t, par=par, half=half, o=o: t.matmul(
                        sm_ps[:, 512 * half:512 * (half + 1)], ones_col[:],
                        xb[par][:, 512 * half:512 * (half + 1)],
                        start=(o == 0), stop=(o == 15)),
                        wait=(([("ve", ve_xb)] + (p5_entry if o == 0 else []))
                              if half == 0 else ()))
                    P.do("tensor", lambda t, par=par, half=half, o=o: t.matmul(
                        sq_ps[:, 512 * half:512 * (half + 1)], ones_col[:],
                        xbsq[par][:, 512 * half:512 * (half + 1)],
                        start=(o == 0), stop=(o == 15)),
                        inc=([("pe", 1)] if half == 1 else ()))
                xb_free[par] = P.val("pe")
            pe_st5 = P.val("pe")
            P.do("vector", lambda v: v.tensor_scalar(
                out=mean_r[:], in0=sm_ps[:], scalar1=1.0 / D, scalar2=None,
                op0=ALU.mult), wait=[("pe", pe_st5)])
            P.do("vector", lambda v: v.tensor_scalar(
                out=var_r[:], in0=sq_ps[:], scalar1=1.0 / D, scalar2=None,
                op0=ALU.mult))
            P.do("vector", lambda v: v.tensor_mul(tmp_r[:], mean_r[:], mean_r[:]))
            P.do("vector", lambda v: v.tensor_sub(var_r[:], var_r[:], tmp_r[:]))
            P.do("vector", lambda v: v.tensor_scalar(
                out=var_r[:], in0=var_r[:], scalar1=1.0, scalar2=1e-5,
                op0=ALU.mult, op1=ALU.add))
            P.do("vector", lambda v: v.tensor_copy(meanb_r[:], mean_r[:]),
                 inc=[("ve", 1)])
            ve_r5 = P.val("ve")
            P.do("scalar", lambda s: s.activation(var_r[:], var_r[:],
                                                  AF.Ln), wait=[("ve", ve_r5)])
            P.do("scalar", lambda s: s.activation(var_r[:], var_r[:],
                                                  AF.Exp, scale=-0.5),
                 inc=[("se", 1)])
            se_r5 = P.val("se")
            P.do("vector", lambda v: v.tensor_copy(rstdb_r[:], var_r[:]),
                 wait=[("se", se_r5)], inc=[("ve", 1)])
            ve_r5b = P.val("ve")
            for half in range(2):
                P.do("tensor", lambda t, half=half: t.matmul(
                    bm_ps[:, 512 * half:512 * (half + 1)], ones_rows[0:1, :],
                    meanb_r[0:1, 512 * half:512 * (half + 1)], start=True, stop=True),
                    wait=[("ve", ve_r5b)] if half == 0 else ())
                P.do("tensor", lambda t, half=half: t.matmul(
                    br_ps[:, 512 * half:512 * (half + 1)], ones_rows[0:1, :],
                    rstdb_r[0:1, 512 * half:512 * (half + 1)], start=True, stop=True),
                    inc=[("pe", 1)] if half == 1 else ())
            pe_bc5 = P.val("pe")
            for o in range(16):
                P.do("vector", lambda v, o=o: v.tensor_sub(nrm[:], x1[o][:], bm_ps[:]),
                     wait=[("pe", pe_bc5)] if o == 0 else ())
                P.do("vector", lambda v, o=o: v.tensor_mul(h_fm[o][:], nrm[:], br_ps[:]),
                     inc=[("ve", 1)])
            ve_h = P.val("ve")

        if debug >= 3:
            for o in range(16):
                P.do("scalar", lambda s, o=o: s.dma_start(
                    out=dbg["h"][128 * o:128 * (o + 1), :], in_=h_fm[o][:]),
                    wait=[("ve", ve_h)] if o == 0 else (), inc=[("sl", 16)], dma=True)

        with ExitStack() as p6:
            sb6 = lambda n, s, d=F32: p6.enter_context(nc.sbuf_tensor(n, s, d))
            ps6 = lambda n, s, d=F32: p6.enter_context(nc.psum_tensor(n, s, d))
            q_rb = sb6("p6_qrb", [64, BT], BF16)
            scr = [sb6(f"p6_scr{p}", [128, 64]) for p in range(2)]
            v8t = sb6("p6_v8", [128, 8])
            u8t = sb6("p6_u8", [128, 8], U32)
            contrib_sb = sb6("p6_contrib", [128, 256])
            qr_ps = ps6("p6_qps", [128, TOK])
            sc_ps = [ps6(f"p6_sps{p}", [128, 64]) for p in range(2)]

            p6_entry = [("ve", P.val("ve")), ("se", P.val("se")),
                        ("pe", P.val("pe"))]
            gw_p6 = gw_p6pre
            for f in range(16):
                for half in range(2):
                    P.do("tensor", lambda t, f=f, half=half: t.matmul(
                        qr_ps[:, 512 * half:512 * (half + 1)],
                        wqr[:, 128 * f:128 * (f + 1)],
                        h_fm[f][:, 512 * half:512 * (half + 1)],
                        start=(f == 0), stop=(f == 15)),
                        wait=(([("gw", gw_p6), ("ve", ve_h)] + p6_entry)
                              if (f == 0 and half == 0) else ()),
                        inc=([("pe", 1)] if (f == 15 and half == 1) else ()))
            pe_qr = P.val("pe")
            P.do("scalar", lambda s: s.copy(q_rb[:, 0:TOK], qr_ps[0:64, :]),
                 wait=[("pe", pe_qr)])
            P.do("scalar", lambda s: s.copy(q_rb[:, TOK:BT], qr_ps[64:128, :]),
                 inc=[("se", 1)])
            se_qrb = P.val("se")
            P.do("vector", lambda v: v.memset(contrib_sb[:], 0.0))
            scps_free = [[], []]
            scr_free = [0, 0]
            for i in range(16):
                par = i % 2
                P.do("tensor", lambda t, i=i, par=par: t.matmul(
                    sc_ps[par][:], q_rb[:, 128 * i:128 * (i + 1)], keysb[:],
                    start=True, stop=True),
                    wait=[("se", se_qrb)] + scps_free[par], inc=[("pe", 1)])
                pe_sc = P.val("pe")
                P.do("scalar", lambda s, par=par: s.copy(scr[par][:], sc_ps[par][:]),
                     wait=[("pe", pe_sc), ("ve", scr_free[par])], inc=[("se", 1)])
                se_scr = P.val("se")
                scps_free[par] = [("se", se_scr)]
                P.do("vector", lambda v, i=i, par=par: v.max(
                    out=contrib_sb[:, 16 * i:16 * i + 8], in_=scr[par][:]),
                    wait=[("se", se_scr)], inc=[("ve", 1)])
                ve_m8 = P.val("ve")
                P.do("vector", lambda v, i=i, par=par: v.max_index(
                    out=u8t[:], in_max=contrib_sb[:, 16 * i:16 * i + 8],
                    in_values=scr[par][:]),
                    wait=[("ve", ve_m8)], inc=[("ve", 1)])
                P.do("vector", lambda v, i=i: v.tensor_scalar(
                    out=contrib_sb[:, 16 * i + 8:16 * i + 16], in0=u8t[:],
                    scalar1=0.0, scalar2=None, op0=ALU.add),
                    inc=[("ve", 1)])
                scr_free[par] = P.val("ve")
            ve_contrib = P.val("ve")
            P.do("gpsimd", lambda e: e.dma_start(
                out=contrib.ap().rearrange("(i p) f -> p i f", p=128),
                in_=contrib_sb.ap().rearrange("p (i f) -> p i f", i=16)),
                wait=[("gg", P.val("gg")), ("ve", ve_contrib)],
                inc=[("gg", 16)], dma=True)
            gg_contrib = P.val("gg")
            P.do("gpsimd", lambda e: e.collective_compute(
                "AllGather", ALU.bypass, replica_groups=[list(range(8))],
                ins=[contrib.ap()], outs=[ag_out.ap()]),
                wait=[("gg", gg_contrib)], inc=[("cc", 1)], dma=True)
            cc_done = P.val("cc")

        NSL = 16
        with ExitStack() as p7:
            sb7 = lambda n, s, d=F32: p7.enter_context(nc.sbuf_tensor(n, s, d))
            ps7 = lambda n, s, d=F32: p7.enter_context(nc.psum_tensor(n, s, d))
            gwt = [sb7("p7_gw0", [128, 16 * 512], BF16)]
            uwt = [sb7("p7_uw0", [128, 16 * 512], BF16)]
            dwt = [sb7("p7_dw0", [128, 4 * D], BF16)]
            m_g = sb7("p7_mg", [128, TOK], BF16)
            m_s = [sb7(f"p7_ms{i}", [128, TOK], BF16) for i in range(4)]
            g_ps = ps7("p7_g", [128, TOK])
            u_ps = ps7("p7_u", [128, TOK])
            y_ps = [ps7(f"p7_y{p}", [128, TOK]) for p in range(2)]

            gv = gate_w.rearrange("(f p) (sl c) -> p f sl c", p=128, sl=NSL)
            uv = up_w.rearrange("(f p) (sl c) -> p f sl c", p=128, sl=NSL)
            dv = down_w.rearrange("(kc p) c -> p kc c", p=128)

            p7_entry = [("ve", P.val("ve")), ("se", P.val("se")),
                        ("pe", P.val("pe")), ("sl", P.val("sl"))]
            gps_free = p7_entry
            ups_free = p7_entry
            yps_free = [p7_entry, p7_entry]
            wt_free = [0, 0]
            ms_free = [0] * 4
            mg_free = 0
            dw_free = 0
            uw_free = 0
            gw_free = 0

            for sl in range(NSL):
                par = sl % 2
                P.do("gpsimd", lambda e, sl=sl: e.dma_start(
                    out=gwt[0].ap().rearrange("p (f c) -> p f c", f=16),
                    in_=gv[:, :, sl, :]),
                    wait=[("gw", P.val("gw")), ("pe", gw_free)] +
                         (p7_entry if sl < 1 else []),
                    inc=[("gw", 16)], dma=True)
                P.do("gpsimd", lambda e, sl=sl: e.dma_start(
                    out=uwt[0].ap().rearrange("p (f c) -> p f c", f=16),
                    in_=uv[:, :, sl, :]),
                    wait=[("gw", P.val("gw")), ("pe", uw_free)],
                    inc=[("gw", 16)], dma=True)
                P.do("gpsimd", lambda e, sl=sl: e.dma_start(
                    out=dwt[0].ap().rearrange("p (kc c) -> p kc c", kc=4),
                    in_=dv[:, 4 * sl:4 * (sl + 1), :]),
                    wait=[("gw", P.val("gw")), ("pe", dw_free)],
                    inc=[("gw", 16)], dma=True)
                gw_sl = P.val("gw")
                for ic in range(4):
                    for f in range(16):
                        for half in range(2):
                            P.do("tensor", lambda t, par=par, ic=ic, f=f, half=half:
                                 t.matmul(g_ps[:, 512 * half:512 * (half + 1)],
                                          gwt[0][:, f * 512 + 128 * ic:
                                                 f * 512 + 128 * (ic + 1)],
                                          h_fm[f][:, 512 * half:512 * (half + 1)],
                                          start=(f == 0), stop=(f == 15)),
                                 wait=(([("gw", gw_sl), ("ve", ve_h)] + gps_free)
                                       if (f == 0 and half == 0) else ()),
                                 inc=([("pe", 1)] if (f == 15 and half == 1) else ()))
                    pe_g = P.val("pe")
                    if ic == 3:
                        pe_gate_last = P.val("pe")
                    P.do("scalar", lambda s: s.activation(m_g[:], g_ps[:], AF.Silu),
                         wait=[("pe", pe_g), ("ve", mg_free)], inc=[("se", 1)])
                    gps_free = [("se", P.val("se"))]
                    for f in range(16):
                        for half in range(2):
                            P.do("tensor", lambda t, par=par, ic=ic, f=f, half=half:
                                 t.matmul(u_ps[:, 512 * half:512 * (half + 1)],
                                          uwt[0][:, f * 512 + 128 * ic:
                                                 f * 512 + 128 * (ic + 1)],
                                          h_fm[f][:, 512 * half:512 * (half + 1)],
                                          start=(f == 0), stop=(f == 15)),
                                 wait=(([("gw", gw_sl)] + ups_free)
                                       if (f == 0 and half == 0) else ()),
                                 inc=([("pe", 1)] if (f == 15 and half == 1) else ()))
                    pe_u = P.val("pe")
                    if ic == 3:
                        pe_up_last = P.val("pe")
                    P.do("vector", lambda v, ic=ic: v.tensor_mul(
                        m_s[ic][:], m_g[:], u_ps[:]),
                        wait=[("pe", pe_u), ("se", P.val("se")),
                              ("pe", ms_free[ic])],
                        inc=[("ve", 1)])
                    ups_free = [("ve", P.val("ve"))]
                    mg_free = P.val("ve")
                for o in range(16):
                    ypar = o % 2
                    for kc in range(4):
                        for half in range(2):
                            P.do("tensor", lambda t, par=par, o=o, kc=kc, half=half,
                                 ypar=ypar: t.matmul(
                                     y_ps[ypar][:, 512 * half:512 * (half + 1)],
                                     dwt[0][:, kc * D + 128 * o:
                                             kc * D + 128 * (o + 1)],
                                     m_s[kc][:, 512 * half:512 * (half + 1)],
                                     start=(kc == 0), stop=(kc == 3)),
                                 wait=(([("ve", P.val("ve"))] + yps_free[ypar])
                                       if (kc == 0 and half == 0) else ()),
                                 inc=([("pe", 1)] if (kc == 3 and half == 1) else ()))
                    pe_y = P.val("pe")
                    P.do("vector", lambda v, o=o, ypar=ypar: v.tensor_add(
                        x1[o][:], x1[o][:], y_ps[ypar][:]),
                        wait=[("pe", pe_y)], inc=[("ve", 1)])
                    yps_free[ypar] = [("ve", P.val("ve"))]
                pe_down = P.val("pe")
                for ic in range(4):
                    ms_free[ic] = pe_down
                dw_free = pe_down
                uw_free = pe_up_last
                gw_free = pe_gate_last
            ve_x2m = P.val("ve")

        h_tm = [sb(f"h_tm{i}", [128, D], BF16) for i in range(8)]
        with ExitStack() as p8t:
            tp2_ps = [p8t.enter_context(
                nc.psum_tensor(f"p8_tp{p}", [128, 512], BF16)) for p in range(2)]
            p8_entry = [("ve", P.val("ve")), ("se", P.val("se"))]
            tp2_free = [0, 0]
            for nt in range(8):
                for grp in range(4):
                    gpar = grp % 2
                    for m in range(4):
                        o = grp * 4 + m
                        P.do("tensor", lambda t, nt=nt, gpar=gpar, m=m, o=o:
                             t.transpose(tp2_ps[gpar][:, 128 * m:128 * (m + 1)],
                                         h_fm[o][:, 128 * nt:128 * (nt + 1)],
                                         identbf[:]),
                             wait=(([("se", tp2_free[gpar])] +
                                    (p8_entry if nt == 0 and grp == 0 else []))
                                   if m == 0 else ()),
                             inc=[("pe", 1)] if m == 3 else ())
                    pe_t8 = P.val("pe")
                    for m in range(4):
                        o = grp * 4 + m
                        P.do("scalar", lambda s, nt=nt, gpar=gpar, m=m, o=o: s.copy(
                            h_tm[nt][:, 128 * o:128 * (o + 1)],
                            tp2_ps[gpar][:, 128 * m:128 * (m + 1)]),
                            wait=[("pe", pe_t8)] if m == 0 else (),
                            inc=[("se", 1)] if m == 3 else ())
                    tp2_free[gpar] = P.val("se")
            se_htm = P.val("se")

        with ExitStack() as p8:
            sb8 = lambda n, s, d=F32: p8.enter_context(nc.sbuf_tensor(n, s, d))
            ps8 = lambda n, s, d=F32: p8.enter_context(nc.psum_tensor(n, s, d))
            agidx_sb = sb8("p8_agidx", [128, 16], I32)
            sxy = sb8("p8_sxy", [128, 32])
            comb = sb8("p8_comb", [128, 64])
            emat = sb8("p8_emat", [128, 64])
            ix64 = sb8("p8_ix64", [128, 8])
            v8p = sb8("p8_v8", [128, 8])
            p8u = sb8("p8_p8u", [128, 8], U32)
            p8f = sb8("p8_p8f", [128, 8])
            dummy64 = sb8("p8_dummy", [128, 64])
            e8 = sb8("p8_e8", [128, 8])
            mxr = sb8("p8_mx", [128, 1])
            ssum = sb8("p8_ss", [128, 1])
            pe8 = sb8("p8_pe8", [128, 8])
            prall = sb8("p8_prall", [128, 64])
            eidxall = sb8("p8_eidx", [128, 64], I32)
            ewr = sb8("p8_ewr", [128, 8])
            ewf = sb8("p8_ewf", [128, 8])
            de_t = [sb8(f"p8_de{p}", [128, D], BF16) for p in range(2)]
            ue_t = [sb8("p8_ue0", [128, D], BF16)]
            prod = sb8("p8_prod", [128, D])
            acc = [sb8(f"p8_acc{p}", [128, D]) for p in range(2)]
            tp3_ps = [ps8(f"p8_tp3{p}", [128, 512]) for p in range(2)]

            p8_bar = [("ve", P.val("ve")), ("se", P.val("se")), ("pe", P.val("pe"))]
            P.do("sync", lambda e: e.dma_start(out=agidx_sb[:], in_=ag_idx),
                 wait=p8_bar, inc=[("hl", 16)], dma=True)
            hl_agidx = P.val("hl")

            sxy_free = 0
            for nt in range(8):
                for which in range(2):
                    P.do("gpsimd", lambda e, nt=nt, which=which: e.indirect_dma_start(
                        out=sxy[:, 16 * which:16 * (which + 1)], out_offset=None,
                        in_=ag_out.ap(),
                        in_offset=bass.IndirectOffsetOnAxis(
                            agidx_sb[:, 8 * which + nt:8 * which + nt + 1], 0)),
                        wait=[("gg", P.val("gg")), ("cc", cc_done),
                              ("hl", hl_agidx), ("ve", sxy_free),
                              ("sl", P.val("sl"))] +
                             (p8_bar if (nt == 0 and which == 0) else []),
                        inc=[("gg", 16)], dma=True)
                gg_sxy = P.val("gg")
                P.do("vector", lambda v: v.tensor_scalar(
                    out=ix64[:], in0=sxy[:, 8:16], scalar1=64.0, scalar2=None,
                    op0=ALU.mult), wait=[("gg", gg_sxy)])
                for i in range(8):
                    P.do("vector", lambda v, i=i: v.tensor_scalar(
                        out=comb[:, 8 * i:8 * (i + 1)], in0=sxy[:, 16:24],
                        scalar1=sxy[:, i:i + 1], scalar2=None, op0=ALU.add))
                    P.do("vector", lambda v, i=i: v.tensor_scalar(
                        out=emat[:, 8 * i:8 * (i + 1)], in0=sxy[:, 24:32],
                        scalar1=ix64[:, i:i + 1], scalar2=None, op0=ALU.add))
                P.do("vector", lambda v: v.max(out=v8p[:], in_=comb[:]),
                     inc=[("ve", 1)])
                ve_v8 = P.val("ve")
                sxy_free = ve_v8
                P.do("vector", lambda v: v.max_index(out=p8u[:], in_max=v8p[:],
                                                     in_values=comb[:]),
                     wait=[("ve", ve_v8)], inc=[("ve", 1)])
                ve_p8 = P.val("ve")
                P.do("vector", lambda v: v.tensor_scalar(
                    out=p8f[:], in0=p8u[:], scalar1=0.0, scalar2=None, op0=ALU.add),
                    wait=[("ve", ve_p8)])
                for s8 in range(8):
                    P.do("vector", lambda v, s8=s8: v.scalar_tensor_tensor(
                        out=dummy64[:], in0=iota64[:], scalar=p8f[:, s8:s8 + 1],
                        in1=emat[:], op0=ALU.is_equal, op1=ALU.mult,
                        accum_out=e8[:, s8:s8 + 1]))
                P.do("vector", lambda v, nt=nt: v.tensor_scalar(
                    out=eidxall[:, 8 * nt:8 * (nt + 1)], in0=e8[:], scalar1=0.0,
                    scalar2=None, op0=ALU.add))
                P.do("vector", lambda v: v.reduce_max(mxr[:], v8p[:], axis=AX.X))
                P.do("vector", lambda v: v.tensor_scalar(
                    out=mxr[:], in0=mxr[:], scalar1=-1.0, scalar2=None, op0=ALU.mult),
                    inc=[("ve", 1)])
                ve_mx = P.val("ve")
                P.do("scalar", lambda s: s.activation(pe8[:], v8p[:], AF.Exp,
                                                      bias=mxr[:, 0:1],
                                                      accum_out=ssum[:, 0:1]),
                     wait=[("ve", ve_mx)], inc=[("se", 1)])
                se_pe8 = P.val("se")
                P.do("vector", lambda v: v.reciprocal(mxr[:], ssum[:]),
                     wait=[("se", se_pe8)])
                P.do("vector", lambda v, nt=nt: v.tensor_scalar(
                    out=prall[:, 8 * nt:8 * (nt + 1)], in0=pe8[:],
                    scalar1=mxr[:, 0:1], scalar2=None, op0=ALU.mult),
                    inc=[("ve", 1)])
                if debug >= 4 and nt == 0:
                    P.do("scalar", lambda s: s.dma_start(out=dbg["sxy0"], in_=sxy[:]),
                         wait=[("ve", P.val("ve"))], inc=[("sl", 16)], dma=True)
                    sxy_free = max(sxy_free, 0)
            ve_peerA = P.val("ve")
            if debug >= 4:
                P.do("scalar", lambda s: s.dma_start(out=dbg["eidx"], in_=eidxall[:]),
                     wait=[("ve", ve_peerA)], inc=[("sl", 16)], dma=True)
                P.do("scalar", lambda s: s.dma_start(out=dbg["prall"], in_=prall[:]),
                     inc=[("sl", 16)], dma=True)

            de_free = [0, 0]
            for nt in range(8):
                for s8 in range(8):
                    par = s8 % 2
                    P.do("gpsimd", lambda e, nt=nt, s8=s8, par=par:
                         e.indirect_dma_start(
                             out=de_t[par][:], out_offset=None, in_=down_embed,
                             in_offset=bass.IndirectOffsetOnAxis(
                                 eidxall[:, 8 * nt + s8:8 * nt + s8 + 1], 0)),
                         wait=[("gg", P.val("gg")), ("ve", de_free[par]),
                               ("sl", P.val("sl"))],
                         inc=[("gg", 16)], dma=True)
                    gg_de = P.val("gg")
                    P.do("vector", lambda v, nt=nt, par=par: v.tensor_mul(
                        prod[:], de_t[par][:], h_tm[nt][:]),
                        wait=[("gg", gg_de), ("se", se_htm)])
                    de_free[par] = P.val("ve")
                    P.do("vector", lambda v, s8=s8: v.reduce_sum(
                        ewr[:, s8:s8 + 1], prod[:], axis=AX.X))
                P.do("vector", lambda v: v.engine_nop(), inc=[("ve", 1)])
                ve_ewr = P.val("ve")
                P.do("scalar", lambda s: s.activation(pe8[:], ewr[:], AF.Silu),
                     wait=[("ve", ve_ewr)], inc=[("se", 1)])
                se_silu = P.val("se")
                P.do("vector", lambda v, nt=nt: v.tensor_mul(
                    ewf[:], pe8[:], prall[:, 8 * nt:8 * (nt + 1)]),
                    wait=[("se", se_silu)], inc=[("ve", 1)])
                if debug >= 4 and nt == 0:
                    P.do("scalar", lambda s: s.dma_start(out=dbg["ewr0"], in_=ewr[:]),
                         wait=[("ve", P.val("ve"))], inc=[("sl", 16)], dma=True)
                    P.do("scalar", lambda s: s.dma_start(out=dbg["ewf0"], in_=ewf[:]),
                         inc=[("sl", 16)], dma=True)
                ue_free = 0
                for s8 in range(8):
                    P.do("gpsimd", lambda e, nt=nt, s8=s8:
                         e.indirect_dma_start(
                             out=ue_t[0][:], out_offset=None, in_=up_embed,
                             in_offset=bass.IndirectOffsetOnAxis(
                                 eidxall[:, 8 * nt + s8:8 * nt + s8 + 1], 0)),
                         wait=[("gg", P.val("gg")), ("ve", ue_free)],
                         inc=[("gg", 16)], dma=True)
                    gg_ue = P.val("gg")
                    if s8 == 0:
                        P.do("vector", lambda v, s8=s8: v.tensor_scalar(
                            out=acc[0][:], in0=ue_t[0][:],
                            scalar1=ewf[:, s8:s8 + 1], scalar2=None, op0=ALU.mult),
                            wait=[("gg", gg_ue)])
                    else:
                        P.do("vector", lambda v, s8=s8: v.scalar_tensor_tensor(
                            out=acc[s8 % 2][:], in0=ue_t[0][:],
                            scalar=ewf[:, s8:s8 + 1], in1=acc[(s8 - 1) % 2][:],
                            op0=ALU.mult, op1=ALU.add),
                            wait=[("gg", gg_ue)])
                    ue_free = P.val("ve")
                P.do("vector", lambda v: v.engine_nop(), inc=[("ve", 1)])
                ve_acc = P.val("ve")
                accf = acc[7 % 2]
                if debug >= 4 and nt == 0:
                    P.do("scalar", lambda s, accf=accf: s.dma_start(
                        out=dbg["acc0"], in_=accf[:]),
                        wait=[("ve", ve_acc)], inc=[("sl", 16)], dma=True)
                for grp in range(4):
                    gpar = grp % 2
                    for m in range(4):
                        o = grp * 4 + m
                        P.do("tensor", lambda t, gpar=gpar, m=m, o=o, accf=accf:
                             t.transpose(tp3_ps[gpar][:, 128 * m:128 * (m + 1)],
                                         accf[:, 128 * o:128 * (o + 1)], ident32[:]),
                             wait=(([("ve", ve_acc)] if grp == 0 else
                                    [("ve", P.val("ve"))]) if m == 0 else []),
                             inc=[("pe", 1)] if m == 3 else ())
                    pe_t3 = P.val("pe")
                    for m in range(4):
                        o = grp * 4 + m
                        P.do("vector", lambda v, nt=nt, gpar=gpar, m=m, o=o:
                             v.tensor_add(x1[o][:, 128 * nt:128 * (nt + 1)],
                                          x1[o][:, 128 * nt:128 * (nt + 1)],
                                          tp3_ps[gpar][:, 128 * m:128 * (m + 1)]),
                             wait=[("pe", pe_t3)] if m == 0 else (),
                             inc=[("ve", 1)])
            ve_x2 = P.val("ve")

        with ExitStack() as p9:
            sb9 = lambda n, s, d=F32: p9.enter_context(nc.sbuf_tensor(n, s, d))
            ps9 = lambda n, s, d=F32: p9.enter_context(nc.psum_tensor(n, s, d))
            ot = [sb9(f"p9_ot{p}", [128, D]) for p in range(2)]
            xop = [ps9(f"p9_xp{p}", [128, 512]) for p in range(2)]
            p9_entry = [("ve", P.val("ve")), ("se", P.val("se"))]
            xop_free = [p9_entry, p9_entry]
            ot_free = [0, 0]
            for nt in range(8):
                par = nt % 2
                for grp in range(4):
                    gpar = grp % 2
                    for m in range(4):
                        o = grp * 4 + m
                        P.do("tensor", lambda t, nt=nt, gpar=gpar, m=m, o=o:
                             t.transpose(xop[gpar][:, 128 * m:128 * (m + 1)],
                                         x1[o][:, 128 * nt:128 * (nt + 1)],
                                         ident32[:]),
                             wait=(([("ve", ve_x2)] + xop_free[gpar]) if m == 0
                                   else ()),
                             inc=[("pe", 1)] if m == 3 else ())
                    pe_x9 = P.val("pe")
                    xop_free[gpar] = []
                    for m in range(4):
                        o = grp * 4 + m
                        P.do("vector", lambda v, par=par, gpar=gpar, m=m, o=o:
                             v.tensor_copy(ot[par][:, 128 * o:128 * (o + 1)],
                                           xop[gpar][:, 128 * m:128 * (m + 1)]),
                             wait=(([("pe", pe_x9)] +
                                    ([("sl", ot_free[par])] if grp == 0 else []))
                                   if m == 0 else []),
                             inc=[("ve", 1)])
                    xop_free[gpar] = [("ve", P.val("ve"))]
                ve_ot = P.val("ve")
                P.do("scalar", lambda s, nt=nt, par=par: s.dma_start(
                    out=out[128 * nt:128 * (nt + 1), :], in_=ot[par][:]),
                    wait=[("ve", ve_ot)], inc=[("sl", 16)], dma=True)
                ot_free[par] = P.val("sl")
            sl_out = P.val("sl")

        if debug:
            for o in range(16):
                P.do("scalar", lambda s, o=o: s.dma_start(
                    out=dbg["x1"][128 * o:128 * (o + 1), :], in_=x1[o][:]),
                    wait=[("ve", ve_x2)] if o == 0 else (), inc=[("sl", 16)], dma=True)

        P.do("sync", lambda e: e.nop(), wait=[("sl", P.val("sl"))])

        block = top.enter_context(nc.Block())
        P.replay(block)

    return nc


_NC_CACHE = {}


def _get_nc():
    if "nc" not in _NC_CACHE:
        _NC_CACHE["nc"] = build(debug=0)
    return _NC_CACHE["nc"]


def _ag_idx_for_core(c):
    idx = np.zeros((128, 16), np.int32)
    c1 = c // 2
    for nt in range(8):
        p = np.arange(128)
        n = 1024 * c + 128 * nt + p
        tok_local = n // 2 - 1024 * c1
        half = n % 2
        idx[:, nt] = c1 * 2048 + half * 1024 + tok_local
        idx[:, 8 + nt] = (4 + c1) * 2048 + half * 1024 + tok_local
    return idx


def kernel(L0_attn_x, L0_attn_w_qkv, L0_attn_w_proj, L0_attn_lambda_q1,
           L0_attn_lambda_k1, L0_attn_lambda_q2, L0_attn_lambda_k2,
           L0_attn_rms_scale, L0_ffn_W_q, L0_ffn_keys, L0_ffn_down_embed,
           L0_ffn_up_embed, L0_ffn_gate_w, L0_ffn_up_w, L0_ffn_down_w):
    x = np.ascontiguousarray(np.asarray(L0_attn_x, dtype=np.float32))
    keys = np.ascontiguousarray(np.asarray(L0_ffn_keys, dtype=np.float32))
    common = {
        "w_qkv": np.ascontiguousarray(L0_attn_w_qkv, dtype=np.float32),
        "w_proj": np.ascontiguousarray(L0_attn_w_proj, dtype=np.float32),
        "lam_q1": np.ascontiguousarray(L0_attn_lambda_q1, dtype=np.float32),
        "lam_k1": np.ascontiguousarray(L0_attn_lambda_k1, dtype=np.float32),
        "lam_q2": np.ascontiguousarray(L0_attn_lambda_q2, dtype=np.float32),
        "lam_k2": np.ascontiguousarray(L0_attn_lambda_k2, dtype=np.float32),
        "rms_scale": np.ascontiguousarray(L0_attn_rms_scale, dtype=np.float32),
        "wq_r": np.ascontiguousarray(L0_ffn_W_q, dtype=np.float32),
        "down_embed": np.ascontiguousarray(L0_ffn_down_embed, dtype=np.float32),
        "up_embed": np.ascontiguousarray(L0_ffn_up_embed, dtype=np.float32),
        "gate_w": np.ascontiguousarray(L0_ffn_gate_w, dtype=np.float32),
        "up_w": np.ascontiguousarray(L0_ffn_up_w, dtype=np.float32),
        "down_w": np.ascontiguousarray(L0_ffn_down_w, dtype=np.float32),
    }
    in_maps = []
    for c in range(8):
        b, half = c // 2, c % 2
        im = dict(common)
        im["x_own"] = np.ascontiguousarray(x[b, 1024 * half:1024 * (half + 1)])
        im["x_partner"] = np.ascontiguousarray(
            x[b, 1024 * (1 - half):1024 * (2 - half)])
        im["keys_own"] = np.ascontiguousarray(keys[c // 4])
        im["ag_idx"] = _ag_idx_for_core(c)
        in_maps.append(im)
    nc = _get_nc()
    res = run_bass_kernel_spmd(nc, in_maps, core_ids=list(range(8)))
    outp = np.empty((4, 2048, 2048), np.float32)
    for c in range(8):
        b, half = c // 2, c % 2
        outp[b, 1024 * half:1024 * (half + 1)] = np.asarray(res.results[c]["out"])
    return outp
